# version 10
# baseline (speedup 1.0000x reference)
import numpy as np
import ml_dtypes
import concourse.bass as bass
import concourse.mybir as mybir
from concourse.bass_utils import run_bass_kernel_spmd

F32 = mybir.dt.float32
BF16 = mybir.dt.bfloat16
OP = mybir.AluOpType
AF = mybir.ActivationFunctionType
AX = mybir.AxisListType.X

MAGIC = 12582912.0
LN1_EPS = 1e-5
RMS_EPS = float(np.finfo(np.float32).eps)
RQMAX = 1.0 / 127.0
QEPS = 1e-8

D = 2048
NB = 2048
NO = 1024
DT = 16
NT = 16
NOT_ = 8
I_FFN = 5632
IT = 44
KB = 1024

JSET_A = [0, 1, 2, 3, 8, 9, 10, 11]
JSET_B = list(range(16))


class P:
    def __init__(self, nc, st, names):
        self.sem = {}
        self.cnt = {}
        for n in names:
            self.sem[n] = st.enter_context(nc.semaphore(n))
            self.cnt[n] = 0

    def inc(self, inst, name, k=1):
        inst.then_inc(self.sem[name], k)
        self.cnt[name] += k
        return self.cnt[name]

    def tick(self, eng, name):
        eng.sem_inc(self.sem[name], 1)
        self.cnt[name] += 1
        return self.cnt[name]

    def w(self, eng, name, v):
        if v > 0:
            eng.wait_ge(self.sem[name], v)

    def chain(self, inst, eng, name):
        self.inc(inst, name)
        eng.wait_ge(self.sem[name], self.cnt[name])

    def sw(self, eng, name):
        if self.cnt[name] > 0:
            eng.wait_ge(self.sem[name], self.cnt[name])


def build(taps=(), phase_limit=99):
    taps = set(taps)
    nc = bass.Bass("TRN2", target_bir_lowering=False,
                   detect_race_conditions=False)

    x_in = nc.dram_tensor("x_in", [NB, D], F32, kind="ExternalInput").ap()
    wq = nc.dram_tensor("wq", [8, 128, 2 * D], F32, kind="ExternalInput").ap()
    wk = nc.dram_tensor("wk", [8, 128, 2 * D], F32, kind="ExternalInput").ap()
    wv = nc.dram_tensor("wv", [8, 128, 2 * D], F32, kind="ExternalInput").ap()
    wg = nc.dram_tensor("wg", [8, 128, 2 * D], F32, kind="ExternalInput").ap()
    wp = nc.dram_tensor("wp", [DT, 128, D], F32, kind="ExternalInput").ap()
    w1 = nc.dram_tensor("w1", [IT, 128, 2 * D], F32, kind="ExternalInput").ap()
    w2 = nc.dram_tensor("w2", [DT, 128, I_FFN], F32, kind="ExternalInput").ap()
    moff = nc.dram_tensor("moff", [128, 24], F32, kind="ExternalInput").ap()

    out = nc.dram_tensor("out", [NO, D], F32, kind="ExternalOutput").ap()

    xn_scr = nc.dram_tensor("xn_scr", [4, 128, DT * 512], BF16).ap()
    v_scr = nc.dram_tensor("v_scr", [NT, 4, 128, 512], BF16).ap()
    g_scr = nc.dram_tensor("g_scr", [NOT_, 4, 128, 512], BF16).ap()
    og_scr = nc.dram_tensor("og_scr", [NOT_, 128, D], BF16).ap()
    x1_scr = nc.dram_tensor("x1_scr", [NOT_, 128, D], F32).ap()

    tap_aps = {}

    def tap(name, shape, dtype=F32):
        if name in taps:
            tap_aps[name] = nc.dram_tensor("tap_" + name, shape, dtype,
                                           kind="ExternalOutput").ap()
            return tap_aps[name]
        return None

    from contextlib import ExitStack
    with ExitStack() as st:
        _allocs = []

        BASE = 24 * KB

        def at(name, shape, dt, off_b):
            per = int(np.prod(shape[1:])) * (2 if dt == BF16 else 4)
            _allocs.append((name, off_b, off_b + per))
            assert BASE + off_b + per <= 224 * KB, f"{name} over budget"
            return nc.alloc_sbuf_tensor_at(name, shape, dt, offset=BASE + off_b)

        k_fm = at("k_fm", [128, DT, NB], BF16, 0)
        h_buf = at("h_buf", [128, IT, NO], BF16, 0)

        xn_fm = at("xn_fm", [128, 2, DT, NO], BF16, 64 * KB)
        q_fm = xn_fm[:, 1]
        xbuf = [at(f"xbuf{i}", [128, D], F32, (64 + 8 * i) * KB) for i in range(2)]
        xbuf1 = [at("xbuf1a", [128, D], F32, 161 * KB + 512),
                 at("xbuf1b", [128, D], F32, 169 * KB + 512)]
        stgP1 = [at(f"stgP1{i}", [128, D], BF16, (148 + 4 * i) * KB)
                 for i in range(2)]
        vevq = [at(f"vevq{i}", [128, 512], BF16, (146 + i) * KB)
                for i in range(2)]
        x1t = at("x1t", [128, D], F32, 80 * KB)
        sqdump = at("sqdump", [128, D], F32, 80 * KB)
        stg = at("stg", [128, D], BF16, 88 * KB)
        stg2 = at("stg2", [128, D], BF16, 92 * KB)
        s2n = at("s2n", [128, I_FFN], BF16, 88 * KB)

        x4buf = [at(f"x4buf{i}", [128, D], F32, (32 + 8 * i) * KB)
                 for i in range(2)]
        x1t4 = [at(f"x1t4{i}", [128, D], F32, (48 + 8 * i) * KB)
                for i in range(2)]
        og_fm4 = at("og_fm4", [128, DT, NO], BF16, 64 * KB)
        stg4 = [at(f"stg4{i}", [128, D], BF16, (161 + 4 * i) * KB + 512)
                for i in range(2)]
        osu = at("osu", [128, D], F32, 169 * KB + 512)
        poob = at("poob", [128, 1024], F32, 161 * KB + 512)
        wpb = [at(f"wpb{dc}", [128, D], BF16,
                  (4 * dc if dc < 8 else 128 + 4 * (dc - 8)) * KB)
               for dc in range(DT)]
        stq = at("stq", [128, 16, 512], BF16, 64 * KB)
        vq = at("vq", [128, 16, 512], BF16, 80 * KB)
        og_fm = at("og_fm", [128, DT, NO], BF16, 96 * KB)
        xq_fm = at("xq_fm", [128, DT, NO], BF16, 96 * KB)
        down_fm = at("down_fm", [128, DT, NO], BF16, 100 * KB)

        wbuf = [at(f"wbuf{i}", [128, 2 * D], BF16, (128 + 8 * i) * KB)
                for i in range(2)]
        gtile = at("gtile", [128, 4, D], BF16, 128 * KB)
        vq2 = at("vq2", [128, 16, 512], BF16, 144 * KB)
        vstg = [at(f"vstg{i}", [128, 512], BF16, (144 + i) * KB) for i in range(2)]
        t1qb = [at(f"t1q{i}", [128, D], BF16, (144 + 4 * i) * KB) for i in range(4)]
        uwb = at("uwb", [128, D], F32, 88 * KB)
        w2b = [at(f"w2b{i}", [128, I_FFN], BF16, (132 + 11 * i) * KB)
               for i in range(2)]
        t2q = at("t2q", [128, IT * 128], BF16, 161 * KB + 512)

        ident = at("ident", [128, 128], BF16, 160 * KB)
        ones1 = at("ones1", [1, 512], BF16, 160 * KB + 256)
        ogst = [at(f"ogst{i}", [128, 512], BF16, 165 * KB + 256 + i * KB)
                for i in range(2)]
        masks3 = at("masks3", [128, 16, 512], BF16, 167 * KB + 256 + 256)
        gsil = at("gsil", [128, NO], F32, 167 * KB + 512)
        hacc = at("hacc", [128, NO], BF16, 172 * KB)
        rbc = at("rbc", [128, NO], F32, 128 * KB)
        rrowb = at("rrowb", [128, 1024], BF16, 176 * KB)
        wuq = [at(f"wuq{i}", [128, I_FFN // 4], F32,
                   (172 if i == 0 else 178) * KB + 512) for i in range(2)]

        iota = at("iota", [128, 512], F32, 184 * KB)
        sv = at("sv", [128, 1024], F32, 186 * KB)
        ost = at("ost", [128, D], F32, 190 * KB)
        mofb = at("mofb", [128, 24], F32, 198 * KB)
        wsc = at("wsc", [128, 64], F32, 198 * KB + 128)
        wscb = at("wscb", [128, 16], BF16, 178 * KB)
        wsc2 = at("wsc2", [128, 16], F32, 178 * KB + 128)
        mgb = at("mgb", [128, 2], F32, 178 * KB + 320)

        ps_mm = [st.enter_context(nc.psum_tensor(f"ps_mm{i}", [128, 512], F32))
                 for i in range(4)]
        ps_tr = [st.enter_context(nc.psum_tensor(f"ps_tr{i}", [128, 1024], BF16))
                 for i in range(2)]
        ps_ex = [st.enter_context(nc.psum_tensor(f"ps_ex{i}", [128, 512], F32))
                 for i in range(2)]

        p = P(nc, st, ["qs", "qsa", "qsb", "qga", "qgb", "qp", "qo",
                       "ten", "vec", "sca", "gps"])
        block = st.enter_context(nc.Block())

        class _B:

            def vector(self, f):
                block.vector(lambda v: (p.sw(v, "vec"), f(v))[1])

            def scalar(self, f):
                block.scalar(lambda s_: (p.sw(s_, "sca"), f(s_))[1])

            def tensor(self, f):
                block.tensor(lambda t: (p.sw(t, "ten"), f(t))[1])

            def gpsimd(self, f):
                block.gpsimd(lambda g: (p.sw(g, "gps"), f(g))[1])

            def sync(self, f):
                block.sync(f)

        blk = _B()

        SUM = sv[:, 0:16]
        SQ = sv[:, 16:32]
        M0 = sv[:, 32:48]
        E0 = sv[:, 48:64]
        V0 = sv[:, 64:80]
        T0 = sv[:, 80:96]
        R0 = sv[:, 96:112]
        S0 = sv[:, 112:128]
        N1 = sv[:, 128:144]
        N2 = sv[:, 144:160]
        SCL = sv[:, 160:176]
        BIA = sv[:, 176:192]
        AM = sv[:, 192:200]
        SX = sv[:, 200:208]
        RX = sv[:, 208:216]
        SH = sv[:, 216:224]
        SHS = sv[:, 224:232]
        RH = sv[:, 232:240]
        TMPA = sv[:, 240:248]

        def WSG(i):
            return wsc[:, i % 8:i % 8 + 1]

        def WRG(i):
            return wsc[:, 8 + i % 8:8 + i % 8 + 1]

        def WSU(i):
            return wsc[:, 16 + i % 8:16 + i % 8 + 1]

        def WRU(i):
            return wsc[:, 24 + i % 8:24 + i % 8 + 1]

        def WS2(i):
            return wsc[:, 32 + i % 8:32 + i % 8 + 1]

        def WR2(i):
            return wsc[:, 40 + i % 8:40 + i % 8 + 1]

        @blk.sync
        def _(sync):
            p.inc(sync.dma_start(out=mofb[:], in_=moff), "qs", 16)
        moff_v = p.cnt["qs"]

        @blk.vector
        def _(v):
            v.memset(ident[:], 0.0)
            p.inc(v.memset(ones1[:], 1.0), "vec")
        vec_ms = p.cnt["vec"]

        @blk.gpsimd
        def _(g):
            p.w(g, "vec", vec_ms)
            p.inc(g.affine_select(out=ident[:], in_=ident[:],
                                  compare_op=OP.not_equal, fill=1.0, base=0,
                                  pattern=[[-1, 128]], channel_multiplier=1),
                  "gps")
            p.inc(g.iota(out=iota[:], pattern=[[1, 512]], base=0,
                         channel_multiplier=-1,
                         allow_small_or_imprecise_dtypes=True), "gps")
        gps_p0 = p.cnt["gps"]

        if "iota" in taps:
            ap = tap("iota", [128, 512], F32)

            @blk.sync
            def _(sync):
                p.w(sync, "gps", gps_p0)
                p.inc(sync.dma_start(out=ap, in_=iota[:]), "qo", 16)

        qsa0, qsb0 = p.cnt["qsa"], p.cnt["qsb"]

        def _ldsem(r):
            return "qsa" if r % 2 == 0 else "qsb"

        def _ldval(r, base_a=qsa0, base_b=qsb0):
            return (base_a if r % 2 == 0 else base_b) + 16 * (r // 2 + 1)

        ORD1 = list(range(8, 16)) + list(range(8))
        vec_10, sca_10, ten_10 = p.cnt["vec"], p.cnt["sca"], p.cnt["ten"]
        TK1 = {}
        c = vec_10
        for i in range(NT + 2):
            if i >= 2:
                c += 1; TK1["evac", ORD1[i - 2]] = c
            if i < NT:
                r = ORD1[i]
                c += 1
                if i == 0:
                    c += 4; TK1["R0", r] = c
                else:
                    c += 1; TK1["R0", r] = c
                    c += 1; TK1["BIA", ORD1[i - 1]] = c
            elif i == NT:
                c += 6; TK1["BIA", ORD1[NT - 1]] = c
        vec_P1e = c
        c = sca_10
        for i in range(NT + 1):
            if i < NT:
                r = ORD1[i]
                c += 1; TK1["SQ", r] = c
                c += 1; TK1["sqrt", r] = c
            if i >= 1:
                c += 1; TK1["apply", ORD1[i - 1]] = c
        c = ten_10
        for j in range(NT):
            c += 1; TK1["tr", ORD1[j]] = c

        @blk.sync
        def _(sync):
            for i, r in enumerate(ORD1):
                if i >= 2:
                    p.w(sync, "sca", TK1["apply", ORD1[i - 2]])
                p.inc(sync.dma_start(out=xbuf1[i % 2][:],
                                     in_=x_in[r * 128:(r + 1) * 128, :]),
                      _ldsem(i), 16)

        @blk.vector
        def _(v):
            def sc(name, r):
                return {"M0": M0, "E0": E0, "V0": V0, "T0": T0, "R0": R0,
                        "S0": S0, "N1": N1, "N2": N2, "SCL": SCL,
                        "BIA": BIA, "SUM": SUM, "SQ": SQ}[name][:, r:r + 1]

            for i in range(NT + 2):
                if i >= 2:
                    rp2 = ORD1[i - 2]
                    p.w(v, "ten", TK1["tr", rp2])
                    last = None
                    for half in range(2):
                        last = v.tensor_copy(
                            out=xn_fm[:, rp2 // 8, half * 8:(half + 1) * 8,
                                      (rp2 % 8) * 128:(rp2 % 8 + 1) * 128],
                            in_=ps_tr[half][:].rearrange("p (c f) -> p c f",
                                                         c=8))
                    p.inc(last, "vec")
                    assert p.cnt["vec"] == TK1["evac", rp2]
                if i < NT:
                    r = ORD1[i]
                    p.w(v, _ldsem(i), _ldval(i))
                    p.chain(v.tensor_reduce(out=SUM[:, r:r + 1],
                                            in_=xbuf1[i % 2][:], axis=AX,
                                            op=OP.add), v, "vec")
                    p.w(v, "sca", TK1["SQ", r])
                    if i == 0:
                        v.tensor_scalar(out=sc("M0", r), in0=sc("SUM", r),
                                        scalar1=1.0 / D, scalar2=None,
                                        op0=OP.mult)
                        v.tensor_scalar(out=sc("E0", r), in0=sc("SQ", r),
                                        scalar1=1.0 / D, scalar2=None,
                                        op0=OP.mult)
                        p.chain(v.tensor_mul(sc("V0", r), sc("M0", r),
                                             sc("M0", r)), v, "vec")
                        p.chain(v.tensor_sub(sc("V0", r), sc("E0", r),
                                             sc("V0", r)), v, "vec")
                        p.chain(v.tensor_scalar(out=sc("T0", r),
                                                in0=sc("V0", r),
                                                scalar1=1.0 + RMS_EPS,
                                                scalar2=RMS_EPS * LN1_EPS,
                                                op0=OP.mult, op1=OP.add),
                                v, "vec")
                        p.inc(v.reciprocal(sc("R0", r), sc("T0", r)), "vec")
                        assert p.cnt["vec"] == TK1["R0", r]
                    else:
                        rp = ORD1[i - 1]
                        p.w(v, "sca", TK1["sqrt", rp])
                        v.tensor_scalar(out=sc("M0", r), in0=sc("SUM", r),
                                        scalar1=1.0 / D, scalar2=None,
                                        op0=OP.mult)
                        v.tensor_mul(sc("N1", rp), sc("S0", rp), sc("S0", rp))
                        v.tensor_scalar(out=sc("E0", r), in0=sc("SQ", r),
                                        scalar1=1.0 / D, scalar2=None,
                                        op0=OP.mult)
                        v.tensor_mul(sc("N1", rp), sc("T0", rp), sc("N1", rp))
                        v.tensor_mul(sc("V0", r), sc("M0", r), sc("M0", r))
                        v.tensor_scalar(out=sc("N1", rp), in0=sc("N1", rp),
                                        scalar1=-0.5, scalar2=1.5,
                                        op0=OP.mult, op1=OP.add)
                        v.tensor_sub(sc("V0", r), sc("E0", r), sc("V0", r))
                        v.tensor_mul(sc("SCL", rp), sc("S0", rp),
                                     sc("N1", rp))
                        v.tensor_scalar(out=sc("T0", r), in0=sc("V0", r),
                                        scalar1=1.0 + RMS_EPS,
                                        scalar2=RMS_EPS * LN1_EPS,
                                        op0=OP.mult, op1=OP.add)
                        v.tensor_mul(sc("N2", rp), sc("M0", rp),
                                     sc("SCL", rp))
                        p.inc(v.reciprocal(sc("R0", r), sc("T0", r)), "vec")
                        assert p.cnt["vec"] == TK1["R0", r]
                        p.inc(v.tensor_scalar(out=sc("BIA", rp),
                                              in0=sc("N2", rp), scalar1=-1.0,
                                              scalar2=None, op0=OP.mult),
                              "vec")
                        assert p.cnt["vec"] == TK1["BIA", rp]
                elif i == NT:
                    rp = ORD1[NT - 1]
                    p.w(v, "sca", TK1["sqrt", rp])
                    p.chain(v.tensor_mul(sc("N1", rp), sc("S0", rp),
                                         sc("S0", rp)), v, "vec")
                    p.chain(v.tensor_mul(sc("N1", rp), sc("T0", rp),
                                         sc("N1", rp)), v, "vec")
                    p.chain(v.tensor_scalar(out=sc("N1", rp),
                                            in0=sc("N1", rp), scalar1=-0.5,
                                            scalar2=1.5, op0=OP.mult,
                                            op1=OP.add), v, "vec")
                    p.chain(v.tensor_mul(sc("SCL", rp), sc("S0", rp),
                                         sc("N1", rp)), v, "vec")
                    p.chain(v.tensor_mul(sc("N2", rp), sc("M0", rp),
                                         sc("SCL", rp)), v, "vec")
                    p.inc(v.tensor_scalar(out=sc("BIA", rp), in0=sc("N2", rp),
                                          scalar1=-1.0, scalar2=None,
                                          op0=OP.mult), "vec")
                    assert p.cnt["vec"] == TK1["BIA", rp]
            assert p.cnt["vec"] == vec_P1e

        @blk.scalar
        def _(s):
            for i in range(NT + 1):
                if i < NT:
                    r = ORD1[i]
                    p.w(s, _ldsem(i), _ldval(i))
                    p.sw(s, "sca")
                    p.inc(s.activation(out=ost[:], in_=xbuf1[i % 2][:],
                                       func=AF.Square,
                                       accum_out=SQ[:, r:r + 1]), "sca")
                    assert p.cnt["sca"] == TK1["SQ", r]
                    p.w(s, "vec", TK1["R0", r])
                    p.inc(s.activation(out=S0[:, r:r + 1], in_=R0[:, r:r + 1],
                                       func=AF.Sqrt), "sca")
                    assert p.cnt["sca"] == TK1["sqrt", r]
                if i >= 1:
                    rp = ORD1[i - 1]
                    p.w(s, "vec", TK1["BIA", rp])
                    if i >= 3:
                        p.w(s, "ten", TK1["tr", ORD1[i - 3]])
                    p.inc(s.activation(out=stgP1[(i - 1) % 2][:],
                                       in_=xbuf1[(i - 1) % 2][:],
                                       func=AF.Identity,
                                       scale=SCL[:, rp:rp + 1],
                                       bias=BIA[:, rp:rp + 1]), "sca")
                    assert p.cnt["sca"] == TK1["apply", rp]

        def emit_p1_tr(t, lo, hi):
            for i in range(lo, hi):
                r = ORD1[i]
                p.w(t, "sca", TK1["apply", r])
                if i >= 1:
                    p.w(t, "vec", TK1["evac", ORD1[i - 1]])
                last = None
                for dc in range(DT):
                    last = t.transpose(
                        ps_tr[dc // 8][:, (dc % 8) * 128:(dc % 8 + 1) * 128],
                        stgP1[i % 2][:, dc * 128:(dc + 1) * 128], ident[:])
                p.inc(last, "ten")
                assert p.cnt["ten"] == TK1["tr", r]

        @blk.tensor
        def _(t):
            p.w(t, "gps", gps_p0)
            emit_p1_tr(t, 0, NT)

        if "xn_scr" in taps:
            ap = tap("xn_scr", [128, 2 * DT * NO], BF16)

            @blk.sync
            def _(sync):
                p.w(sync, "vec", vec_P1e)
                p.inc(sync.dma_start(
                    out=ap, in_=xn_fm[:].rearrange("p a b c -> p (a b c)")),
                    "qo", 16)

        def emit_proj_fm(wsrc, dst_fm, sp, act, dst_base=None, first=False):
            qga0, qgb0 = p.cnt["qga"], p.cnt["qgb"]
            sca_0, ten_0 = p.cnt["sca"], p.cnt["ten"]
            db = 1024 * sp if dst_base is None else dst_base

            def _wsem(grp):
                return "qga" if grp % 2 == 0 else "qgb"

            def _wval(grp):
                return (qga0 if grp % 2 == 0 else qgb0) + 16 * (grp // 2 + 1)

            @blk.gpsimd
            def _(g):
                for grp in range(8):
                    if grp >= 2:
                        p.w(g, "ten", ten_0 + 4 * (grp - 1))
                    elif not first:
                        p.w(g, "ten", ten_0)
                    p.inc(g.dma_start(out=wbuf[grp % 2][:], in_=wsrc[grp]),
                          _wsem(grp), 16)

            @blk.tensor
            def _(t):
                for ot in range(DT):
                    grp, half = ot // 2, ot % 2
                    if half == 0:
                        p.w(t, _wsem(grp), _wval(grp))
                    for sl in range(2):
                        u = 2 * ot + sl
                        if u >= 2:
                            p.w(t, "sca", sca_0 + u - 1)
                        ps = ps_mm[u % 2]
                        last = None
                        for dc in range(DT):
                            last = t.matmul(ps[:],
                                            wbuf[grp % 2][:, half * D + dc * 128:
                                                          half * D + (dc + 1) * 128],
                                            xn_fm[:, sp, dc, sl * 512:
                                                  (sl + 1) * 512],
                                            start=(dc == 0), stop=(dc == DT - 1))
                        p.inc(last, "ten")

            @blk.scalar
            def _(s):
                for u in range(32):
                    p.w(s, "ten", ten_0 + u + 1)
                    ot, sl = u // 2, u % 2
                    p.inc(s.activation(out=dst_fm[:, ot, db + 512 * sl:
                                               db + 512 * (sl + 1)],
                                       in_=ps_mm[u % 2][:], func=act), "sca")

        def emit_proj_tm(wsrc, scr, sp, act, base_tile):
            qga0, qgb0 = p.cnt["qga"], p.cnt["qgb"]
            sca_0, ten_0, vec_0 = p.cnt["sca"], p.cnt["ten"], p.cnt["vec"]
            qp_0 = p.cnt["qp"]

            def _wsem(grp):
                return "qga" if grp % 2 == 0 else "qgb"

            def _wval(grp):
                return (qga0 if grp % 2 == 0 else qgb0) + 16 * (grp // 2 + 1)

            @blk.gpsimd
            def _(g):
                for grp in range(8):
                    if grp >= 2:
                        p.w(g, "ten", ten_0 + 8 * (grp - 1))
                    else:
                        p.w(g, "ten", ten_0)
                    p.inc(g.dma_start(out=wbuf[grp % 2][:], in_=wsrc[grp]),
                          _wsem(grp), 16)

            @blk.tensor
            def _(t):
                for ot in range(DT):
                    grp, half = ot // 2, ot % 2
                    if half == 0:
                        p.w(t, _wsem(grp), _wval(grp))
                    for sl in range(2):
                        u = 2 * ot + sl
                        if u >= 2:
                            p.w(t, "sca", sca_0 + u - 1)
                        ps = ps_mm[u % 2]
                        last = None
                        for dc in range(DT):
                            last = t.matmul(ps[:],
                                            wbuf[grp % 2][:, half * D + dc * 128:
                                                          half * D + (dc + 1) * 128],
                                            xn_fm[:, sp, dc, sl * 512:
                                                  (sl + 1) * 512],
                                            start=(dc == 0), stop=(dc == DT - 1))
                        p.inc(last, "ten")
                        p.w(t, "sca", sca_0 + u + 1)
                        if u >= 2:
                            p.w(t, "vec", vec_0 + u - 1)
                        for q4 in range(4):
                            last = t.transpose(
                                ps_tr[u % 2][:, q4 * 128:(q4 + 1) * 128],
                                vstg[u % 2][:, q4 * 128:(q4 + 1) * 128],
                                ident[:])
                        p.inc(last, "ten")

            @blk.vector
            def _(v):
                for u in range(32):
                    p.w(v, "ten", ten_0 + 2 * u + 2)
                    if u < 2:
                        p.w(v, "qp", qp_0)
                    else:
                        p.w(v, "qp", qp_0 + 64 * (u - 1))
                    p.inc(v.tensor_copy(out=vevq[u % 2][:],
                                        in_=ps_tr[u % 2][:, 0:512]), "vec")

            @blk.scalar
            def _(s):
                for u in range(33):
                    if u < 32:
                        p.w(s, "ten", ten_0 + 2 * u + 1)
                        p.inc(s.activation(out=vstg[u % 2][:],
                                           in_=ps_mm[u % 2][:], func=act),
                              "sca")
                    if u >= 1:
                        up = u - 1
                        ot, sl = up // 2, up % 2
                        p.w(s, "vec", vec_0 + up + 1)
                        for q4 in range(4):
                            p.inc(s.dma_start(
                                out=scr[base_tile + 4 * sl + q4, ot // 4, :,
                                        (ot % 4) * 128:(ot % 4 + 1) * 128],
                                in_=vevq[up % 2][:, q4 * 128:(q4 + 1) * 128]),
                                "qp", 16)

        sp_marks = {}
        if phase_limit >= 2:
            for sp in (1, 0):
                @blk.tensor
                def _(t, sp=sp):
                    p.w(t, "vec", TK1["evac", 15 if sp == 1 else 7])

                emit_proj_fm(wk, k_fm, sp, AF.Silu, first=(sp == 1))
                if phase_limit >= 3:
                    emit_proj_tm(wv, v_scr, sp, AF.Silu, base_tile=8 * sp)
                if sp == 0 and phase_limit >= 4:
                    emit_proj_tm(wg, g_scr, 0, AF.Sigmoid, base_tile=0)
                if sp == 0 and phase_limit >= 5:
                    qp_now = p.cnt["qp"]

                    @blk.scalar
                    def _(s_, qp_now=qp_now):
                        p.w(s_, "qp", qp_now)
                    emit_proj_fm(wq, q_fm, 0, AF.Silu, dst_base=0)
                sp_marks[("ten_end", sp)] = p.cnt["ten"]

        sca_P2, qp_P2 = p.cnt["sca"], p.cnt["qp"]

        if "k_fm" in taps and phase_limit >= 2:
            ap = tap("k_fm", [128, DT * NB], BF16)

            @blk.sync
            def _(sync):
                p.w(sync, "sca", sca_P2)
                p.inc(sync.dma_start(
                    out=ap, in_=k_fm[:].rearrange("p a b -> p (a b)")), "qo", 16)
        if "q_fm" in taps and phase_limit >= 5:
            ap = tap("q_fm", [128, DT * NO], BF16)

            @blk.sync
            def _(sync):
                p.w(sync, "sca", sca_P2)
                p.inc(sync.dma_start(
                    out=ap, in_=q_fm[:].rearrange("p a b -> p (a b)")), "qo", 16)
        if "v_tm" in taps and phase_limit >= 3:
            ap = tap("v_tm", [NT * 4 * 128, 512], BF16)

            @blk.sync
            def _(sync):
                p.w(sync, "qp", qp_P2)
                p.inc(sync.dma_start(
                    out=ap, in_=v_scr.rearrange("a e b c -> (a e b) c")),
                    "qo", 16)
        if "g_tm" in taps and phase_limit >= 4:
            ap = tap("g_tm", [NOT_ * 4 * 128, 512], BF16)

            @blk.sync
            def _(sync):
                p.w(sync, "qp", qp_P2)
                p.inc(sync.dma_start(
                    out=ap, in_=g_scr.rearrange("a e b c -> (a e b) c")), "qo", 16)

        qp_og = p.cnt["qp"]
        p3m = {}
        if phase_limit >= 6:
            for ib, jset in ((0, JSET_A), (1, JSET_B)):
                nj = len(jset)
                vec_mk0 = p.cnt["vec"]

                @blk.vector
                def _(v, ib=ib, jset=jset):
                    if ib == 0:
                        p.w(v, "qs", moff_v)
                        p.w(v, "gps", gps_p0)
                        p.w(v, "qp", qp_P2)
                    last = None
                    for idx in range(len(jset)):
                        c = idx if ib == 0 else 8 + idx
                        last = v.tensor_scalar(out=masks3[:, idx, :], in0=iota[:],
                                               scalar1=mofb[:, c:c + 1],
                                               scalar2=0.0,
                                               op0=OP.add, op1=OP.is_ge)
                    p.inc(last, "vec")
                vec_mk = p.cnt["vec"]

                ten_s0, vec_s0 = p.cnt["ten"], p.cnt["vec"]

                @blk.tensor
                def _(t, ib=ib, jset=jset, ten_s0=ten_s0, vec_s0=vec_s0,
                      vec_mk=vec_mk):
                    p.w(t, "vec", vec_mk)
                    if ib == 0:
                        p.w(t, "sca", sca_P2)
                    for idx, jt in enumerate(jset):
                        if idx >= 2:
                            p.w(t, "vec", vec_s0 + idx - 1)
                        ps = ps_mm[idx % 2]
                        last = None
                        for dc in range(DT):
                            last = t.matmul(ps[:],
                                            k_fm[:, dc, jt * 128:(jt + 1) * 128],
                                            q_fm[:, dc, ib * 512:(ib + 1) * 512],
                                            start=(dc == 0), stop=(dc == DT - 1))
                        p.inc(last, "ten")

                if ib == 1:
                    p3m["kdone"] = p.cnt["ten"]

                @blk.vector
                def _(v, jset=jset, ten_s0=ten_s0):
                    for idx in range(len(jset)):
                        p.w(v, "ten", ten_s0 + idx + 1)
                        p.inc(v.tensor_tensor(out=stq[:, idx, :],
                                              in0=ps_mm[idx % 2][:],
                                              in1=masks3[:, idx, :],
                                              op=OP.mult), "vec")
                vec_s = p.cnt["vec"]

                qs_g0 = p.cnt["qs"]

                vec_now = p.cnt["vec"]

                @blk.sync
                def _(sync, ib=ib, vec_now=vec_now):
                    if ib == 0:
                        p.w(sync, "qp", qp_P2)
                        p.w(sync, "sca", sca_P2)
                    else:
                        p.w(sync, "vec", vec_now)
                    for it in range(4):
                        for eh in range(4):
                            p.inc(sync.dma_start(
                                out=gtile[:, it, eh * 512:(eh + 1) * 512],
                                in_=g_scr[4 * ib + it, eh]), "qs", 16)
                qs_g = p.cnt["qs"]

                ten_v0, vec_v0 = p.cnt["ten"], p.cnt["vec"]
                qsa_v0, qsb_v0 = p.cnt["qsa"], p.cnt["qsb"]
                qp_o0 = p.cnt["qp"]

                @blk.sync
                def _(sync, ib=ib, jset=jset, vec_v0=vec_v0):
                    nj = len(jset)
                    if ib == 0:
                        p.w(sync, "sca", sca_P2)
                    for eh in range(4):
                        if eh >= 2:
                            p.w(sync, "vec", vec_v0 + 4 * (eh - 1))
                        else:
                            p.w(sync, "vec", vec_v0)
                        buf = vq if eh % 2 == 0 else vq2
                        for idx, jt in enumerate(jset):
                            p.inc(sync.dma_start(
                                out=buf[:, idx, :],
                                in_=v_scr[jt, eh]),
                                "qsa" if eh % 2 == 0 else "qsb", 16)

                @blk.tensor
                def _(t, jset=jset, ten_v0=ten_v0, vec_s=vec_s,
                      qsa_v0=qsa_v0, qsb_v0=qsb_v0, vec_v0=vec_v0):
                    nj = len(jset)
                    p.w(t, "vec", vec_s)
                    for eh in range(4):
                        p.w(t, "qsa" if eh % 2 == 0 else "qsb",
                            (qsa_v0 if eh % 2 == 0 else qsb_v0)
                            + 16 * nj * (eh // 2 + 1))
                        buf = vq if eh % 2 == 0 else vq2
                        for it in range(4):
                            u = 4 * eh + it
                            if u >= 2:
                                p.w(t, "vec", vec_v0 + u - 1)
                            ps = ps_mm[2 + u % 2]
                            last = None
                            for idx in range(nj):
                                last = t.matmul(ps[:],
                                                stq[:, idx, it * 128:(it + 1) * 128],
                                                buf[:, idx, :],
                                                start=(idx == 0),
                                                stop=(idx == nj - 1))
                            p.inc(last, "ten")

                dbg_ap = None
                if ib == 0 and "dbgo" in taps:
                    dbg_ap = tap("dbgo", [128, 3 * 8192], F32)
                    dbg_bf = tap("dbgb", [128, 2 * 8192], BF16)

                @blk.vector
                def _(v, ib=ib, ten_v0=ten_v0, qs_g=qs_g, qp_o0=qp_o0,
                      jset=jset):
                    p.w(v, "qs", qs_g)
                    for eh in range(4):
                        for it in range(4):
                            u = 4 * eh + it
                            p.w(v, "ten", ten_v0 + u + 1)
                            if u >= 2:
                                p.w(v, "qp", qp_o0 + 16 * u)
                            else:
                                p.w(v, "qp", qp_o0)
                            if ib == 0 and eh == 3 and it == 0 and dbg_ap is not None:
                                v.tensor_copy(out=ost[:, 0:512],
                                              in_=ps_mm[2 + u % 2][:])
                            p.inc(v.tensor_tensor(
                                out=ogst[u % 2][:],
                                in0=ps_mm[2 + u % 2][:],
                                in1=gtile[:, it, eh * 512:(eh + 1) * 512],
                                op=OP.mult), "vec")

                @blk.scalar
                def _(s, ib=ib, vec_v0=vec_v0):
                    for eh in range(4):
                        for it in range(4):
                            u = 4 * eh + it
                            p.w(s, "vec", vec_v0 + u + 1)
                            p.inc(s.dma_start(
                                out=og_scr[4 * ib + it, :, eh * 512:(eh + 1) * 512],
                                in_=ogst[u % 2][:]), "qp", 16)

                if dbg_ap is not None and ib == 0:
                    @blk.sync
                    def _(sync, vec_v0=vec_v0):
                        p.w(sync, "vec", vec_v0 + 16)
                        p.inc(sync.dma_start(out=dbg_ap[:, 0:512],
                                             in_=ost[:, 0:512]), "qo", 16)
                        p.inc(sync.dma_start(
                            out=dbg_bf[:, 0:8 * 512],
                            in_=stq[:, 0:8, :].rearrange("p a b -> p (a b)")),
                            "qo", 16)
                        p.inc(sync.dma_start(
                            out=dbg_bf[:, 8192:8192 + 8 * 512],
                            in_=vq2[:, 0:8, :].rearrange("p a b -> p (a b)")),
                            "qo", 16)
            qp_og = p.cnt["qp"]

            if "og" in taps:
                ap = tap("og", [NOT_ * 128, D], BF16)

                @blk.sync
                def _(sync):
                    p.w(sync, "qp", qp_og)
                    p.inc(sync.dma_start(
                        out=ap, in_=og_scr.rearrange("a b c -> (a b) c")), "qo", 16)

        vec_xq = p.cnt["vec"]
        qp_x1 = p.cnt["qp"]
        if phase_limit >= 7:
            ten_a0, vec_a0 = p.cnt["ten"], p.cnt["vec"]
            vec_P3e = p.cnt["vec"]
            ten_P3e = p.cnt["ten"]
            qga_w0, qgb_w0 = p.cnt["qga"], p.cnt["qgb"]
            qsa_a0, qsb_a0 = p.cnt["qsa"], p.cnt["qsb"]

            @blk.gpsimd
            def _(g):
                for dc in range(DT):
                    if dc < 8:
                        p.w(g, "ten", p3m.get("kdone", ten_P3e))
                    else:
                        p.w(g, "ten", ten_P3e)
                        p.w(g, "vec", vec_P3e)
                    p.inc(g.dma_start(out=wpb[dc][:], in_=wp[dc]),
                          "qga" if dc % 2 == 0 else "qgb", 16)

            @blk.sync
            def _(sync):
                p.w(sync, "qp", qp_og)
                for it in range(NOT_):
                    if it >= 2:
                        p.w(sync, "ten", ten_a0 + (it - 1))
                    p.inc(sync.dma_start(out=stg4[it % 2][:],
                                         in_=og_scr[it]), _ldsem(it), 16)

            @blk.tensor
            def _(t):
                for it in range(NOT_):
                    p.w(t, _ldsem(it), _ldval(it, qsa_a0, qsb_a0))
                    if it >= 1:
                        p.w(t, "vec", vec_a0 + it)
                    last = None
                    for ec in range(DT):
                        last = t.transpose(
                            ps_tr[ec // 8][:, (ec % 8) * 128:(ec % 8 + 1) * 128],
                            stg4[it % 2][:, ec * 128:(ec + 1) * 128], ident[:])
                    p.inc(last, "ten")

            @blk.vector
            def _(v):
                for it in range(NOT_):
                    p.w(v, "ten", ten_a0 + it + 1)
                    last = None
                    for half in range(2):
                        last = v.tensor_copy(
                            out=og_fm4[:, half * 8:(half + 1) * 8,
                                       it * 128:(it + 1) * 128],
                            in_=ps_tr[half][:].rearrange("p (c f) -> p c f", c=8))
                    p.inc(last, "vec")
            vec_ogfm = p.cnt["vec"]
            ten_a1 = p.cnt["ten"]

            ten_b0, vec_b0, sca_b0 = p.cnt["ten"], p.cnt["vec"], p.cnt["sca"]
            qsa_b0, qsb_b0 = p.cnt["qsa"], p.cnt["qsb"]
            qp_b0 = p.cnt["qp"]

            TK4 = {}
            c = vec_b0
            for k in range(NOT_ + 1):
                if k < NOT_:
                    for ch in range(4):
                        c += 1; TK4["va", k, ch] = c
                    c += 1; TK4["SUMr", k] = c
                if k >= 1:
                    c += 1; TK4["evQ", k - 1] = c
                if k < NOT_:
                    c += 4; TK4["R0", k] = c
                    c += 6; TK4["BIA", k] = c
                    c += 3; TK4["RX", k] = c
                    c += 1; TK4["r2b", k] = c
            vec_b_end = c
            c = sca_b0
            for k in range(NOT_):
                c += 1; TK4["SQ", k] = c
                c += 1; TK4["sqrt", k] = c
                c += 1; TK4["apply", k] = c
                c += 2; TK4["r2a", k] = c
            c = ten_b0
            for k in range(NOT_ + 1):
                if k < NOT_:
                    for ch in range(4):
                        c += 1; TK4["mmB", k, ch] = c
                if k >= 1:
                    c += 1; TK4["trQ", k - 1] = c
            ten_b_end = c

            @blk.sync
            def _(sync):
                p.w(sync, "ten", ten_P3e)
                for k in range(NOT_):
                    if k >= 2:
                        p.w(sync, "vec", TK4["va", k - 2, 3])
                    p.inc(sync.dma_start(out=x4buf[k % 2][:],
                                         in_=x_in[k * 128:(k + 1) * 128, :]),
                          _ldsem(k), 16)

            @blk.tensor
            def _(t):
                p.w(t, "vec", vec_ogfm)
                for k in range(NOT_ + 1):
                    if k < NOT_:
                        for ch in range(4):
                            if k >= 1:
                                p.w(t, "vec", TK4["va", k - 1, ch])
                            last = None
                            for dc in range(DT):
                                if k == 0 and ch == 0:
                                    p.w(t, "qga" if dc % 2 == 0 else "qgb",
                                        (qga_w0 if dc % 2 == 0 else qgb_w0)
                                        + 16 * (dc // 2 + 1))
                                last = t.matmul(
                                    ps_mm[ch][:],
                                    og_fm4[:, dc, k * 128:(k + 1) * 128],
                                    wpb[dc][:, ch * 512:(ch + 1) * 512],
                                    start=(dc == 0), stop=(dc == DT - 1))
                            p.inc(last, "ten")
                            assert p.cnt["ten"] == TK4["mmB", k, ch]
                    if k >= 1:
                        p.w(t, "vec", TK4["r2b", k - 1])
                        if k >= 2:
                            p.w(t, "vec", TK4["evQ", k - 2])
                        last = None
                        for ec in range(DT):
                            last = t.transpose(
                                ps_tr[ec // 8][:, (ec % 8) * 128:
                                               (ec % 8 + 1) * 128],
                                stg4[(k - 1) % 2][:, ec * 128:(ec + 1) * 128],
                                ident[:])
                        p.inc(last, "ten")
                        assert p.cnt["ten"] == TK4["trQ", k - 1]

            @blk.vector
            def _(v):
                v.memset(mgb[:, 0:1], MAGIC)
                v.memset(mgb[:, 1:2], -MAGIC)
                for k in range(NOT_ + 1):
                    if k < NOT_:
                        p.w(v, _ldsem(k), _ldval(k, qsa_b0, qsb_b0))
                        if k >= 2:
                            p.w(v, "qp", qp_b0 + 16 * (k - 1))
                        for ch in range(4):
                            p.w(v, "ten", TK4["mmB", k, ch])
                            inst = v.tensor_tensor(
                                out=x1t4[k % 2][:, ch * 512:(ch + 1) * 512],
                                in0=ps_mm[ch][:],
                                in1=x4buf[k % 2][:, ch * 512:(ch + 1) * 512],
                                op=OP.add)
                            if ch < 3:
                                p.inc(inst, "vec")
                            else:
                                p.chain(inst, v, "vec")
                            assert p.cnt["vec"] == TK4["va", k, ch]
                        p.inc(v.tensor_reduce(out=SUM[:, k:k + 1],
                                              in_=x1t4[k % 2][:], axis=AX,
                                              op=OP.add), "vec")
                        assert p.cnt["vec"] == TK4["SUMr", k]
                    if k >= 1:
                        p.w(v, "ten", TK4["trQ", k - 1])
                        last = None
                        for half in range(2):
                            last = v.tensor_copy(
                                out=xq_fm[:, half * 8:(half + 1) * 8,
                                          (k - 1) * 128:k * 128],
                                in_=ps_tr[half][:].rearrange(
                                    "p (c f) -> p c f", c=8))
                        p.inc(last, "vec")
                        assert p.cnt["vec"] == TK4["evQ", k - 1]
                    if k < NOT_:
                        p.w(v, "sca", TK4["SQ", k])
                        v.tensor_scalar(out=M0[:, k:k + 1], in0=SUM[:, k:k + 1],
                                        scalar1=1.0 / D, scalar2=None,
                                        op0=OP.mult)
                        v.tensor_scalar(out=E0[:, k:k + 1], in0=SQ[:, k:k + 1],
                                        scalar1=1.0 / D, scalar2=None,
                                        op0=OP.mult)
                        p.chain(v.tensor_mul(V0[:, k:k + 1], M0[:, k:k + 1],
                                             M0[:, k:k + 1]), v, "vec")
                        p.chain(v.tensor_sub(V0[:, k:k + 1], E0[:, k:k + 1],
                                             V0[:, k:k + 1]), v, "vec")
                        p.chain(v.tensor_scalar(out=T0[:, k:k + 1],
                                                in0=V0[:, k:k + 1],
                                                scalar1=1.0, scalar2=LN1_EPS,
                                                op0=OP.mult, op1=OP.add),
                                v, "vec")
                        p.inc(v.reciprocal(R0[:, k:k + 1], T0[:, k:k + 1]),
                              "vec")
                        assert p.cnt["vec"] == TK4["R0", k]
                        p.w(v, "sca", TK4["sqrt", k])
                        p.chain(v.tensor_mul(N1[:, k:k + 1], S0[:, k:k + 1],
                                             S0[:, k:k + 1]), v, "vec")
                        p.chain(v.tensor_mul(N1[:, k:k + 1], T0[:, k:k + 1],
                                             N1[:, k:k + 1]), v, "vec")
                        p.chain(v.tensor_scalar(out=N1[:, k:k + 1],
                                                in0=N1[:, k:k + 1],
                                                scalar1=-0.5, scalar2=1.5,
                                                op0=OP.mult, op1=OP.add),
                                v, "vec")
                        p.chain(v.tensor_mul(SCL[:, k:k + 1], S0[:, k:k + 1],
                                             N1[:, k:k + 1]), v, "vec")
                        p.chain(v.tensor_mul(N2[:, k:k + 1], M0[:, k:k + 1],
                                             SCL[:, k:k + 1]), v, "vec")
                        p.inc(v.tensor_scalar(out=BIA[:, k:k + 1],
                                              in0=N2[:, k:k + 1], scalar1=-1.0,
                                              scalar2=None, op0=OP.mult),
                              "vec")
                        assert p.cnt["vec"] == TK4["BIA", k]
                        p.w(v, "sca", TK4["apply", k])
                        p.chain(v.tensor_reduce(out=AM[:, k:k + 1], in_=osu[:],
                                                axis=AX, op=OP.max,
                                                apply_absolute_value=True),
                                v, "vec")
                        p.chain(v.tensor_scalar(out=SX[:, k:k + 1],
                                                in0=AM[:, k:k + 1],
                                                scalar1=RQMAX, scalar2=QEPS,
                                                op0=OP.mult, op1=OP.max),
                                v, "vec")
                        p.inc(v.reciprocal(RX[:, k:k + 1], SX[:, k:k + 1]),
                              "vec")
                        assert p.cnt["vec"] == TK4["RX", k]
                        p.w(v, "sca", TK4["r2a", k])
                        if k >= 2:
                            p.w(v, "ten", TK4["trQ", k - 2])
                        else:
                            p.w(v, "ten", ten_a1)
                        p.inc(v.tensor_scalar(out=stg4[k % 2][:], in0=osu[:],
                                              scalar1=SX[:, k:k + 1],
                                              scalar2=None, op0=OP.mult),
                              "vec")
                        assert p.cnt["vec"] == TK4["r2b", k]
                assert p.cnt["vec"] == vec_b_end

            @blk.scalar
            def _(s):
                for k in range(NOT_):
                    p.w(s, "vec", TK4["va", k, 3])
                    p.inc(s.activation(out=ost[:], in_=x1t4[k % 2][:],
                                       func=AF.Square,
                                       accum_out=SQ[:, k:k + 1]), "sca")
                    assert p.cnt["sca"] == TK4["SQ", k]
                    p.w(s, "vec", TK4["R0", k])
                    p.inc(s.activation(out=S0[:, k:k + 1], in_=R0[:, k:k + 1],
                                       func=AF.Sqrt), "sca")
                    assert p.cnt["sca"] == TK4["sqrt", k]
                    p.w(s, "vec", TK4["BIA", k])
                    if k >= 1:
                        p.w(s, "vec", TK4["r2b", k - 1])
                    p.inc(s.activation(out=osu[:], in_=x1t4[k % 2][:],
                                       func=AF.Identity, scale=SCL[:, k:k + 1],
                                       bias=BIA[:, k:k + 1]), "sca")
                    assert p.cnt["sca"] == TK4["apply", k]
                    p.w(s, "vec", TK4["RX", k])
                    p.chain(s.activation(out=ost[:], in_=osu[:],
                                         func=AF.Identity,
                                         scale=RX[:, k:k + 1],
                                         bias=mgb[:, 0:1]), s, "sca")
                    p.inc(s.activation(out=osu[:], in_=ost[:],
                                       func=AF.Identity, bias=mgb[:, 1:2]),
                          "sca")
                    assert p.cnt["sca"] == TK4["r2a", k]
                    p.inc(s.dma_start(out=x1_scr[k], in_=x1t4[k % 2][:]),
                          "qp", 16)
            qp_x1 = p.cnt["qp"]

            if "x1" in taps:
                ap = tap("x1", [NOT_ * 128, D], F32)

                @blk.sync
                def _(sync):
                    p.w(sync, "qp", qp_x1)
                    p.inc(sync.dma_start(
                        out=ap, in_=x1_scr.rearrange("a b c -> (a b) c")), "qo", 16)

            assert p.cnt["ten"] == ten_b_end
            vec_xq = p.cnt["vec"]

            if "xq" in taps:
                ap = tap("xq", [128, DT * NO], BF16)

                @blk.sync
                def _(sync):
                    p.w(sync, "vec", vec_xq)
                    p.inc(sync.dma_start(
                        out=ap, in_=xq_fm[:].rearrange("p a b -> p (a b)")), "qo", 16)

        vec_thq = p.cnt["vec"]
        if phase_limit >= 8:
            qga_f0, qgb_f0 = p.cnt["qga"], p.cnt["qgb"]
            ten_f0, vec_f0, sca_f0 = p.cnt["ten"], p.cnt["vec"], p.cnt["sca"]

            def _p_wsem(ppi):
                return "qga" if ppi % 2 == 0 else "qgb"

            def _p_wval(ppi):
                return (qga_f0 if ppi % 2 == 0 else qgb_f0) + 16 * (ppi // 2 + 1)

            def dslot(ppi, which):
                off = (ppi % 2) * 256 + which * 128
                return rrowb[:, off:off + 128]

            def WBG(i):
                return wscb[:, 2 * (i % 8):2 * (i % 8) + 1]

            def WBU(i):
                return wscb[:, 2 * (i % 8) + 1:2 * (i % 8) + 2]

            def WSGc(i):
                return wsc2[:, 2 * (i % 8):2 * (i % 8) + 1]

            def WSUc(i):
                return wsc2[:, 2 * (i % 8) + 1:2 * (i % 8) + 2]

            def t1q(ppi, which):
                return t1qb[2 * (ppi % 2) + which]

            TK = {}
            c = sca_f0
            for it_ in range(IT + 1):
                if it_ < IT:
                    c += 1; TK["gAa2", it_] = c
                if it_ >= 1:
                    c += 1; TK["silu", it_ - 1] = c
                if it_ < IT:
                    c += 1; TK["gBa2", it_] = c
                    c += 1; TK["gAa3", it_] = c
                    c += 1; TK["gBa3", it_] = c
                    c += 1; TK["uAa2", it_] = c
                    c += 1; TK["uBa2", it_] = c
                    c += 1; TK["uAa3", it_] = c
                    c += 1; TK["uBa3", it_] = c
            sca_P5e = c
            c = vec_f0
            for it_ in range(IT + 2):
                if it_ < IT:
                    c += 1; TK["Ag", it_] = c
                    c += 1; TK["Au", it_] = c
                if it_ >= 2:
                    c += 1; TK["E", it_ - 2] = c
                    c += 1; TK["F1", it_ - 2] = c
                    c += 1; TK["F2", it_ - 2] = c
            vec_P5e = c
            c = ten_f0
            for it_ in range(IT + 1):
                if it_ < IT:
                    c += 1; TK["trgA", it_] = c
                if it_ >= 1:
                    c += 1; TK["mmg", it_ - 1] = c
                if it_ < IT:
                    c += 1; TK["trgB", it_] = c
                if it_ >= 1:
                    c += 1; TK["mmu", it_ - 1] = c
                if it_ < IT:
                    c += 1; TK["truA", it_] = c
                    c += 1; TK["truB", it_] = c
            ten_P5e = c

            @blk.gpsimd
            def _(g):
                for j in range(IT):
                    if j >= 2:
                        p.w(g, "ten", TK["truB", j - 2])
                        p.w(g, "vec", TK["Au", j - 2])
                    else:
                        p.w(g, "ten", ten_f0)
                        p.w(g, "vec", vec_xq)
                    p.inc(g.dma_start(out=wbuf[j % 2][:], in_=w1[j]),
                          _p_wsem(j), 16)

            @blk.vector
            def _(v):
                p.w(v, "vec", vec_xq)
                v.memset(hacc[:], 0.0)
                for it_ in range(IT + 2):
                    if it_ < IT:
                        j = it_
                        p.w(v, _p_wsem(j), _p_wval(j))
                        if j >= 8:
                            p.w(v, "sca", TK["silu", j - 8])
                        wbg = wbuf[j % 2][:, 0:D]
                        wbu = wbuf[j % 2][:, D:2 * D]
                        v.tensor_reduce(out=TMPA[:, 0:1], in_=wbg, axis=AX,
                                        op=OP.max, apply_absolute_value=True)
                        v.tensor_reduce(out=TMPA[:, 1:2], in_=wbu, axis=AX,
                                        op=OP.max, apply_absolute_value=True)
                        v.tensor_scalar(out=WSG(j), in0=TMPA[:, 0:1],
                                        scalar1=RQMAX, scalar2=QEPS,
                                        op0=OP.mult, op1=OP.max)
                        v.tensor_scalar(out=WSU(j), in0=TMPA[:, 1:2],
                                        scalar1=RQMAX, scalar2=QEPS,
                                        op0=OP.mult, op1=OP.max)
                        v.reciprocal(WRG(j), WSG(j))
                        v.reciprocal(WRU(j), WSU(j))
                        v.tensor_scalar(out=WBG(j), in0=WRG(j), scalar1=1.0,
                                        scalar2=None, op0=OP.mult)
                        v.tensor_scalar(out=WBU(j), in0=WRU(j), scalar1=1.0,
                                        scalar2=None, op0=OP.mult)
                        v.reciprocal(WSGc(j), WBG(j))
                        v.reciprocal(WSUc(j), WBU(j))
                        p.inc(v.tensor_scalar(out=dslot(j, 0), in0=ident[:],
                                              scalar1=WRG(j), scalar2=None,
                                              op0=OP.mult), "vec")
                        p.inc(v.tensor_scalar(out=dslot(j, 1), in0=ident[:],
                                              scalar1=WRU(j), scalar2=None,
                                              op0=OP.mult), "vec")
                        assert p.cnt["vec"] == TK["Au", j]
                    if it_ >= 2:
                        j = it_ - 2
                        p.w(v, "ten", TK["mmu", j])
                        p.w(v, "sca", TK["silu", j])
                        last = None
                        for half in range(2):
                            last = v.scalar_tensor_tensor(
                                out=h_buf[:, j, half * 512:(half + 1) * 512],
                                in0=ps_mm[2 + half][:], scalar=WSUc(j),
                                in1=gsil[:, half * 512:(half + 1) * 512],
                                op0=OP.mult, op1=OP.mult)
                        p.chain(last, v, "vec")
                        assert p.cnt["vec"] == TK["E", j]
                        p.chain(v.tensor_max(hacc[:], hacc[:], h_buf[:, j, :]),
                                v, "vec")
                        p.inc(v.scalar_tensor_tensor(out=hacc[:],
                                                     in0=h_buf[:, j, :],
                                                     scalar=-1.0, in1=hacc[:],
                                                     op0=OP.mult, op1=OP.max),
                              "vec")

            def emit_tr(t, j, which, hB):
                wbw = wbuf[j % 2][:, which * D:(which + 1) * D]
                last = None
                for cc in range(8):
                    hc = 8 * hB + cc
                    last = t.matmul(
                        ps_ex[cc // 4][:, (cc % 4) * 128:(cc % 4 + 1) * 128],
                        wbw[:, hc * 128:(hc + 1) * 128],
                        dslot(j, which), start=True, stop=True)
                p.inc(last, "ten")

            def emit_mm(t, j, which):
                last = None
                for half in range(2):
                    ps = ps_mm[2 * which + half]
                    for hc in range(DT):
                        last = t.matmul(
                            ps[:],
                            t1q(j, which)[:, hc * 128:(hc + 1) * 128],
                            xq_fm[:, hc, half * 512:(half + 1) * 512],
                            start=(hc == 0), stop=(hc == DT - 1))
                p.inc(last, "ten")

            @blk.tensor
            def _(t):
                for it_ in range(IT + 1):
                    if it_ < IT:
                        if it_ >= 1:
                            p.w(t, "sca", TK["uBa2", it_ - 1])
                        p.w(t, "vec", TK["Au", it_])
                        emit_tr(t, it_, 0, 0)
                        assert p.cnt["ten"] == TK["trgA", it_]
                    if it_ >= 1:
                        p.w(t, "sca", TK["gBa3", it_ - 1])
                        emit_mm(t, it_ - 1, 0)
                        assert p.cnt["ten"] == TK["mmg", it_ - 1]
                    if it_ < IT:
                        p.w(t, "sca", TK["gAa2", it_])
                        emit_tr(t, it_, 0, 1)
                        assert p.cnt["ten"] == TK["trgB", it_]
                    if it_ >= 1:
                        p.w(t, "sca", TK["uBa3", it_ - 1])
                        if it_ >= 2:
                            p.w(t, "vec", TK["E", it_ - 2])
                        emit_mm(t, it_ - 1, 1)
                        assert p.cnt["ten"] == TK["mmu", it_ - 1]
                    if it_ < IT:
                        p.w(t, "sca", TK["gBa2", it_])
                        emit_tr(t, it_, 1, 0)
                        assert p.cnt["ten"] == TK["truA", it_]
                        p.w(t, "sca", TK["uAa2", it_])
                        emit_tr(t, it_, 1, 1)
                        assert p.cnt["ten"] == TK["truB", it_]

            def emit_r12(s, hB, k2):
                base = 1024 * hB
                s.activation(out=uwb[:, base:base + 512],
                             in_=ps_ex[0][:], func=AF.Identity,
                             bias=mgb[:, 0:1])
                p.inc(s.activation(out=uwb[:, base + 512:base + 1024],
                                   in_=ps_ex[1][:], func=AF.Identity,
                                   bias=mgb[:, 0:1]), "sca")
                assert p.cnt["sca"] == k2

            def emit_r3(s, j, which, hB, k3):
                base = 1024 * hB
                p.inc(s.activation(out=t1q(j, which)[:, base:base + 1024],
                                   in_=uwb[:, base:base + 1024],
                                   func=AF.Identity, bias=mgb[:, 1:2]), "sca")
                assert p.cnt["sca"] == k3

            @blk.scalar
            def _(s):
                for it_ in range(IT + 1):
                    if it_ < IT:
                        p.w(s, "ten", TK["trgA", it_])
                        emit_r12(s, 0, TK["gAa2", it_])
                    if it_ >= 1:
                        j = it_ - 1
                        p.w(s, "ten", TK["mmg", j])
                        if j >= 1:
                            p.w(s, "vec", TK["E", j - 1])
                        last = None
                        for half in range(2):
                            last = s.activation(
                                out=gsil[:, half * 512:(half + 1) * 512],
                                in_=ps_mm[half][:], func=AF.Silu,
                                scale=WSGc(j))
                        p.inc(last, "sca")
                        assert p.cnt["sca"] == TK["silu", j]
                    if it_ < IT:
                        p.w(s, "ten", TK["trgB", it_])
                        emit_r12(s, 1, TK["gBa2", it_])
                        emit_r3(s, it_, 0, 0, TK["gAa3", it_])
                        emit_r3(s, it_, 0, 1, TK["gBa3", it_])
                        p.w(s, "ten", TK["truA", it_])
                        emit_r12(s, 0, TK["uAa2", it_])
                        p.w(s, "ten", TK["truB", it_])
                        emit_r12(s, 1, TK["uBa2", it_])
                        emit_r3(s, it_, 1, 0, TK["uAa3", it_])
                        emit_r3(s, it_, 1, 1, TK["uBa3", it_])

            assert p.cnt["vec"] == vec_P5e and p.cnt["ten"] == ten_P5e \
                and p.cnt["sca"] == sca_P5e
            vec_P5 = p.cnt["vec"]
            ten_P5 = p.cnt["ten"]

            @blk.tensor
            def _(t):
                p.w(t, "vec", vec_P5)
                last = None
                for bk in range(8):
                    last = t.transpose(ps_tr[0][:, bk * 128:(bk + 1) * 128],
                                       hacc[:, bk * 128:(bk + 1) * 128], ident[:])
                p.inc(last, "ten")
            ten_sh1 = p.cnt["ten"]

            @blk.vector
            def _(v):
                p.w(v, "ten", ten_sh1)
                last = None
                for bk in range(8):
                    last = v.tensor_reduce(out=SH[:, bk:bk + 1],
                                           in_=ps_tr[0][:, bk * 128:(bk + 1) * 128],
                                           axis=AX, op=OP.max)
                p.chain(last, v, "vec")
                p.chain(v.tensor_scalar(out=SHS[:, 0:8], in0=SH[:, 0:8],
                                        scalar1=RQMAX, scalar2=QEPS,
                                        op0=OP.mult, op1=OP.max), v, "vec")
                p.inc(v.reciprocal(RH[:, 0:8], SHS[:, 0:8]), "vec")
            vec_sh1 = p.cnt["vec"]

            @blk.vector
            def _(v):
                p.w(v, "vec", vec_sh1)
                p.chain(v.memset(rrowb[:, 0:128], 0.0), v, "vec")
                p.inc(v.tensor_copy(out=rrowb[:, 0:8], in_=RH[:, 0:8]), "vec")
            vec_rr1 = p.cnt["vec"]

            @blk.tensor
            def _(t):
                p.w(t, "vec", vec_rr1)
                p.inc(t.transpose(ps_tr[0][:, 0:128], rrowb[:, 0:128],
                                  ident[:]), "ten")
            ten_rr = p.cnt["ten"]

            @blk.vector
            def _(v):
                p.w(v, "ten", ten_rr)
                p.inc(v.tensor_copy(out=stg[0:8, 0:128],
                                    in_=ps_tr[0][0:8, 0:128]), "vec")
            vec_rr2 = p.cnt["vec"]

            @blk.sync
            def _(sync):
                p.w(sync, "vec", vec_rr2)
                p.inc(sync.dma_start(out=rrowb[0:1, 0:1024],
                                     in_=stg[0:8, 0:128]), "qp", 16)
            qp_rr = p.cnt["qp"]

            @blk.tensor
            def _(t):
                p.w(t, "qp", qp_rr)
                last = None
                for hf in range(2):
                    last = t.matmul(ps_mm[hf][:], ones1[0:1, 0:128],
                                    rrowb[0:1, hf * 512:(hf + 1) * 512],
                                    start=True, stop=True)
                p.inc(last, "ten")
            ten_rbc = p.cnt["ten"]

            vec_rbc0 = p.cnt["vec"] + 1
            qga_d0, qgb_d0 = p.cnt["qga"], p.cnt["qgb"]
            ten_d0, sca_d0 = p.cnt["ten"], p.cnt["sca"]
            GRP_SZ = [8, 8, 8, 8, 8, 4]
            QN = [0, 1, 2, 2, 3, 3]
            Q4 = I_FFN // 4

            TK6 = {}
            V_N = 32
            c = vec_rbc0
            c += 4; TK6["A", 0] = c
            for j in range(V_N):
                c += 1; TK6["thq", j] = c
            for ht in range(DT):
                if ht + 1 < DT:
                    c += 4; TK6["A", ht + 1] = c
                for gg in range(6):
                    c += 1; TK6["evac", ht, gg] = c
            vec_P6e = c
            c = sca_d0
            for ht in range(DT):
                for q in range(4):
                    c += 1; TK6["B", ht, q] = c
                if ht >= 1:
                    c += 1; TK6["dfm", ht - 1] = c
            c += 1; TK6["dfm", DT - 1] = c
            sca_P6e = c
            c = ten_d0
            for ht in range(DT):
                for gg in range(6):
                    c += 1; TK6["tr", ht, gg] = c
                c += 1; TK6["mm", ht] = c
            ten_P6e = c
            vec_thq = TK6["thq", V_N - 1]
            gps_t0 = p.cnt["gps"]
            for j in range(V_N, IT):
                TK6["thqp", j] = gps_t0 + (j - V_N) + 1
            gps_thq = gps_t0 + (IT - V_N)

            def emit_w2b(g, ht):
                if ht >= 2:
                    p.w(g, "sca", TK6["B", ht - 2, 3])
                    p.w(g, "vec", TK6["A", ht - 2])
                else:
                    p.w(g, "ten", TK["mmu", IT - 1])
                    p.w(g, "vec", TK["Au", IT - 1])
                p.inc(g.dma_start(out=w2b[ht % 2][:], in_=w2[ht]),
                      "qga" if ht % 2 == 0 else "qgb", 16)

            @blk.gpsimd
            def _(g):
                for ht in range(2):
                    emit_w2b(g, ht)
                p.w(g, "vec", vec_rbc0)
                for j in range(V_N, IT):
                    g.tensor_tensor(out=poob[:], in0=h_buf[:, j, :],
                                    in1=rbc[:], op=OP.mult)
                    p.inc(g.tensor_scalar(out=h_buf[:, j, :],
                                          in0=poob[:], scalar1=MAGIC,
                                          scalar2=MAGIC, op0=OP.add,
                                          op1=OP.subtract), "gps")
                assert p.cnt["gps"] == gps_thq
                for ht in range(2, DT):
                    emit_w2b(g, ht)

            def emit_A(v, ht):
                p.w(v, "qga" if ht % 2 == 0 else "qgb",
                    (qga_d0 if ht % 2 == 0 else qgb_d0) + 16 * (ht // 2 + 1))
                if ht >= 8:
                    p.w(v, "sca", TK6["dfm", ht - 8])
                wb2 = w2b[ht % 2]
                v.tensor_reduce(out=TMPA[:, 0:1], in_=wb2[:, 0:I_FFN // 2],
                                axis=AX, op=OP.max, apply_absolute_value=True)
                p.chain(v.tensor_reduce(out=TMPA[:, 1:2],
                                        in_=wb2[:, I_FFN // 2:],
                                        axis=AX, op=OP.max,
                                        apply_absolute_value=True), v, "vec")
                p.chain(v.tensor_max(TMPA[:, 2:3], TMPA[:, 0:1],
                                     TMPA[:, 1:2]), v, "vec")
                p.chain(v.tensor_scalar(out=WS2(ht), in0=TMPA[:, 2:3],
                                        scalar1=RQMAX, scalar2=QEPS,
                                        op0=OP.mult, op1=OP.max), v, "vec")
                p.inc(v.reciprocal(WR2(ht), WS2(ht)), "vec")
                assert p.cnt["vec"] == TK6["A", ht]

            @blk.vector
            def _(v):
                p.w(v, "ten", ten_rbc)
                v.tensor_copy(out=rbc[:, 0:512], in_=ps_mm[0][:])
                p.chain(v.tensor_copy(out=rbc[:, 512:1024], in_=ps_mm[1][:]),
                        v, "vec")
                assert p.cnt["vec"] == vec_rbc0
                emit_A(v, 0)
                tsc = [ost[:, 0:1024], ost[:, 1024:2048]]
                for j in range(V_N + 1):
                    if j < V_N:
                        v.tensor_tensor(out=tsc[j % 2], in0=h_buf[:, j, :],
                                        in1=rbc[:], op=OP.mult)
                    if j >= 1:
                        p.inc(v.tensor_scalar(out=h_buf[:, j - 1, :],
                                              in0=tsc[(j - 1) % 2],
                                              scalar1=MAGIC, scalar2=MAGIC,
                                              op0=OP.add, op1=OP.subtract),
                              "vec")
                        assert p.cnt["vec"] == TK6["thq", j - 1]
                for ht in range(DT):
                    if ht + 1 < DT:
                        emit_A(v, ht + 1)
                    for gg, sz in enumerate(GRP_SZ):
                        p.w(v, "ten", TK6["tr", ht, gg])
                        off = sum(GRP_SZ[:gg])
                        p.inc(v.tensor_copy(
                            out=t2q[:, off * 128:(off + sz) * 128],
                            in_=ps_tr[gg % 2][:, 0:sz * 128]), "vec")
                        assert p.cnt["vec"] == TK6["evac", ht, gg]
                assert p.cnt["vec"] == vec_P6e

            @blk.scalar
            def _(s):
                for ht in range(DT):
                    p.w(s, "vec", TK6["A", ht])
                    p.w(s, "qga" if ht % 2 == 0 else "qgb",
                        (qga_d0 if ht % 2 == 0 else qgb_d0) + 16 * (ht // 2 + 1))
                    if ht >= 1:
                        p.w(s, "ten", TK6["tr", ht - 1, 5])
                    wb2 = w2b[ht % 2]
                    def a1(qq):
                        s.activation(out=wuq[qq % 2][:],
                                     in_=wb2[:, qq * Q4:(qq + 1) * Q4],
                                     func=AF.Identity, scale=WR2(ht),
                                     bias=mgb[:, 0:1])

                    def a2(qq):
                        p.inc(s.activation(out=s2n[:, qq * Q4:(qq + 1) * Q4],
                                           in_=wuq[qq % 2][:],
                                           func=AF.Identity,
                                           bias=mgb[:, 1:2]), "sca")

                    a1(0); a1(1); a2(0); a1(2); a2(1); a1(3); a2(2); a2(3)
                    assert p.cnt["sca"] == TK6["B", ht, 3]
                    if ht >= 1:
                        p.w(s, "ten", TK6["mm", ht - 1])
                        last = None
                        for half in range(2):
                            last = s.activation(
                                out=down_fm[:, ht - 1,
                                            half * 512:(half + 1) * 512],
                                in_=ps_mm[half][:], func=AF.Identity,
                                scale=WS2(ht - 1))
                        p.inc(last, "sca")
                        assert p.cnt["sca"] == TK6["dfm", ht - 1]
                p.w(s, "ten", TK6["mm", DT - 1])
                last = None
                for half in range(2):
                    last = s.activation(
                        out=down_fm[:, DT - 1, half * 512:(half + 1) * 512],
                        in_=ps_mm[half][:], func=AF.Identity,
                        scale=WS2(DT - 1))
                p.inc(last, "sca")
                assert p.cnt["sca"] == sca_P6e

            @blk.tensor
            def _(t):
                for ht in range(DT):
                    off = 0
                    for gg, sz in enumerate(GRP_SZ):
                        p.w(t, "sca", TK6["B", ht, QN[gg]])
                        if gg >= 2:
                            p.w(t, "vec", TK6["evac", ht, gg - 2])
                        elif ht >= 1:
                            p.w(t, "vec", TK6["evac", ht - 1, gg + 4])
                        last = None
                        for j in range(sz):
                            ic = off + j
                            last = t.transpose(
                                ps_tr[gg % 2][:, j * 128:(j + 1) * 128],
                                s2n[:, ic * 128:(ic + 1) * 128], ident[:])
                        p.inc(last, "ten")
                        assert p.cnt["ten"] == TK6["tr", ht, gg]
                        off += sz
                    p.w(t, "vec", TK6["evac", ht, 5])
                    if ht >= 1:
                        p.w(t, "sca", TK6["dfm", ht - 1])
                    last = None
                    for half in range(2):
                        ps = ps_mm[half]
                        for ic in range(IT):
                            if ht == 0 and half == 0:
                                if ic < V_N:
                                    p.w(t, "vec", TK6["thq", ic])
                                else:
                                    p.w(t, "gps", TK6["thqp", ic])
                            last = t.matmul(
                                ps[:], t2q[:, ic * 128:(ic + 1) * 128],
                                h_buf[:, ic, half * 512:(half + 1) * 512],
                                start=(ic == 0), stop=(ic == IT - 1))
                    p.inc(last, "ten")
                    assert p.cnt["ten"] == TK6["mm", ht]
            ten_P6 = p.cnt["ten"]

            if "thq" in taps:
                ap = tap("thq", [128, IT * NO], BF16)

                @blk.sync
                def _(sync):
                    p.w(sync, "vec", vec_thq)
                    p.w(sync, "gps", gps_thq)
                    p.inc(sync.dma_start(
                        out=ap, in_=h_buf[:].rearrange("p a b -> p (a b)")),
                        "qo", 16)

            sca_P6 = p.cnt["sca"]

            ten_fin0, vec_fin0 = p.cnt["ten"], p.cnt["vec"]
            qsa_f0, qsb_f0 = p.cnt["qsa"], p.cnt["qsb"]
            qo_f0 = p.cnt["qo"]
            obuf = [ost, osu]

            @blk.sync
            def _(sync):
                p.w(sync, "sca", sca_P6)
                p.w(sync, "qp", qp_x1)
                for nt in range(NOT_):
                    if nt >= 2:
                        p.w(sync, "vec", vec_fin0 + 2 * (nt - 2) + 2)
                    p.inc(sync.dma_start(out=xbuf[nt % 2][:], in_=x1_scr[nt]),
                          _ldsem(nt), 16)

            @blk.scalar
            def _(s):
                for nt in range(NOT_):
                    p.w(s, "vec", vec_fin0 + 2 * nt + 2)
                    p.inc(s.dma_start(out=out[nt * 128:(nt + 1) * 128, :],
                                      in_=obuf[nt % 2][:]), "qo", 16)

            @blk.tensor
            def _(t):
                p.w(t, "sca", sca_P6)
                for nt in range(NOT_):
                    if nt >= 1:
                        p.w(t, "vec", vec_fin0 + 2 * (nt - 1) + 1)
                    last = None
                    for ht in range(DT):
                        last = t.transpose(
                            ps_tr[ht // 8][:, (ht % 8) * 128:(ht % 8 + 1) * 128],
                            down_fm[:, ht, nt * 128:(nt + 1) * 128], ident[:])
                    p.inc(last, "ten")

            @blk.vector
            def _(v):
                for nt in range(NOT_):
                    p.w(v, "ten", ten_fin0 + nt + 1)
                    p.w(v, _ldsem(nt), _ldval(nt, qsa_f0, qsb_f0))
                    if nt >= 2:
                        p.w(v, "qo", qo_f0 + 16 * (nt - 1))
                    last = None
                    for half in range(2):
                        last = v.tensor_scalar(
                            out=obuf[nt % 2][:, half * 1024:(half + 1) * 1024],
                            in0=ps_tr[half][:],
                            scalar1=SHS[:, nt:nt + 1], scalar2=None,
                            op0=OP.mult)
                    p.chain(last, v, "vec")
                    p.inc(v.tensor_add(obuf[nt % 2][:], obuf[nt % 2][:],
                                       xbuf[nt % 2][:]), "vec")



        @blk.sync
        def _(sync):
            for nm in ("qs", "qsa", "qsb", "qga", "qgb", "qp", "qo"):
                p.w(sync, nm, p.cnt[nm])

    return nc, tap_aps


def _tile_w(W):
    O, Din = W.shape
    A = W.reshape(O // 256, 2, 128, Din // 128, 128)
    A = A.transpose(0, 4, 1, 3, 2)
    return np.ascontiguousarray(A.reshape(O // 256, 128, 2 * Din))


def _perm_for_core(c):
    if c % 2 == 0:
        return (0, 3), (1, 2)
    return (1, 2), (0, 3)


def _moff_for_core(c):
    own, partner = _perm_for_core(c)
    blocks = [own[0], own[1], partner[0], partner[1]]
    gstart = [512 * b for b in blocks]
    offs = []
    for jt in JSET_A:
        offs.append(gstart[0] - (gstart[jt // 4] + 128 * (jt % 4)))
    for jt in JSET_B:
        offs.append(gstart[1] - (gstart[jt // 4] + 128 * (jt % 4)))
    m = np.zeros((128, 24), np.float32)
    m[:] = np.array(offs, np.float32)[None, :]
    return m


def make_inputs_for_core(c, X, W):
    own, partner = _perm_for_core(c)
    xb = X[c // 2]
    rows = np.concatenate([xb[512 * blk:512 * (blk + 1)]
                           for blk in (own + partner)], axis=0)
    d = {"x_in": np.ascontiguousarray(rows), "moff": _moff_for_core(c)}
    d.update(W)
    return d


def prep_weights(L0_attn_Wq, L0_attn_Wk, L0_attn_Wv, L0_attn_Wg, L0_attn_Wp,
                 L0_ffn_w_gate_up, L0_ffn_w_down):
    return {
        "wq": _tile_w(np.asarray(L0_attn_Wq, np.float32)),
        "wk": _tile_w(np.asarray(L0_attn_Wk, np.float32)),
        "wv": _tile_w(np.asarray(L0_attn_Wv, np.float32)),
        "wg": _tile_w(np.asarray(L0_attn_Wg, np.float32)),
        "wp": np.ascontiguousarray(
            np.asarray(L0_attn_Wp, np.float32).T.reshape(DT, 128, D)),
        "w1": np.ascontiguousarray(
            np.asarray(L0_ffn_w_gate_up, np.float32).reshape(2, IT, 128, D)
            .transpose(1, 2, 0, 3).reshape(IT, 128, 2 * D)),
        "w2": np.ascontiguousarray(
            np.asarray(L0_ffn_w_down, np.float32).reshape(DT, 128, I_FFN)),
    }


def kernel(L0_attn_x, L0_attn_Wq, L0_attn_Wk, L0_attn_Wv, L0_attn_Wg,
           L0_attn_Wp, L0_ffn_w_gate_up, L0_ffn_w_down):
    X = np.ascontiguousarray(np.asarray(L0_attn_x, np.float32))
    W = prep_weights(L0_attn_Wq, L0_attn_Wk, L0_attn_Wv, L0_attn_Wg,
                     L0_attn_Wp, L0_ffn_w_gate_up, L0_ffn_w_down)
    nc, _ = build()
    in_maps = [make_inputs_for_core(c, X, W) for c in range(8)]
    res = run_bass_kernel_spmd(nc, in_maps, core_ids=list(range(8)))
    outp = np.empty((4, 2048, 2048), np.float32)
    for c in range(8):
        own, _ = _perm_for_core(c)
        o = np.asarray(res.results[c]["out"], np.float32)
        b = c // 2
        outp[b, 512 * own[0]:512 * (own[0] + 1)] = o[0:512]
        outp[b, 512 * own[1]:512 * (own[1] + 1)] = o[512:1024]
    return outp


# version 12
# speedup vs baseline: 1.1107x; 1.1107x over previous
import numpy as np
import ml_dtypes
import concourse.bass as bass
import concourse.mybir as mybir
from concourse.bass_utils import run_bass_kernel_spmd

F32 = mybir.dt.float32
BF16 = mybir.dt.bfloat16
OP = mybir.AluOpType
AF = mybir.ActivationFunctionType
AX = mybir.AxisListType.X

MAGIC = 12582912.0
LN1_EPS = 1e-5
RMS_EPS = float(np.finfo(np.float32).eps)
RQMAX = 1.0 / 127.0
QEPS = 1e-8

D = 2048
NB = 2048
NO = 1024
DT = 16
NT = 16
NOT_ = 8
I_FFN = 5632
IT = 44
KB = 1024

JSET_A = [0, 1, 2, 3, 8, 9, 10, 11]
JSET_B = list(range(16))


class P:
    def __init__(self, nc, st, names):
        self.sem = {}
        self.cnt = {}
        for n in names:
            self.sem[n] = st.enter_context(nc.semaphore(n))
            self.cnt[n] = 0

    def inc(self, inst, name, k=1):
        inst.then_inc(self.sem[name], k)
        self.cnt[name] += k
        return self.cnt[name]

    def tick(self, eng, name):
        eng.sem_inc(self.sem[name], 1)
        self.cnt[name] += 1
        return self.cnt[name]

    def w(self, eng, name, v):
        if v > 0:
            eng.wait_ge(self.sem[name], v)

    def chain(self, inst, eng, name):
        self.inc(inst, name)
        eng.wait_ge(self.sem[name], self.cnt[name])

    def sw(self, eng, name):
        if self.cnt[name] > 0:
            eng.wait_ge(self.sem[name], self.cnt[name])


def build(taps=(), phase_limit=99):
    taps = set(taps)
    nc = bass.Bass("TRN2", target_bir_lowering=False,
                   detect_race_conditions=False)

    x_in = nc.dram_tensor("x_in", [NB, D], F32, kind="ExternalInput").ap()
    wq = nc.dram_tensor("wq", [8, 128, 2 * D], F32, kind="ExternalInput").ap()
    wk = nc.dram_tensor("wk", [8, 128, 2 * D], F32, kind="ExternalInput").ap()
    wv = nc.dram_tensor("wv", [8, 128, 2 * D], F32, kind="ExternalInput").ap()
    wg = nc.dram_tensor("wg", [8, 128, 2 * D], F32, kind="ExternalInput").ap()
    wp = nc.dram_tensor("wp", [DT, 128, D], F32, kind="ExternalInput").ap()
    w1 = nc.dram_tensor("w1", [IT, 128, 2 * D], F32, kind="ExternalInput").ap()
    w2 = nc.dram_tensor("w2", [DT, 128, I_FFN], F32, kind="ExternalInput").ap()
    moff = nc.dram_tensor("moff", [128, 24], F32, kind="ExternalInput").ap()

    out = nc.dram_tensor("out", [NO, D], F32, kind="ExternalOutput").ap()

    xn_scr = nc.dram_tensor("xn_scr", [4, 128, DT * 512], BF16).ap()
    v_scr = nc.dram_tensor("v_scr", [NT, 4, 128, 512], BF16).ap()
    g_scr = nc.dram_tensor("g_scr", [NOT_, 4, 128, 512], BF16).ap()
    og_scr = nc.dram_tensor("og_scr", [NOT_, 128, D], BF16).ap()
    x1_scr = nc.dram_tensor("x1_scr", [NOT_, 128, D], F32).ap()

    tap_aps = {}

    def tap(name, shape, dtype=F32):
        if name in taps:
            tap_aps[name] = nc.dram_tensor("tap_" + name, shape, dtype,
                                           kind="ExternalOutput").ap()
            return tap_aps[name]
        return None

    from contextlib import ExitStack
    with ExitStack() as st:
        _allocs = []

        BASE = 24 * KB

        def at(name, shape, dt, off_b):
            per = int(np.prod(shape[1:])) * (2 if dt == BF16 else 4)
            _allocs.append((name, off_b, off_b + per))
            assert BASE + off_b + per <= 224 * KB, f"{name} over budget"
            return nc.alloc_sbuf_tensor_at(name, shape, dt, offset=BASE + off_b)

        k_fm = at("k_fm", [128, DT, NB], BF16, 0)
        h_buf = at("h_buf", [128, IT, NO], BF16, 0)

        xn_fm = at("xn_fm", [128, 2, DT, NO], BF16, 64 * KB)
        q_fm = xn_fm[:, 1]
        xbuf = [at(f"xbuf{i}", [128, D], F32, (64 + 8 * i) * KB) for i in range(2)]
        xbuf1 = [at("xbuf1a", [128, D], F32, 161 * KB + 512),
                 at("xbuf1b", [128, D], F32, 169 * KB + 512)]
        xb2 = [at(f"xb2{i}", [128, D], F32, (64 + 8 * i) * KB)
               for i in range(2)]
        stgP1 = [at(f"stgP1{i}", [128, D], BF16, (148 + 4 * i) * KB)
                 for i in range(2)]
        vevq = [at(f"vevq{i}", [128, 512], BF16, (146 + i) * KB)
                for i in range(2)]
        x1t = at("x1t", [128, D], F32, 80 * KB)
        sqdump = at("sqdump", [128, D], F32, 80 * KB)
        stg = at("stg", [128, D], BF16, 88 * KB)
        stg2 = at("stg2", [128, D], BF16, 92 * KB)
        s2n = at("s2n", [128, I_FFN], BF16, 88 * KB)

        x4buf = [at(f"x4buf{i}", [128, D], F32, (32 + 8 * i) * KB)
                 for i in range(2)]
        x1t4 = [at(f"x1t4{i}", [128, D], F32, (48 + 8 * i) * KB)
                for i in range(2)]
        og_fm4 = at("og_fm4", [128, DT, NO], BF16, 64 * KB)
        stg4 = [at(f"stg4{i}", [128, D], BF16, (161 + 4 * i) * KB + 512)
                for i in range(2)]
        osu = at("osu", [128, D], F32, 169 * KB + 512)
        poob = at("poob", [128, 1024], F32, 161 * KB + 512)
        wpb = [at(f"wpb{dc}", [128, D], BF16,
                  (4 * dc if dc < 8 else 128 + 4 * (dc - 8)) * KB)
               for dc in range(DT)]
        stq = at("stq", [128, 16, 512], BF16, 64 * KB)
        vq = at("vq", [128, 16, 512], BF16, 80 * KB)
        og_fm = at("og_fm", [128, DT, NO], BF16, 96 * KB)
        xq_fm = at("xq_fm", [128, DT, NO], BF16, 96 * KB)
        down_fm = at("down_fm", [128, DT, NO], BF16, 100 * KB)

        wbuf = [at(f"wbuf{i}", [128, 2 * D], BF16, (128 + 8 * i) * KB)
                for i in range(2)]
        gtile = at("gtile", [128, 4, D], BF16, 128 * KB)
        vq2 = at("vq2", [128, 16, 512], BF16, 144 * KB)
        vstg = [at(f"vstg{i}", [128, 512], BF16, (144 + i) * KB) for i in range(2)]
        t1qb = [at(f"t1q{i}", [128, D], BF16, (144 + 4 * i) * KB) for i in range(4)]
        uwb = at("uwb", [128, D], F32, 88 * KB)
        w2b = [at(f"w2b{i}", [128, I_FFN], BF16, (132 + 11 * i) * KB)
               for i in range(2)]
        t2q = at("t2q", [128, IT * 128], BF16, 161 * KB + 512)

        ident = at("ident", [128, 128], BF16, 160 * KB)
        ones1 = at("ones1", [1, 512], BF16, 160 * KB + 256)
        ogst = [at(f"ogst{i}", [128, 512], BF16, 165 * KB + 256 + i * KB)
                for i in range(2)]
        masks3 = at("masks3", [128, 16, 512], BF16, 167 * KB + 256 + 256)
        gsil = at("gsil", [128, NO], F32, 167 * KB + 512)
        hacc = at("hacc", [128, NO], BF16, 172 * KB)
        rbc = at("rbc", [128, NO], F32, 128 * KB)
        rrowb = at("rrowb", [128, 1024], BF16, 176 * KB)
        wuq = [at(f"wuq{i}", [128, I_FFN // 4], F32,
                   (172 if i == 0 else 178) * KB + 512) for i in range(2)]

        iota = at("iota", [128, 512], F32, 184 * KB)
        sv = at("sv", [128, 1024], F32, 186 * KB)
        ost = at("ost", [128, D], F32, 190 * KB)
        mofb = at("mofb", [128, 24], F32, 198 * KB)
        wsc = at("wsc", [128, 64], F32, 198 * KB + 128)
        wscb = at("wscb", [128, 16], BF16, 178 * KB)
        wsc2 = at("wsc2", [128, 16], F32, 178 * KB + 128)
        mgb = at("mgb", [128, 2], F32, 178 * KB + 320)

        ps_mm = [st.enter_context(nc.psum_tensor(f"ps_mm{i}", [128, 512], F32))
                 for i in range(4)]
        ps_tr = [st.enter_context(nc.psum_tensor(f"ps_tr{i}", [128, 1024], BF16))
                 for i in range(2)]
        ps_ex = [st.enter_context(nc.psum_tensor(f"ps_ex{i}", [128, 512], F32))
                 for i in range(2)]

        p = P(nc, st, ["qs", "qsa", "qsb", "qga", "qgb", "qp", "qo",
                       "ten", "vec", "sca", "gps"])
        block = st.enter_context(nc.Block())

        class _B:

            def vector(self, f):
                block.vector(lambda v: (p.sw(v, "vec"), f(v))[1])

            def scalar(self, f):
                block.scalar(lambda s_: (p.sw(s_, "sca"), f(s_))[1])

            def tensor(self, f):
                block.tensor(lambda t: (p.sw(t, "ten"), f(t))[1])

            def gpsimd(self, f):
                block.gpsimd(lambda g: (p.sw(g, "gps"), f(g))[1])

            def sync(self, f):
                block.sync(f)

        blk = _B()

        SUM = sv[:, 0:16]
        SQ = sv[:, 16:32]
        M0 = sv[:, 32:48]
        E0 = sv[:, 48:64]
        V0 = sv[:, 64:80]
        T0 = sv[:, 80:96]
        R0 = sv[:, 96:112]
        S0 = sv[:, 112:128]
        N1 = sv[:, 128:144]
        N2 = sv[:, 144:160]
        SCL = sv[:, 160:176]
        BIA = sv[:, 176:192]
        AM = sv[:, 192:200]
        SX = sv[:, 200:208]
        RX = sv[:, 208:216]
        SH = sv[:, 216:224]
        SHS = sv[:, 224:232]
        RH = sv[:, 232:240]
        TMPA = sv[:, 240:248]

        def WSG(i):
            return wsc[:, i % 8:i % 8 + 1]

        def WRG(i):
            return wsc[:, 8 + i % 8:8 + i % 8 + 1]

        def WSU(i):
            return wsc[:, 16 + i % 8:16 + i % 8 + 1]

        def WRU(i):
            return wsc[:, 24 + i % 8:24 + i % 8 + 1]

        def WS2(i):
            return wsc[:, 32 + i % 8:32 + i % 8 + 1]

        def WR2(i):
            return wsc[:, 40 + i % 8:40 + i % 8 + 1]

        @blk.sync
        def _(sync):
            p.inc(sync.dma_start(out=mofb[:], in_=moff), "qs", 16)
        moff_v = p.cnt["qs"]

        @blk.vector
        def _(v):
            v.memset(ident[:], 0.0)
            p.inc(v.memset(ones1[:], 1.0), "vec")
        vec_ms = p.cnt["vec"]

        @blk.gpsimd
        def _(g):
            p.w(g, "vec", vec_ms)
            p.inc(g.affine_select(out=ident[:], in_=ident[:],
                                  compare_op=OP.not_equal, fill=1.0, base=0,
                                  pattern=[[-1, 128]], channel_multiplier=1),
                  "gps")
            p.inc(g.iota(out=iota[:], pattern=[[1, 512]], base=0,
                         channel_multiplier=-1,
                         allow_small_or_imprecise_dtypes=True), "gps")
        gps_p0 = p.cnt["gps"]

        if "iota" in taps:
            ap = tap("iota", [128, 512], F32)

            @blk.sync
            def _(sync):
                p.w(sync, "gps", gps_p0)
                p.inc(sync.dma_start(out=ap, in_=iota[:]), "qo", 16)

        qsa0, qsb0 = p.cnt["qsa"], p.cnt["qsb"]

        def _ldsem(r):
            return "qsa" if r % 2 == 0 else "qsb"

        def _ldval(r, base_a=qsa0, base_b=qsb0):
            return (base_a if r % 2 == 0 else base_b) + 16 * (r // 2 + 1)

        ORD1 = list(range(8, 16)) + list(range(8))
        vec_10, sca_10, ten_10 = p.cnt["vec"], p.cnt["sca"], p.cnt["ten"]
        qs_10 = p.cnt["qs"]
        TK1 = {}
        c = vec_10
        for i in range(8):
            c += 1; TK1["SUMr", i] = c
        c += 4; TK1["R0p"] = c
        c += 6; TK1["BIAp"] = c
        for i in range(8, 16):
            c += 1; TK1["SUMr", i] = c
            c += 1; TK1["evac", i - 8] = c
        c += 4; TK1["R0o"] = c
        c += 6; TK1["BIAo"] = c
        for j in range(8, 16):
            c += 1; TK1["evac", j] = c
        vec_P1e = c
        c = sca_10
        for i in range(8):
            c += 1; TK1["SQ", i] = c
        c += 1; TK1["sqp"] = c
        for i in range(8, 16):
            c += 1; TK1["SQ", i] = c
            c += 1; TK1["apply", i - 8] = c
        c += 1; TK1["sqo"] = c
        for j in range(8, 16):
            c += 1; TK1["apply", j] = c
        c = ten_10
        for j in range(NT):
            c += 1; TK1["tr", j] = c

        @blk.sync
        def _(sync):
            for i, r in enumerate(ORD1):
                if i >= 2:
                    p.w(sync, "vec", TK1["SUMr", i - 2])
                    p.w(sync, "sca", TK1["SQ", i - 2])
                p.inc(sync.dma_start(out=xbuf1[i % 2][:],
                                     in_=x_in[r * 128:(r + 1) * 128, :]),
                      _ldsem(i), 16)

        def issue_b(s, i):
            if i >= NT:
                return
            buf = xb2[i % 2] if i < 8 else xbuf1[i % 2]
            if i == 8:
                p.w(s, "vec", TK1["SUMr", 14])
                p.w(s, "sca", TK1["SQ", 14])
            if i == 9:
                p.w(s, "vec", TK1["SUMr", 15])
                p.w(s, "sca", TK1["SQ", 15])
            if i >= 2:
                p.w(s, "sca", TK1["apply", i - 2])
            r = ORD1[i]
            p.inc(s.dma_start(out=buf[:],
                              in_=x_in[r * 128:(r + 1) * 128, :]), "qs", 16)

        def scale_batch(v, lo, hi, kR):
            sl = slice(8 * lo // 8, hi)

            def cc(t_):
                return t_[:, lo:hi]

            v.tensor_scalar(out=cc(M0), in0=cc(SUM), scalar1=1.0 / D,
                            scalar2=None, op0=OP.mult)
            v.tensor_scalar(out=cc(E0), in0=cc(SQ), scalar1=1.0 / D,
                            scalar2=None, op0=OP.mult)
            p.chain(v.tensor_mul(cc(V0), cc(M0), cc(M0)), v, "vec")
            p.chain(v.tensor_sub(cc(V0), cc(E0), cc(V0)), v, "vec")
            p.chain(v.tensor_scalar(out=cc(T0), in0=cc(V0),
                                    scalar1=1.0 + RMS_EPS,
                                    scalar2=RMS_EPS * LN1_EPS,
                                    op0=OP.mult, op1=OP.add), v, "vec")
            p.inc(v.reciprocal(cc(R0), cc(T0)), "vec")
            assert p.cnt["vec"] == TK1[kR]

        def newton_batch(v, lo, hi, kB):
            def cc(t_):
                return t_[:, lo:hi]

            p.chain(v.tensor_mul(cc(N1), cc(S0), cc(S0)), v, "vec")
            p.chain(v.tensor_mul(cc(N1), cc(T0), cc(N1)), v, "vec")
            p.chain(v.tensor_scalar(out=cc(N1), in0=cc(N1), scalar1=-0.5,
                                    scalar2=1.5, op0=OP.mult, op1=OP.add),
                    v, "vec")
            p.chain(v.tensor_mul(cc(SCL), cc(S0), cc(N1)), v, "vec")
            p.chain(v.tensor_mul(cc(N2), cc(M0), cc(SCL)), v, "vec")
            p.inc(v.tensor_scalar(out=cc(BIA), in0=cc(N2), scalar1=-1.0,
                                  scalar2=None, op0=OP.mult), "vec")
            assert p.cnt["vec"] == TK1[kB]

        @blk.vector
        def _(v):
            for i in range(8):
                p.w(v, _ldsem(i), _ldval(i))
                inst = v.tensor_reduce(out=SUM[:, ORD1[i]:ORD1[i] + 1],
                                       in_=xbuf1[i % 2][:], axis=AX,
                                       op=OP.add)
                if i == 7:
                    p.chain(inst, v, "vec")
                else:
                    p.inc(inst, "vec")
                assert p.cnt["vec"] == TK1["SUMr", i]
            p.w(v, "sca", TK1["SQ", 7])
            scale_batch(v, 8, 16, "R0p")
            p.w(v, "sca", TK1["sqp"])
            newton_batch(v, 8, 16, "BIAp")
            for i in range(8, 16):
                p.w(v, _ldsem(i), _ldval(i))
                inst = v.tensor_reduce(out=SUM[:, ORD1[i]:ORD1[i] + 1],
                                       in_=xbuf1[i % 2][:], axis=AX,
                                       op=OP.add)
                if i == 15:
                    p.chain(inst, v, "vec")
                else:
                    p.inc(inst, "vec")
                assert p.cnt["vec"] == TK1["SUMr", i]
                k = i - 8
                p.w(v, "ten", TK1["tr", k])
                last = None
                rp = ORD1[k]
                for half in range(2):
                    last = v.tensor_copy(
                        out=xn_fm[:, rp // 8, half * 8:(half + 1) * 8,
                                  (rp % 8) * 128:(rp % 8 + 1) * 128],
                        in_=ps_tr[half][:].rearrange("p (c f) -> p c f", c=8))
                p.inc(last, "vec")
                assert p.cnt["vec"] == TK1["evac", k]
            p.w(v, "sca", TK1["SQ", 15])
            scale_batch(v, 0, 8, "R0o")
            p.w(v, "sca", TK1["sqo"])
            newton_batch(v, 0, 8, "BIAo")
            for j in range(8, 16):
                p.w(v, "ten", TK1["tr", j])
                last = None
                rp = ORD1[j]
                for half in range(2):
                    last = v.tensor_copy(
                        out=xn_fm[:, rp // 8, half * 8:(half + 1) * 8,
                                  (rp % 8) * 128:(rp % 8 + 1) * 128],
                        in_=ps_tr[half][:].rearrange("p (c f) -> p c f", c=8))
                p.inc(last, "vec")
                assert p.cnt["vec"] == TK1["evac", j]
            assert p.cnt["vec"] == vec_P1e

        @blk.scalar
        def _(s):
            issue_b(s, 0)
            issue_b(s, 1)
            for i in range(8):
                p.w(s, _ldsem(i), _ldval(i))
                p.sw(s, "sca")
                p.inc(s.activation(out=ost[:], in_=xbuf1[i % 2][:],
                                   func=AF.Square,
                                   accum_out=SQ[:, ORD1[i]:ORD1[i] + 1]),
                      "sca")
                assert p.cnt["sca"] == TK1["SQ", i]
            p.w(s, "vec", TK1["R0p"])
            p.inc(s.activation(out=S0[:, 8:16], in_=R0[:, 8:16],
                               func=AF.Sqrt), "sca")
            assert p.cnt["sca"] == TK1["sqp"]
            for i in range(8, 16):
                p.w(s, _ldsem(i), _ldval(i))
                p.sw(s, "sca")
                p.inc(s.activation(out=ost[:], in_=xbuf1[i % 2][:],
                                   func=AF.Square,
                                   accum_out=SQ[:, ORD1[i]:ORD1[i] + 1]),
                      "sca")
                assert p.cnt["sca"] == TK1["SQ", i]
                k = i - 8
                rp = ORD1[k]
                if k == 0:
                    p.w(s, "vec", TK1["BIAp"])
                p.w(s, "qs", qs_10 + 16 * (k + 1))
                if k >= 2:
                    p.w(s, "ten", TK1["tr", k - 2])
                p.inc(s.activation(out=stgP1[k % 2][:], in_=xb2[k % 2][:],
                                   func=AF.Identity,
                                   scale=SCL[:, rp:rp + 1],
                                   bias=BIA[:, rp:rp + 1]), "sca")
                assert p.cnt["sca"] == TK1["apply", k]
                issue_b(s, k + 2)
            p.w(s, "vec", TK1["R0o"])
            p.inc(s.activation(out=S0[:, 0:8], in_=R0[:, 0:8],
                               func=AF.Sqrt), "sca")
            assert p.cnt["sca"] == TK1["sqo"]
            for j in range(8, 16):
                rp = ORD1[j]
                if j == 8:
                    p.w(s, "vec", TK1["BIAo"])
                p.w(s, "qs", qs_10 + 16 * (j + 1))
                p.w(s, "ten", TK1["tr", j - 2])
                p.inc(s.activation(out=stgP1[j % 2][:], in_=xbuf1[j % 2][:],
                                   func=AF.Identity,
                                   scale=SCL[:, rp:rp + 1],
                                   bias=BIA[:, rp:rp + 1]), "sca")
                assert p.cnt["sca"] == TK1["apply", j]
                issue_b(s, j + 2)

        @blk.tensor
        def _(t):
            p.w(t, "gps", gps_p0)
            for j in range(NT):
                p.w(t, "sca", TK1["apply", j])
                if j >= 1:
                    p.w(t, "vec", TK1["evac", j - 1])
                last = None
                for dc in range(DT):
                    last = t.transpose(
                        ps_tr[dc // 8][:, (dc % 8) * 128:(dc % 8 + 1) * 128],
                        stgP1[j % 2][:, dc * 128:(dc + 1) * 128], ident[:])
                p.inc(last, "ten")
                assert p.cnt["ten"] == TK1["tr", j]

        if "xn_scr" in taps:
            ap = tap("xn_scr", [128, 2 * DT * NO], BF16)

            @blk.sync
            def _(sync):
                p.w(sync, "vec", vec_P1e)
                p.inc(sync.dma_start(
                    out=ap, in_=xn_fm[:].rearrange("p a b c -> p (a b c)")),
                    "qo", 16)

        def emit_proj_fm(wsrc, dst_fm, sp, act, dst_base=None, first=False):
            qga0, qgb0 = p.cnt["qga"], p.cnt["qgb"]
            sca_0, ten_0 = p.cnt["sca"], p.cnt["ten"]
            db = 1024 * sp if dst_base is None else dst_base

            def _wsem(grp):
                return "qga" if grp % 2 == 0 else "qgb"

            def _wval(grp):
                return (qga0 if grp % 2 == 0 else qgb0) + 16 * (grp // 2 + 1)

            @blk.gpsimd
            def _(g):
                for grp in range(8):
                    if grp >= 2:
                        p.w(g, "ten", ten_0 + 4 * (grp - 1))
                    elif not first:
                        p.w(g, "ten", ten_0)
                    p.inc(g.dma_start(out=wbuf[grp % 2][:], in_=wsrc[grp]),
                          _wsem(grp), 16)

            @blk.tensor
            def _(t):
                for ot in range(DT):
                    grp, half = ot // 2, ot % 2
                    if half == 0:
                        p.w(t, _wsem(grp), _wval(grp))
                    for sl in range(2):
                        u = 2 * ot + sl
                        if u >= 2:
                            p.w(t, "sca", sca_0 + u - 1)
                        ps = ps_mm[u % 2]
                        last = None
                        for dc in range(DT):
                            last = t.matmul(ps[:],
                                            wbuf[grp % 2][:, half * D + dc * 128:
                                                          half * D + (dc + 1) * 128],
                                            xn_fm[:, sp, dc, sl * 512:
                                                  (sl + 1) * 512],
                                            start=(dc == 0), stop=(dc == DT - 1))
                        p.inc(last, "ten")

            @blk.scalar
            def _(s):
                for u in range(32):
                    p.w(s, "ten", ten_0 + u + 1)
                    ot, sl = u // 2, u % 2
                    p.inc(s.activation(out=dst_fm[:, ot, db + 512 * sl:
                                               db + 512 * (sl + 1)],
                                       in_=ps_mm[u % 2][:], func=act), "sca")

        def emit_proj_tm(wsrc, scr, sp, act, base_tile):
            qga0, qgb0 = p.cnt["qga"], p.cnt["qgb"]
            sca_0, ten_0, vec_0 = p.cnt["sca"], p.cnt["ten"], p.cnt["vec"]
            qp_0 = p.cnt["qp"]

            def _wsem(grp):
                return "qga" if grp % 2 == 0 else "qgb"

            def _wval(grp):
                return (qga0 if grp % 2 == 0 else qgb0) + 16 * (grp // 2 + 1)

            @blk.gpsimd
            def _(g):
                for grp in range(8):
                    if grp >= 2:
                        p.w(g, "ten", ten_0 + 8 * (grp - 1))
                    else:
                        p.w(g, "ten", ten_0)
                    p.inc(g.dma_start(out=wbuf[grp % 2][:], in_=wsrc[grp]),
                          _wsem(grp), 16)

            @blk.tensor
            def _(t):
                for ot in range(DT):
                    grp, half = ot // 2, ot % 2
                    if half == 0:
                        p.w(t, _wsem(grp), _wval(grp))
                    for sl in range(2):
                        u = 2 * ot + sl
                        if u >= 2:
                            p.w(t, "sca", sca_0 + u - 1)
                        ps = ps_mm[u % 2]
                        last = None
                        for dc in range(DT):
                            last = t.matmul(ps[:],
                                            wbuf[grp % 2][:, half * D + dc * 128:
                                                          half * D + (dc + 1) * 128],
                                            xn_fm[:, sp, dc, sl * 512:
                                                  (sl + 1) * 512],
                                            start=(dc == 0), stop=(dc == DT - 1))
                        p.inc(last, "ten")
                        p.w(t, "sca", sca_0 + u + 1)
                        if u >= 2:
                            p.w(t, "vec", vec_0 + u - 1)
                        for q4 in range(4):
                            last = t.transpose(
                                ps_tr[u % 2][:, q4 * 128:(q4 + 1) * 128],
                                vstg[u % 2][:, q4 * 128:(q4 + 1) * 128],
                                ident[:])
                        p.inc(last, "ten")

            @blk.vector
            def _(v):
                for u in range(32):
                    p.w(v, "ten", ten_0 + 2 * u + 2)
                    if u < 2:
                        p.w(v, "qp", qp_0)
                    else:
                        p.w(v, "qp", qp_0 + 64 * (u - 1))
                    p.inc(v.tensor_copy(out=vevq[u % 2][:],
                                        in_=ps_tr[u % 2][:, 0:512]), "vec")

            @blk.scalar
            def _(s):
                for u in range(33):
                    if u < 32:
                        p.w(s, "ten", ten_0 + 2 * u + 1)
                        p.inc(s.activation(out=vstg[u % 2][:],
                                           in_=ps_mm[u % 2][:], func=act),
                              "sca")
                    if u >= 1:
                        up = u - 1
                        ot, sl = up // 2, up % 2
                        p.w(s, "vec", vec_0 + up + 1)
                        for q4 in range(4):
                            p.inc(s.dma_start(
                                out=scr[base_tile + 4 * sl + q4, ot // 4, :,
                                        (ot % 4) * 128:(ot % 4 + 1) * 128],
                                in_=vevq[up % 2][:, q4 * 128:(q4 + 1) * 128]),
                                "qp", 16)

        sp_marks = {}
        if phase_limit >= 2:
            for sp in (1, 0):
                @blk.tensor
                def _(t, sp=sp):
                    p.w(t, "vec", TK1["evac", 7 if sp == 1 else 15])

                emit_proj_fm(wk, k_fm, sp, AF.Silu, first=(sp == 1))
                if phase_limit >= 3:
                    emit_proj_tm(wv, v_scr, sp, AF.Silu, base_tile=8 * sp)
                if sp == 0 and phase_limit >= 4:
                    emit_proj_tm(wg, g_scr, 0, AF.Sigmoid, base_tile=0)
                if sp == 0 and phase_limit >= 5:
                    qp_now = p.cnt["qp"]

                    @blk.scalar
                    def _(s_, qp_now=qp_now):
                        p.w(s_, "qp", qp_now)
                    emit_proj_fm(wq, q_fm, 0, AF.Silu, dst_base=0)
                sp_marks[("ten_end", sp)] = p.cnt["ten"]

        sca_P2, qp_P2 = p.cnt["sca"], p.cnt["qp"]

        if "k_fm" in taps and phase_limit >= 2:
            ap = tap("k_fm", [128, DT * NB], BF16)

            @blk.sync
            def _(sync):
                p.w(sync, "sca", sca_P2)
                p.inc(sync.dma_start(
                    out=ap, in_=k_fm[:].rearrange("p a b -> p (a b)")), "qo", 16)
        if "q_fm" in taps and phase_limit >= 5:
            ap = tap("q_fm", [128, DT * NO], BF16)

            @blk.sync
            def _(sync):
                p.w(sync, "sca", sca_P2)
                p.inc(sync.dma_start(
                    out=ap, in_=q_fm[:].rearrange("p a b -> p (a b)")), "qo", 16)
        if "v_tm" in taps and phase_limit >= 3:
            ap = tap("v_tm", [NT * 4 * 128, 512], BF16)

            @blk.sync
            def _(sync):
                p.w(sync, "qp", qp_P2)
                p.inc(sync.dma_start(
                    out=ap, in_=v_scr.rearrange("a e b c -> (a e b) c")),
                    "qo", 16)
        if "g_tm" in taps and phase_limit >= 4:
            ap = tap("g_tm", [NOT_ * 4 * 128, 512], BF16)

            @blk.sync
            def _(sync):
                p.w(sync, "qp", qp_P2)
                p.inc(sync.dma_start(
                    out=ap, in_=g_scr.rearrange("a e b c -> (a e b) c")), "qo", 16)

        qp_og = p.cnt["qp"]
        p3m = {}
        if phase_limit >= 6:
            for ib, jset in ((0, JSET_A), (1, JSET_B)):
                nj = len(jset)
                vec_mk0 = p.cnt["vec"]

                @blk.vector
                def _(v, ib=ib, jset=jset):
                    if ib == 0:
                        p.w(v, "qs", moff_v)
                        p.w(v, "gps", gps_p0)
                        p.w(v, "qp", qp_P2)
                    last = None
                    for idx in range(len(jset)):
                        c = idx if ib == 0 else 8 + idx
                        last = v.tensor_scalar(out=masks3[:, idx, :], in0=iota[:],
                                               scalar1=mofb[:, c:c + 1],
                                               scalar2=0.0,
                                               op0=OP.add, op1=OP.is_ge)
                    p.inc(last, "vec")
                vec_mk = p.cnt["vec"]

                ten_s0, vec_s0 = p.cnt["ten"], p.cnt["vec"]

                @blk.tensor
                def _(t, ib=ib, jset=jset, ten_s0=ten_s0, vec_s0=vec_s0,
                      vec_mk=vec_mk):
                    p.w(t, "vec", vec_mk)
                    if ib == 0:
                        p.w(t, "sca", sca_P2)
                    for idx, jt in enumerate(jset):
                        if idx >= 2:
                            p.w(t, "vec", vec_s0 + idx - 1)
                        ps = ps_mm[idx % 2]
                        last = None
                        for dc in range(DT):
                            last = t.matmul(ps[:],
                                            k_fm[:, dc, jt * 128:(jt + 1) * 128],
                                            q_fm[:, dc, ib * 512:(ib + 1) * 512],
                                            start=(dc == 0), stop=(dc == DT - 1))
                        p.inc(last, "ten")

                if ib == 1:
                    p3m["kdone"] = p.cnt["ten"]

                @blk.vector
                def _(v, jset=jset, ten_s0=ten_s0):
                    for idx in range(len(jset)):
                        p.w(v, "ten", ten_s0 + idx + 1)
                        p.inc(v.tensor_tensor(out=stq[:, idx, :],
                                              in0=ps_mm[idx % 2][:],
                                              in1=masks3[:, idx, :],
                                              op=OP.mult), "vec")
                vec_s = p.cnt["vec"]

                qs_g0 = p.cnt["qs"]

                vec_now = p.cnt["vec"]

                @blk.sync
                def _(sync, ib=ib, vec_now=vec_now):
                    if ib == 0:
                        p.w(sync, "qp", qp_P2)
                        p.w(sync, "sca", sca_P2)
                    else:
                        p.w(sync, "vec", vec_now)
                    for it in range(4):
                        for eh in range(4):
                            p.inc(sync.dma_start(
                                out=gtile[:, it, eh * 512:(eh + 1) * 512],
                                in_=g_scr[4 * ib + it, eh]), "qs", 16)
                qs_g = p.cnt["qs"]

                ten_v0, vec_v0 = p.cnt["ten"], p.cnt["vec"]
                qsa_v0, qsb_v0 = p.cnt["qsa"], p.cnt["qsb"]
                qp_o0 = p.cnt["qp"]

                @blk.sync
                def _(sync, ib=ib, jset=jset, vec_v0=vec_v0):
                    nj = len(jset)
                    if ib == 0:
                        p.w(sync, "sca", sca_P2)
                    for eh in range(4):
                        if eh >= 2:
                            p.w(sync, "vec", vec_v0 + 4 * (eh - 1))
                        else:
                            p.w(sync, "vec", vec_v0)
                        buf = vq if eh % 2 == 0 else vq2
                        for idx, jt in enumerate(jset):
                            p.inc(sync.dma_start(
                                out=buf[:, idx, :],
                                in_=v_scr[jt, eh]),
                                "qsa" if eh % 2 == 0 else "qsb", 16)

                @blk.tensor
                def _(t, jset=jset, ten_v0=ten_v0, vec_s=vec_s,
                      qsa_v0=qsa_v0, qsb_v0=qsb_v0, vec_v0=vec_v0):
                    nj = len(jset)
                    p.w(t, "vec", vec_s)
                    for eh in range(4):
                        p.w(t, "qsa" if eh % 2 == 0 else "qsb",
                            (qsa_v0 if eh % 2 == 0 else qsb_v0)
                            + 16 * nj * (eh // 2 + 1))
                        buf = vq if eh % 2 == 0 else vq2
                        for it in range(4):
                            u = 4 * eh + it
                            if u >= 2:
                                p.w(t, "vec", vec_v0 + u - 1)
                            ps = ps_mm[2 + u % 2]
                            last = None
                            for idx in range(nj):
                                last = t.matmul(ps[:],
                                                stq[:, idx, it * 128:(it + 1) * 128],
                                                buf[:, idx, :],
                                                start=(idx == 0),
                                                stop=(idx == nj - 1))
                            p.inc(last, "ten")

                dbg_ap = None
                if ib == 0 and "dbgo" in taps:
                    dbg_ap = tap("dbgo", [128, 3 * 8192], F32)
                    dbg_bf = tap("dbgb", [128, 2 * 8192], BF16)

                @blk.vector
                def _(v, ib=ib, ten_v0=ten_v0, qs_g=qs_g, qp_o0=qp_o0,
                      jset=jset):
                    p.w(v, "qs", qs_g)
                    for eh in range(4):
                        for it in range(4):
                            u = 4 * eh + it
                            p.w(v, "ten", ten_v0 + u + 1)
                            if u >= 2:
                                p.w(v, "qp", qp_o0 + 16 * u)
                            else:
                                p.w(v, "qp", qp_o0)
                            if ib == 0 and eh == 3 and it == 0 and dbg_ap is not None:
                                v.tensor_copy(out=ost[:, 0:512],
                                              in_=ps_mm[2 + u % 2][:])
                            p.inc(v.tensor_tensor(
                                out=ogst[u % 2][:],
                                in0=ps_mm[2 + u % 2][:],
                                in1=gtile[:, it, eh * 512:(eh + 1) * 512],
                                op=OP.mult), "vec")

                @blk.scalar
                def _(s, ib=ib, vec_v0=vec_v0):
                    for eh in range(4):
                        for it in range(4):
                            u = 4 * eh + it
                            p.w(s, "vec", vec_v0 + u + 1)
                            p.inc(s.dma_start(
                                out=og_scr[4 * ib + it, :, eh * 512:(eh + 1) * 512],
                                in_=ogst[u % 2][:]), "qp", 16)

                if dbg_ap is not None and ib == 0:
                    @blk.sync
                    def _(sync, vec_v0=vec_v0):
                        p.w(sync, "vec", vec_v0 + 16)
                        p.inc(sync.dma_start(out=dbg_ap[:, 0:512],
                                             in_=ost[:, 0:512]), "qo", 16)
                        p.inc(sync.dma_start(
                            out=dbg_bf[:, 0:8 * 512],
                            in_=stq[:, 0:8, :].rearrange("p a b -> p (a b)")),
                            "qo", 16)
                        p.inc(sync.dma_start(
                            out=dbg_bf[:, 8192:8192 + 8 * 512],
                            in_=vq2[:, 0:8, :].rearrange("p a b -> p (a b)")),
                            "qo", 16)
            qp_og = p.cnt["qp"]

            if "og" in taps:
                ap = tap("og", [NOT_ * 128, D], BF16)

                @blk.sync
                def _(sync):
                    p.w(sync, "qp", qp_og)
                    p.inc(sync.dma_start(
                        out=ap, in_=og_scr.rearrange("a b c -> (a b) c")), "qo", 16)

        vec_xq = p.cnt["vec"]
        qp_x1 = p.cnt["qp"]
        if phase_limit >= 7:
            ten_a0, vec_a0 = p.cnt["ten"], p.cnt["vec"]
            vec_P3e = p.cnt["vec"]
            ten_P3e = p.cnt["ten"]
            qga_w0, qgb_w0 = p.cnt["qga"], p.cnt["qgb"]
            qsa_a0, qsb_a0 = p.cnt["qsa"], p.cnt["qsb"]

            @blk.gpsimd
            def _(g):
                for dc in range(DT):
                    if dc < 8:
                        p.w(g, "ten", p3m.get("kdone", ten_P3e))
                    else:
                        p.w(g, "ten", ten_P3e)
                        p.w(g, "vec", vec_P3e)
                    p.inc(g.dma_start(out=wpb[dc][:], in_=wp[dc]),
                          "qga" if dc % 2 == 0 else "qgb", 16)

            @blk.sync
            def _(sync):
                p.w(sync, "qp", qp_og)
                for it in range(NOT_):
                    if it >= 2:
                        p.w(sync, "ten", ten_a0 + (it - 1))
                    p.inc(sync.dma_start(out=stg4[it % 2][:],
                                         in_=og_scr[it]), _ldsem(it), 16)

            @blk.tensor
            def _(t):
                for it in range(NOT_):
                    p.w(t, _ldsem(it), _ldval(it, qsa_a0, qsb_a0))
                    if it >= 1:
                        p.w(t, "vec", vec_a0 + it)
                    last = None
                    for ec in range(DT):
                        last = t.transpose(
                            ps_tr[ec // 8][:, (ec % 8) * 128:(ec % 8 + 1) * 128],
                            stg4[it % 2][:, ec * 128:(ec + 1) * 128], ident[:])
                    p.inc(last, "ten")

            @blk.vector
            def _(v):
                for it in range(NOT_):
                    p.w(v, "ten", ten_a0 + it + 1)
                    last = None
                    for half in range(2):
                        last = v.tensor_copy(
                            out=og_fm4[:, half * 8:(half + 1) * 8,
                                       it * 128:(it + 1) * 128],
                            in_=ps_tr[half][:].rearrange("p (c f) -> p c f", c=8))
                    p.inc(last, "vec")
            vec_ogfm = p.cnt["vec"]
            ten_a1 = p.cnt["ten"]

            ten_b0, vec_b0, sca_b0 = p.cnt["ten"], p.cnt["vec"], p.cnt["sca"]
            qsa_b0, qsb_b0 = p.cnt["qsa"], p.cnt["qsb"]
            qp_b0 = p.cnt["qp"]

            TK4 = {}
            c = vec_b0
            for k in range(NOT_ + 1):
                if k < NOT_:
                    for ch in range(4):
                        c += 1; TK4["va", k, ch] = c
                    c += 1; TK4["SUMr", k] = c
                if k >= 1:
                    c += 1; TK4["evQ", k - 1] = c
                if k < NOT_:
                    c += 4; TK4["R0", k] = c
                    c += 6; TK4["BIA", k] = c
                    c += 3; TK4["RX", k] = c
                    c += 1; TK4["r2b", k] = c
            vec_b_end = c
            c = sca_b0
            for k in range(NOT_):
                c += 1; TK4["SQ", k] = c
                c += 1; TK4["sqrt", k] = c
                c += 1; TK4["apply", k] = c
                c += 2; TK4["r2a", k] = c
            c = ten_b0
            for k in range(NOT_ + 1):
                if k < NOT_:
                    for ch in range(4):
                        c += 1; TK4["mmB", k, ch] = c
                if k >= 1:
                    c += 1; TK4["trQ", k - 1] = c
            ten_b_end = c

            @blk.sync
            def _(sync):
                p.w(sync, "ten", ten_P3e)
                for k in range(NOT_):
                    if k >= 2:
                        p.w(sync, "vec", TK4["va", k - 2, 3])
                    p.inc(sync.dma_start(out=x4buf[k % 2][:],
                                         in_=x_in[k * 128:(k + 1) * 128, :]),
                          _ldsem(k), 16)

            @blk.tensor
            def _(t):
                p.w(t, "vec", vec_ogfm)
                for k in range(NOT_ + 1):
                    if k < NOT_:
                        for ch in range(4):
                            if k >= 1:
                                p.w(t, "vec", TK4["va", k - 1, ch])
                            last = None
                            for dc in range(DT):
                                if k == 0 and ch == 0:
                                    p.w(t, "qga" if dc % 2 == 0 else "qgb",
                                        (qga_w0 if dc % 2 == 0 else qgb_w0)
                                        + 16 * (dc // 2 + 1))
                                last = t.matmul(
                                    ps_mm[ch][:],
                                    og_fm4[:, dc, k * 128:(k + 1) * 128],
                                    wpb[dc][:, ch * 512:(ch + 1) * 512],
                                    start=(dc == 0), stop=(dc == DT - 1))
                            p.inc(last, "ten")
                            assert p.cnt["ten"] == TK4["mmB", k, ch]
                    if k >= 1:
                        p.w(t, "vec", TK4["r2b", k - 1])
                        if k >= 2:
                            p.w(t, "vec", TK4["evQ", k - 2])
                        last = None
                        for ec in range(DT):
                            last = t.transpose(
                                ps_tr[ec // 8][:, (ec % 8) * 128:
                                               (ec % 8 + 1) * 128],
                                stg4[(k - 1) % 2][:, ec * 128:(ec + 1) * 128],
                                ident[:])
                        p.inc(last, "ten")
                        assert p.cnt["ten"] == TK4["trQ", k - 1]

            @blk.vector
            def _(v):
                v.memset(mgb[:, 0:1], MAGIC)
                v.memset(mgb[:, 1:2], -MAGIC)
                for k in range(NOT_ + 1):
                    if k < NOT_:
                        p.w(v, _ldsem(k), _ldval(k, qsa_b0, qsb_b0))
                        if k >= 2:
                            p.w(v, "qp", qp_b0 + 16 * (k - 1))
                        for ch in range(4):
                            p.w(v, "ten", TK4["mmB", k, ch])
                            inst = v.tensor_tensor(
                                out=x1t4[k % 2][:, ch * 512:(ch + 1) * 512],
                                in0=ps_mm[ch][:],
                                in1=x4buf[k % 2][:, ch * 512:(ch + 1) * 512],
                                op=OP.add)
                            if ch < 3:
                                p.inc(inst, "vec")
                            else:
                                p.chain(inst, v, "vec")
                            assert p.cnt["vec"] == TK4["va", k, ch]
                        p.inc(v.tensor_reduce(out=SUM[:, k:k + 1],
                                              in_=x1t4[k % 2][:], axis=AX,
                                              op=OP.add), "vec")
                        assert p.cnt["vec"] == TK4["SUMr", k]
                    if k >= 1:
                        p.w(v, "ten", TK4["trQ", k - 1])
                        last = None
                        for half in range(2):
                            last = v.tensor_copy(
                                out=xq_fm[:, half * 8:(half + 1) * 8,
                                          (k - 1) * 128:k * 128],
                                in_=ps_tr[half][:].rearrange(
                                    "p (c f) -> p c f", c=8))
                        p.inc(last, "vec")
                        assert p.cnt["vec"] == TK4["evQ", k - 1]
                    if k < NOT_:
                        p.w(v, "sca", TK4["SQ", k])
                        v.tensor_scalar(out=M0[:, k:k + 1], in0=SUM[:, k:k + 1],
                                        scalar1=1.0 / D, scalar2=None,
                                        op0=OP.mult)
                        v.tensor_scalar(out=E0[:, k:k + 1], in0=SQ[:, k:k + 1],
                                        scalar1=1.0 / D, scalar2=None,
                                        op0=OP.mult)
                        p.chain(v.tensor_mul(V0[:, k:k + 1], M0[:, k:k + 1],
                                             M0[:, k:k + 1]), v, "vec")
                        p.chain(v.tensor_sub(V0[:, k:k + 1], E0[:, k:k + 1],
                                             V0[:, k:k + 1]), v, "vec")
                        p.chain(v.tensor_scalar(out=T0[:, k:k + 1],
                                                in0=V0[:, k:k + 1],
                                                scalar1=1.0, scalar2=LN1_EPS,
                                                op0=OP.mult, op1=OP.add),
                                v, "vec")
                        p.inc(v.reciprocal(R0[:, k:k + 1], T0[:, k:k + 1]),
                              "vec")
                        assert p.cnt["vec"] == TK4["R0", k]
                        p.w(v, "sca", TK4["sqrt", k])
                        p.chain(v.tensor_mul(N1[:, k:k + 1], S0[:, k:k + 1],
                                             S0[:, k:k + 1]), v, "vec")
                        p.chain(v.tensor_mul(N1[:, k:k + 1], T0[:, k:k + 1],
                                             N1[:, k:k + 1]), v, "vec")
                        p.chain(v.tensor_scalar(out=N1[:, k:k + 1],
                                                in0=N1[:, k:k + 1],
                                                scalar1=-0.5, scalar2=1.5,
                                                op0=OP.mult, op1=OP.add),
                                v, "vec")
                        p.chain(v.tensor_mul(SCL[:, k:k + 1], S0[:, k:k + 1],
                                             N1[:, k:k + 1]), v, "vec")
                        p.chain(v.tensor_mul(N2[:, k:k + 1], M0[:, k:k + 1],
                                             SCL[:, k:k + 1]), v, "vec")
                        p.inc(v.tensor_scalar(out=BIA[:, k:k + 1],
                                              in0=N2[:, k:k + 1], scalar1=-1.0,
                                              scalar2=None, op0=OP.mult),
                              "vec")
                        assert p.cnt["vec"] == TK4["BIA", k]
                        p.w(v, "sca", TK4["apply", k])
                        p.chain(v.tensor_reduce(out=AM[:, k:k + 1], in_=osu[:],
                                                axis=AX, op=OP.max,
                                                apply_absolute_value=True),
                                v, "vec")
                        p.chain(v.tensor_scalar(out=SX[:, k:k + 1],
                                                in0=AM[:, k:k + 1],
                                                scalar1=RQMAX, scalar2=QEPS,
                                                op0=OP.mult, op1=OP.max),
                                v, "vec")
                        p.inc(v.reciprocal(RX[:, k:k + 1], SX[:, k:k + 1]),
                              "vec")
                        assert p.cnt["vec"] == TK4["RX", k]
                        p.w(v, "sca", TK4["r2a", k])
                        if k >= 2:
                            p.w(v, "ten", TK4["trQ", k - 2])
                        else:
                            p.w(v, "ten", ten_a1)
                        p.inc(v.tensor_scalar(out=stg4[k % 2][:], in0=osu[:],
                                              scalar1=SX[:, k:k + 1],
                                              scalar2=None, op0=OP.mult),
                              "vec")
                        assert p.cnt["vec"] == TK4["r2b", k]
                assert p.cnt["vec"] == vec_b_end

            @blk.scalar
            def _(s):
                for k in range(NOT_):
                    p.w(s, "vec", TK4["va", k, 3])
                    p.inc(s.activation(out=ost[:], in_=x1t4[k % 2][:],
                                       func=AF.Square,
                                       accum_out=SQ[:, k:k + 1]), "sca")
                    assert p.cnt["sca"] == TK4["SQ", k]
                    p.w(s, "vec", TK4["R0", k])
                    p.inc(s.activation(out=S0[:, k:k + 1], in_=R0[:, k:k + 1],
                                       func=AF.Sqrt), "sca")
                    assert p.cnt["sca"] == TK4["sqrt", k]
                    p.w(s, "vec", TK4["BIA", k])
                    if k >= 1:
                        p.w(s, "vec", TK4["r2b", k - 1])
                    p.inc(s.activation(out=osu[:], in_=x1t4[k % 2][:],
                                       func=AF.Identity, scale=SCL[:, k:k + 1],
                                       bias=BIA[:, k:k + 1]), "sca")
                    assert p.cnt["sca"] == TK4["apply", k]
                    p.w(s, "vec", TK4["RX", k])
                    p.chain(s.activation(out=ost[:], in_=osu[:],
                                         func=AF.Identity,
                                         scale=RX[:, k:k + 1],
                                         bias=mgb[:, 0:1]), s, "sca")
                    p.inc(s.activation(out=osu[:], in_=ost[:],
                                       func=AF.Identity, bias=mgb[:, 1:2]),
                          "sca")
                    assert p.cnt["sca"] == TK4["r2a", k]
                    p.inc(s.dma_start(out=x1_scr[k], in_=x1t4[k % 2][:]),
                          "qp", 16)
            qp_x1 = p.cnt["qp"]

            if "x1" in taps:
                ap = tap("x1", [NOT_ * 128, D], F32)

                @blk.sync
                def _(sync):
                    p.w(sync, "qp", qp_x1)
                    p.inc(sync.dma_start(
                        out=ap, in_=x1_scr.rearrange("a b c -> (a b) c")), "qo", 16)

            assert p.cnt["ten"] == ten_b_end
            vec_xq = p.cnt["vec"]

            if "xq" in taps:
                ap = tap("xq", [128, DT * NO], BF16)

                @blk.sync
                def _(sync):
                    p.w(sync, "vec", vec_xq)
                    p.inc(sync.dma_start(
                        out=ap, in_=xq_fm[:].rearrange("p a b -> p (a b)")), "qo", 16)

        vec_thq = p.cnt["vec"]
        if phase_limit >= 8:
            qga_f0, qgb_f0 = p.cnt["qga"], p.cnt["qgb"]
            ten_f0, vec_f0, sca_f0 = p.cnt["ten"], p.cnt["vec"], p.cnt["sca"]

            def _p_wsem(ppi):
                return "qga" if ppi % 2 == 0 else "qgb"

            def _p_wval(ppi):
                return (qga_f0 if ppi % 2 == 0 else qgb_f0) + 16 * (ppi // 2 + 1)

            def dslot(ppi, which):
                off = (ppi % 2) * 256 + which * 128
                return rrowb[:, off:off + 128]

            def WBG(i):
                return wscb[:, 2 * (i % 8):2 * (i % 8) + 1]

            def WBU(i):
                return wscb[:, 2 * (i % 8) + 1:2 * (i % 8) + 2]

            def WSGc(i):
                return wsc2[:, 2 * (i % 8):2 * (i % 8) + 1]

            def WSUc(i):
                return wsc2[:, 2 * (i % 8) + 1:2 * (i % 8) + 2]

            def t1q(ppi, which):
                return t1qb[2 * (ppi % 2) + which]

            TK = {}
            c = sca_f0
            for it_ in range(IT + 1):
                if it_ < IT:
                    c += 1; TK["gAa2", it_] = c
                if it_ >= 1:
                    c += 1; TK["silu", it_ - 1] = c
                if it_ < IT:
                    c += 1; TK["gBa2", it_] = c
                    c += 1; TK["gAa3", it_] = c
                    c += 1; TK["gBa3", it_] = c
                    c += 1; TK["uAa2", it_] = c
                    c += 1; TK["uBa2", it_] = c
                    c += 1; TK["uAa3", it_] = c
                    c += 1; TK["uBa3", it_] = c
            sca_P5e = c
            c = vec_f0
            for it_ in range(IT + 2):
                if it_ < IT:
                    c += 1; TK["Ag", it_] = c
                    c += 1; TK["Au", it_] = c
                if it_ >= 2:
                    c += 1; TK["E", it_ - 2] = c
                    c += 1; TK["F1", it_ - 2] = c
                    c += 1; TK["F2", it_ - 2] = c
            vec_P5e = c
            c = ten_f0
            for it_ in range(IT + 1):
                if it_ < IT:
                    c += 1; TK["trgA", it_] = c
                if it_ >= 1:
                    c += 1; TK["mmg", it_ - 1] = c
                if it_ < IT:
                    c += 1; TK["trgB", it_] = c
                if it_ >= 1:
                    c += 1; TK["mmu", it_ - 1] = c
                if it_ < IT:
                    c += 1; TK["truA", it_] = c
                    c += 1; TK["truB", it_] = c
            ten_P5e = c

            @blk.gpsimd
            def _(g):
                for j in range(IT):
                    if j >= 2:
                        p.w(g, "ten", TK["truB", j - 2])
                        p.w(g, "vec", TK["Au", j - 2])
                    else:
                        p.w(g, "ten", ten_f0)
                        p.w(g, "vec", vec_xq)
                    p.inc(g.dma_start(out=wbuf[j % 2][:], in_=w1[j]),
                          _p_wsem(j), 16)

            @blk.vector
            def _(v):
                p.w(v, "vec", vec_xq)
                v.memset(hacc[:], 0.0)
                for it_ in range(IT + 2):
                    if it_ < IT:
                        j = it_
                        p.w(v, _p_wsem(j), _p_wval(j))
                        if j >= 8:
                            p.w(v, "sca", TK["silu", j - 8])
                        wbg = wbuf[j % 2][:, 0:D]
                        wbu = wbuf[j % 2][:, D:2 * D]
                        v.tensor_reduce(out=TMPA[:, 0:1], in_=wbg, axis=AX,
                                        op=OP.max, apply_absolute_value=True)
                        v.tensor_reduce(out=TMPA[:, 1:2], in_=wbu, axis=AX,
                                        op=OP.max, apply_absolute_value=True)
                        v.tensor_scalar(out=WSG(j), in0=TMPA[:, 0:1],
                                        scalar1=RQMAX, scalar2=QEPS,
                                        op0=OP.mult, op1=OP.max)
                        v.tensor_scalar(out=WSU(j), in0=TMPA[:, 1:2],
                                        scalar1=RQMAX, scalar2=QEPS,
                                        op0=OP.mult, op1=OP.max)
                        v.reciprocal(WRG(j), WSG(j))
                        v.reciprocal(WRU(j), WSU(j))
                        v.tensor_scalar(out=WBG(j), in0=WRG(j), scalar1=1.0,
                                        scalar2=None, op0=OP.mult)
                        v.tensor_scalar(out=WBU(j), in0=WRU(j), scalar1=1.0,
                                        scalar2=None, op0=OP.mult)
                        v.reciprocal(WSGc(j), WBG(j))
                        v.reciprocal(WSUc(j), WBU(j))
                        p.inc(v.tensor_scalar(out=dslot(j, 0), in0=ident[:],
                                              scalar1=WRG(j), scalar2=None,
                                              op0=OP.mult), "vec")
                        p.inc(v.tensor_scalar(out=dslot(j, 1), in0=ident[:],
                                              scalar1=WRU(j), scalar2=None,
                                              op0=OP.mult), "vec")
                        assert p.cnt["vec"] == TK["Au", j]
                    if it_ >= 2:
                        j = it_ - 2
                        p.w(v, "ten", TK["mmu", j])
                        p.w(v, "sca", TK["silu", j])
                        last = None
                        for half in range(2):
                            last = v.scalar_tensor_tensor(
                                out=h_buf[:, j, half * 512:(half + 1) * 512],
                                in0=ps_mm[2 + half][:], scalar=WSUc(j),
                                in1=gsil[:, half * 512:(half + 1) * 512],
                                op0=OP.mult, op1=OP.mult)
                        p.chain(last, v, "vec")
                        assert p.cnt["vec"] == TK["E", j]
                        p.chain(v.tensor_max(hacc[:], hacc[:], h_buf[:, j, :]),
                                v, "vec")
                        p.inc(v.scalar_tensor_tensor(out=hacc[:],
                                                     in0=h_buf[:, j, :],
                                                     scalar=-1.0, in1=hacc[:],
                                                     op0=OP.mult, op1=OP.max),
                              "vec")

            def emit_tr(t, j, which, hB):
                wbw = wbuf[j % 2][:, which * D:(which + 1) * D]
                last = None
                for cc in range(8):
                    hc = 8 * hB + cc
                    last = t.matmul(
                        ps_ex[cc // 4][:, (cc % 4) * 128:(cc % 4 + 1) * 128],
                        wbw[:, hc * 128:(hc + 1) * 128],
                        dslot(j, which), start=True, stop=True)
                p.inc(last, "ten")

            def emit_mm(t, j, which):
                last = None
                for half in range(2):
                    ps = ps_mm[2 * which + half]
                    for hc in range(DT):
                        last = t.matmul(
                            ps[:],
                            t1q(j, which)[:, hc * 128:(hc + 1) * 128],
                            xq_fm[:, hc, half * 512:(half + 1) * 512],
                            start=(hc == 0), stop=(hc == DT - 1))
                p.inc(last, "ten")

            @blk.tensor
            def _(t):
                for it_ in range(IT + 1):
                    if it_ < IT:
                        if it_ >= 1:
                            p.w(t, "sca", TK["uBa2", it_ - 1])
                        p.w(t, "vec", TK["Au", it_])
                        emit_tr(t, it_, 0, 0)
                        assert p.cnt["ten"] == TK["trgA", it_]
                    if it_ >= 1:
                        p.w(t, "sca", TK["gBa3", it_ - 1])
                        emit_mm(t, it_ - 1, 0)
                        assert p.cnt["ten"] == TK["mmg", it_ - 1]
                    if it_ < IT:
                        p.w(t, "sca", TK["gAa2", it_])
                        emit_tr(t, it_, 0, 1)
                        assert p.cnt["ten"] == TK["trgB", it_]
                    if it_ >= 1:
                        p.w(t, "sca", TK["uBa3", it_ - 1])
                        if it_ >= 2:
                            p.w(t, "vec", TK["E", it_ - 2])
                        emit_mm(t, it_ - 1, 1)
                        assert p.cnt["ten"] == TK["mmu", it_ - 1]
                    if it_ < IT:
                        p.w(t, "sca", TK["gBa2", it_])
                        emit_tr(t, it_, 1, 0)
                        assert p.cnt["ten"] == TK["truA", it_]
                        p.w(t, "sca", TK["uAa2", it_])
                        emit_tr(t, it_, 1, 1)
                        assert p.cnt["ten"] == TK["truB", it_]

            def emit_r12(s, hB, k2):
                base = 1024 * hB
                s.activation(out=uwb[:, base:base + 512],
                             in_=ps_ex[0][:], func=AF.Identity,
                             bias=mgb[:, 0:1])
                p.inc(s.activation(out=uwb[:, base + 512:base + 1024],
                                   in_=ps_ex[1][:], func=AF.Identity,
                                   bias=mgb[:, 0:1]), "sca")
                assert p.cnt["sca"] == k2

            def emit_r3(s, j, which, hB, k3):
                base = 1024 * hB
                p.inc(s.activation(out=t1q(j, which)[:, base:base + 1024],
                                   in_=uwb[:, base:base + 1024],
                                   func=AF.Identity, bias=mgb[:, 1:2]), "sca")
                assert p.cnt["sca"] == k3

            @blk.scalar
            def _(s):
                for it_ in range(IT + 1):
                    if it_ < IT:
                        p.w(s, "ten", TK["trgA", it_])
                        emit_r12(s, 0, TK["gAa2", it_])
                    if it_ >= 1:
                        j = it_ - 1
                        p.w(s, "ten", TK["mmg", j])
                        if j >= 1:
                            p.w(s, "vec", TK["E", j - 1])
                        last = None
                        for half in range(2):
                            last = s.activation(
                                out=gsil[:, half * 512:(half + 1) * 512],
                                in_=ps_mm[half][:], func=AF.Silu,
                                scale=WSGc(j))
                        p.inc(last, "sca")
                        assert p.cnt["sca"] == TK["silu", j]
                    if it_ < IT:
                        p.w(s, "ten", TK["trgB", it_])
                        emit_r12(s, 1, TK["gBa2", it_])
                        emit_r3(s, it_, 0, 0, TK["gAa3", it_])
                        emit_r3(s, it_, 0, 1, TK["gBa3", it_])
                        p.w(s, "ten", TK["truA", it_])
                        emit_r12(s, 0, TK["uAa2", it_])
                        p.w(s, "ten", TK["truB", it_])
                        emit_r12(s, 1, TK["uBa2", it_])
                        emit_r3(s, it_, 1, 0, TK["uAa3", it_])
                        emit_r3(s, it_, 1, 1, TK["uBa3", it_])

            assert p.cnt["vec"] == vec_P5e and p.cnt["ten"] == ten_P5e \
                and p.cnt["sca"] == sca_P5e
            vec_P5 = p.cnt["vec"]
            ten_P5 = p.cnt["ten"]

            @blk.tensor
            def _(t):
                p.w(t, "vec", vec_P5)
                last = None
                for bk in range(8):
                    last = t.transpose(ps_tr[0][:, bk * 128:(bk + 1) * 128],
                                       hacc[:, bk * 128:(bk + 1) * 128], ident[:])
                p.inc(last, "ten")
            ten_sh1 = p.cnt["ten"]

            @blk.vector
            def _(v):
                p.w(v, "ten", ten_sh1)
                last = None
                for bk in range(8):
                    last = v.tensor_reduce(out=SH[:, bk:bk + 1],
                                           in_=ps_tr[0][:, bk * 128:(bk + 1) * 128],
                                           axis=AX, op=OP.max)
                p.chain(last, v, "vec")
                p.chain(v.tensor_scalar(out=SHS[:, 0:8], in0=SH[:, 0:8],
                                        scalar1=RQMAX, scalar2=QEPS,
                                        op0=OP.mult, op1=OP.max), v, "vec")
                p.inc(v.reciprocal(RH[:, 0:8], SHS[:, 0:8]), "vec")
            vec_sh1 = p.cnt["vec"]

            @blk.vector
            def _(v):
                p.w(v, "vec", vec_sh1)
                p.chain(v.memset(rrowb[:, 0:128], 0.0), v, "vec")
                p.inc(v.tensor_copy(out=rrowb[:, 0:8], in_=RH[:, 0:8]), "vec")
            vec_rr1 = p.cnt["vec"]

            @blk.tensor
            def _(t):
                p.w(t, "vec", vec_rr1)
                p.inc(t.transpose(ps_tr[0][:, 0:128], rrowb[:, 0:128],
                                  ident[:]), "ten")
            ten_rr = p.cnt["ten"]

            @blk.vector
            def _(v):
                p.w(v, "ten", ten_rr)
                p.inc(v.tensor_copy(out=stg[0:8, 0:128],
                                    in_=ps_tr[0][0:8, 0:128]), "vec")
            vec_rr2 = p.cnt["vec"]

            @blk.sync
            def _(sync):
                p.w(sync, "vec", vec_rr2)
                p.inc(sync.dma_start(out=rrowb[0:1, 0:1024],
                                     in_=stg[0:8, 0:128]), "qp", 16)
            qp_rr = p.cnt["qp"]

            @blk.tensor
            def _(t):
                p.w(t, "qp", qp_rr)
                last = None
                for hf in range(2):
                    last = t.matmul(ps_mm[hf][:], ones1[0:1, 0:128],
                                    rrowb[0:1, hf * 512:(hf + 1) * 512],
                                    start=True, stop=True)
                p.inc(last, "ten")
            ten_rbc = p.cnt["ten"]

            vec_rbc0 = p.cnt["vec"] + 1
            qga_d0, qgb_d0 = p.cnt["qga"], p.cnt["qgb"]
            ten_d0, sca_d0 = p.cnt["ten"], p.cnt["sca"]
            GRP_SZ = [8, 8, 8, 8, 8, 4]
            QN = [0, 1, 2, 2, 3, 3]
            Q4 = I_FFN // 4

            TK6 = {}
            c = vec_rbc0
            c += 4; TK6["A", 0] = c
            for j in range(IT):
                c += 1; TK6["thq", j] = c
            for ht in range(DT):
                if ht + 1 < DT:
                    c += 4; TK6["A", ht + 1] = c
                for gg in range(6):
                    c += 1; TK6["evac", ht, gg] = c
            vec_P6e = c
            c = sca_d0
            for ht in range(DT):
                for q in range(4):
                    c += 1; TK6["B", ht, q] = c
                if ht >= 1:
                    c += 1; TK6["dfm", ht - 1] = c
            c += 1; TK6["dfm", DT - 1] = c
            sca_P6e = c
            c = ten_d0
            for ht in range(DT):
                for gg in range(6):
                    c += 1; TK6["tr", ht, gg] = c
                c += 1; TK6["mm", ht] = c
            ten_P6e = c
            vec_thq = TK6["thq", IT - 1]

            @blk.gpsimd
            def _(g):
                for ht in range(DT):
                    if ht >= 2:
                        p.w(g, "sca", TK6["B", ht - 2, 3])
                        p.w(g, "vec", TK6["A", ht - 2])
                    else:
                        p.w(g, "ten", TK["mmu", IT - 1])
                        p.w(g, "vec", TK["Au", IT - 1])
                    p.inc(g.dma_start(out=w2b[ht % 2][:], in_=w2[ht]),
                          "qga" if ht % 2 == 0 else "qgb", 16)

            def emit_A(v, ht):
                p.w(v, "qga" if ht % 2 == 0 else "qgb",
                    (qga_d0 if ht % 2 == 0 else qgb_d0) + 16 * (ht // 2 + 1))
                if ht >= 8:
                    p.w(v, "sca", TK6["dfm", ht - 8])
                wb2 = w2b[ht % 2]
                v.tensor_reduce(out=TMPA[:, 0:1], in_=wb2[:, 0:I_FFN // 2],
                                axis=AX, op=OP.max, apply_absolute_value=True)
                p.chain(v.tensor_reduce(out=TMPA[:, 1:2],
                                        in_=wb2[:, I_FFN // 2:],
                                        axis=AX, op=OP.max,
                                        apply_absolute_value=True), v, "vec")
                p.chain(v.tensor_max(TMPA[:, 2:3], TMPA[:, 0:1],
                                     TMPA[:, 1:2]), v, "vec")
                p.chain(v.tensor_scalar(out=WS2(ht), in0=TMPA[:, 2:3],
                                        scalar1=RQMAX, scalar2=QEPS,
                                        op0=OP.mult, op1=OP.max), v, "vec")
                p.inc(v.reciprocal(WR2(ht), WS2(ht)), "vec")
                assert p.cnt["vec"] == TK6["A", ht]

            @blk.vector
            def _(v):
                p.w(v, "ten", ten_rbc)
                v.tensor_copy(out=rbc[:, 0:512], in_=ps_mm[0][:])
                p.chain(v.tensor_copy(out=rbc[:, 512:1024], in_=ps_mm[1][:]),
                        v, "vec")
                assert p.cnt["vec"] == vec_rbc0
                emit_A(v, 0)
                tsc = [ost[:, 0:1024], ost[:, 1024:2048]]
                for j in range(IT + 1):
                    if j < IT:
                        v.tensor_tensor(out=tsc[j % 2], in0=h_buf[:, j, :],
                                        in1=rbc[:], op=OP.mult)
                    if j >= 1:
                        p.inc(v.tensor_scalar(out=h_buf[:, j - 1, :],
                                              in0=tsc[(j - 1) % 2],
                                              scalar1=MAGIC, scalar2=MAGIC,
                                              op0=OP.add, op1=OP.subtract),
                              "vec")
                        assert p.cnt["vec"] == TK6["thq", j - 1]
                for ht in range(DT):
                    if ht + 1 < DT:
                        emit_A(v, ht + 1)
                    for gg, sz in enumerate(GRP_SZ):
                        p.w(v, "ten", TK6["tr", ht, gg])
                        off = sum(GRP_SZ[:gg])
                        p.inc(v.tensor_copy(
                            out=t2q[:, off * 128:(off + sz) * 128],
                            in_=ps_tr[gg % 2][:, 0:sz * 128]), "vec")
                        assert p.cnt["vec"] == TK6["evac", ht, gg]
                assert p.cnt["vec"] == vec_P6e

            @blk.scalar
            def _(s):
                for ht in range(DT):
                    p.w(s, "vec", TK6["A", ht])
                    p.w(s, "qga" if ht % 2 == 0 else "qgb",
                        (qga_d0 if ht % 2 == 0 else qgb_d0) + 16 * (ht // 2 + 1))
                    if ht >= 1:
                        p.w(s, "ten", TK6["tr", ht - 1, 5])
                    wb2 = w2b[ht % 2]
                    def a1(qq):
                        s.activation(out=wuq[qq % 2][:],
                                     in_=wb2[:, qq * Q4:(qq + 1) * Q4],
                                     func=AF.Identity, scale=WR2(ht),
                                     bias=mgb[:, 0:1])

                    def a2(qq):
                        p.inc(s.activation(out=s2n[:, qq * Q4:(qq + 1) * Q4],
                                           in_=wuq[qq % 2][:],
                                           func=AF.Identity,
                                           bias=mgb[:, 1:2]), "sca")

                    a1(0); a1(1); a2(0); a1(2); a2(1); a1(3); a2(2); a2(3)
                    assert p.cnt["sca"] == TK6["B", ht, 3]
                    if ht >= 1:
                        p.w(s, "ten", TK6["mm", ht - 1])
                        last = None
                        for half in range(2):
                            last = s.activation(
                                out=down_fm[:, ht - 1,
                                            half * 512:(half + 1) * 512],
                                in_=ps_mm[half][:], func=AF.Identity,
                                scale=WS2(ht - 1))
                        p.inc(last, "sca")
                        assert p.cnt["sca"] == TK6["dfm", ht - 1]
                p.w(s, "ten", TK6["mm", DT - 1])
                last = None
                for half in range(2):
                    last = s.activation(
                        out=down_fm[:, DT - 1, half * 512:(half + 1) * 512],
                        in_=ps_mm[half][:], func=AF.Identity,
                        scale=WS2(DT - 1))
                p.inc(last, "sca")
                assert p.cnt["sca"] == sca_P6e

            @blk.tensor
            def _(t):
                for ht in range(DT):
                    off = 0
                    for gg, sz in enumerate(GRP_SZ):
                        p.w(t, "sca", TK6["B", ht, QN[gg]])
                        if gg >= 2:
                            p.w(t, "vec", TK6["evac", ht, gg - 2])
                        elif ht >= 1:
                            p.w(t, "vec", TK6["evac", ht - 1, gg + 4])
                        last = None
                        for j in range(sz):
                            ic = off + j
                            last = t.transpose(
                                ps_tr[gg % 2][:, j * 128:(j + 1) * 128],
                                s2n[:, ic * 128:(ic + 1) * 128], ident[:])
                        p.inc(last, "ten")
                        assert p.cnt["ten"] == TK6["tr", ht, gg]
                        off += sz
                    p.w(t, "vec", TK6["evac", ht, 5])
                    if ht >= 1:
                        p.w(t, "sca", TK6["dfm", ht - 1])
                    last = None
                    for half in range(2):
                        ps = ps_mm[half]
                        for ic in range(IT):
                            if ht == 0 and half == 0:
                                p.w(t, "vec", TK6["thq", ic])
                            last = t.matmul(
                                ps[:], t2q[:, ic * 128:(ic + 1) * 128],
                                h_buf[:, ic, half * 512:(half + 1) * 512],
                                start=(ic == 0), stop=(ic == IT - 1))
                    p.inc(last, "ten")
                    assert p.cnt["ten"] == TK6["mm", ht]
            ten_P6 = p.cnt["ten"]

            if "thq" in taps:
                ap = tap("thq", [128, IT * NO], BF16)

                @blk.sync
                def _(sync):
                    p.w(sync, "vec", vec_thq)
                    p.inc(sync.dma_start(
                        out=ap, in_=h_buf[:].rearrange("p a b -> p (a b)")),
                        "qo", 16)

            sca_P6 = p.cnt["sca"]

            ten_fin0, vec_fin0 = p.cnt["ten"], p.cnt["vec"]
            qsa_f0, qsb_f0 = p.cnt["qsa"], p.cnt["qsb"]
            qo_f0 = p.cnt["qo"]
            obuf = [ost, osu]

            @blk.sync
            def _(sync):
                p.w(sync, "sca", sca_P6)
                p.w(sync, "qp", qp_x1)
                for nt in range(NOT_):
                    if nt >= 2:
                        p.w(sync, "vec", vec_fin0 + 2 * (nt - 2) + 2)
                    p.inc(sync.dma_start(out=xbuf[nt % 2][:], in_=x1_scr[nt]),
                          _ldsem(nt), 16)

            @blk.scalar
            def _(s):
                for nt in range(NOT_):
                    p.w(s, "vec", vec_fin0 + 2 * nt + 2)
                    p.inc(s.dma_start(out=out[nt * 128:(nt + 1) * 128, :],
                                      in_=obuf[nt % 2][:]), "qo", 16)

            @blk.tensor
            def _(t):
                p.w(t, "sca", sca_P6)
                for nt in range(NOT_):
                    if nt >= 1:
                        p.w(t, "vec", vec_fin0 + 2 * (nt - 1) + 1)
                    last = None
                    for ht in range(DT):
                        last = t.transpose(
                            ps_tr[ht // 8][:, (ht % 8) * 128:(ht % 8 + 1) * 128],
                            down_fm[:, ht, nt * 128:(nt + 1) * 128], ident[:])
                    p.inc(last, "ten")

            @blk.vector
            def _(v):
                for nt in range(NOT_):
                    p.w(v, "ten", ten_fin0 + nt + 1)
                    p.w(v, _ldsem(nt), _ldval(nt, qsa_f0, qsb_f0))
                    if nt >= 2:
                        p.w(v, "qo", qo_f0 + 16 * (nt - 1))
                    last = None
                    for half in range(2):
                        last = v.tensor_scalar(
                            out=obuf[nt % 2][:, half * 1024:(half + 1) * 1024],
                            in0=ps_tr[half][:],
                            scalar1=SHS[:, nt:nt + 1], scalar2=None,
                            op0=OP.mult)
                    p.chain(last, v, "vec")
                    p.inc(v.tensor_add(obuf[nt % 2][:], obuf[nt % 2][:],
                                       xbuf[nt % 2][:]), "vec")



        @blk.sync
        def _(sync):
            for nm in ("qs", "qsa", "qsb", "qga", "qgb", "qp", "qo"):
                p.w(sync, nm, p.cnt[nm])

    return nc, tap_aps


def _tile_w(W):
    O, Din = W.shape
    A = W.reshape(O // 256, 2, 128, Din // 128, 128)
    A = A.transpose(0, 4, 1, 3, 2)
    return np.ascontiguousarray(A.reshape(O // 256, 128, 2 * Din))


def _perm_for_core(c):
    if c % 2 == 0:
        return (0, 3), (1, 2)
    return (1, 2), (0, 3)


def _moff_for_core(c):
    own, partner = _perm_for_core(c)
    blocks = [own[0], own[1], partner[0], partner[1]]
    gstart = [512 * b for b in blocks]
    offs = []
    for jt in JSET_A:
        offs.append(gstart[0] - (gstart[jt // 4] + 128 * (jt % 4)))
    for jt in JSET_B:
        offs.append(gstart[1] - (gstart[jt // 4] + 128 * (jt % 4)))
    m = np.zeros((128, 24), np.float32)
    m[:] = np.array(offs, np.float32)[None, :]
    return m


def make_inputs_for_core(c, X, W):
    own, partner = _perm_for_core(c)
    xb = X[c // 2]
    rows = np.concatenate([xb[512 * blk:512 * (blk + 1)]
                           for blk in (own + partner)], axis=0)
    d = {"x_in": np.ascontiguousarray(rows), "moff": _moff_for_core(c)}
    d.update(W)
    return d


def prep_weights(L0_attn_Wq, L0_attn_Wk, L0_attn_Wv, L0_attn_Wg, L0_attn_Wp,
                 L0_ffn_w_gate_up, L0_ffn_w_down):
    return {
        "wq": _tile_w(np.asarray(L0_attn_Wq, np.float32)),
        "wk": _tile_w(np.asarray(L0_attn_Wk, np.float32)),
        "wv": _tile_w(np.asarray(L0_attn_Wv, np.float32)),
        "wg": _tile_w(np.asarray(L0_attn_Wg, np.float32)),
        "wp": np.ascontiguousarray(
            np.asarray(L0_attn_Wp, np.float32).T.reshape(DT, 128, D)),
        "w1": np.ascontiguousarray(
            np.asarray(L0_ffn_w_gate_up, np.float32).reshape(2, IT, 128, D)
            .transpose(1, 2, 0, 3).reshape(IT, 128, 2 * D)),
        "w2": np.ascontiguousarray(
            np.asarray(L0_ffn_w_down, np.float32).reshape(DT, 128, I_FFN)),
    }


def kernel(L0_attn_x, L0_attn_Wq, L0_attn_Wk, L0_attn_Wv, L0_attn_Wg,
           L0_attn_Wp, L0_ffn_w_gate_up, L0_ffn_w_down):
    X = np.ascontiguousarray(np.asarray(L0_attn_x, np.float32))
    W = prep_weights(L0_attn_Wq, L0_attn_Wk, L0_attn_Wv, L0_attn_Wg,
                     L0_attn_Wp, L0_ffn_w_gate_up, L0_ffn_w_down)
    nc, _ = build()
    in_maps = [make_inputs_for_core(c, X, W) for c in range(8)]
    res = run_bass_kernel_spmd(nc, in_maps, core_ids=list(range(8)))
    outp = np.empty((4, 2048, 2048), np.float32)
    for c in range(8):
        own, _ = _perm_for_core(c)
        o = np.asarray(res.results[c]["out"], np.float32)
        b = c // 2
        outp[b, 512 * own[0]:512 * (own[0] + 1)] = o[0:512]
        outp[b, 512 * own[1]:512 * (own[1] + 1)] = o[512:1024]
    return outp


# version 13
# speedup vs baseline: 1.1356x; 1.0224x over previous
import numpy as np
import ml_dtypes
import concourse.bass as bass
import concourse.mybir as mybir
from concourse.bass_utils import run_bass_kernel_spmd

F32 = mybir.dt.float32
BF16 = mybir.dt.bfloat16
OP = mybir.AluOpType
AF = mybir.ActivationFunctionType
AX = mybir.AxisListType.X

MAGIC = 12582912.0
LN1_EPS = 1e-5
RMS_EPS = float(np.finfo(np.float32).eps)
RQMAX = 1.0 / 127.0
QEPS = 1e-8

D = 2048
NB = 2048
NO = 1024
DT = 16
NT = 16
NOT_ = 8
I_FFN = 5632
IT = 44
KB = 1024

JSET_A = [0, 1, 2, 3, 8, 9, 10, 11]
JSET_B = list(range(16))


class P:
    def __init__(self, nc, st, names):
        self.sem = {}
        self.cnt = {}
        for n in names:
            self.sem[n] = st.enter_context(nc.semaphore(n))
            self.cnt[n] = 0

    def inc(self, inst, name, k=1):
        inst.then_inc(self.sem[name], k)
        self.cnt[name] += k
        return self.cnt[name]

    def tick(self, eng, name):
        eng.sem_inc(self.sem[name], 1)
        self.cnt[name] += 1
        return self.cnt[name]

    def w(self, eng, name, v):
        if v > 0:
            eng.wait_ge(self.sem[name], v)

    def chain(self, inst, eng, name):
        self.inc(inst, name)
        eng.wait_ge(self.sem[name], self.cnt[name])

    def sw(self, eng, name):
        if self.cnt[name] > 0:
            eng.wait_ge(self.sem[name], self.cnt[name])


def build(taps=(), phase_limit=99):
    taps = set(taps)
    nc = bass.Bass("TRN2", target_bir_lowering=False,
                   detect_race_conditions=False)

    x_in = nc.dram_tensor("x_in", [NB, D], F32, kind="ExternalInput").ap()
    wq = nc.dram_tensor("wq", [8, 128, 2 * D], F32, kind="ExternalInput").ap()
    wk = nc.dram_tensor("wk", [8, 128, 2 * D], F32, kind="ExternalInput").ap()
    wv = nc.dram_tensor("wv", [8, 128, 2 * D], F32, kind="ExternalInput").ap()
    wg = nc.dram_tensor("wg", [8, 128, 2 * D], F32, kind="ExternalInput").ap()
    wp = nc.dram_tensor("wp", [DT, 128, D], F32, kind="ExternalInput").ap()
    w1 = nc.dram_tensor("w1", [IT, 128, 2 * D], F32, kind="ExternalInput").ap()
    w2 = nc.dram_tensor("w2", [DT, 128, I_FFN], F32, kind="ExternalInput").ap()
    moff = nc.dram_tensor("moff", [128, 24], F32, kind="ExternalInput").ap()

    out = nc.dram_tensor("out", [NO, D], F32, kind="ExternalOutput").ap()

    xn_scr = nc.dram_tensor("xn_scr", [4, 128, DT * 512], BF16).ap()
    v_scr = nc.dram_tensor("v_scr", [NT, 4, 128, 512], BF16).ap()
    g_scr = nc.dram_tensor("g_scr", [NOT_, 4, 128, 512], BF16).ap()
    og_scr = nc.dram_tensor("og_scr", [NOT_, 128, D], BF16).ap()
    x1_scr = nc.dram_tensor("x1_scr", [NOT_, 128, D], F32).ap()

    tap_aps = {}

    def tap(name, shape, dtype=F32):
        if name in taps:
            tap_aps[name] = nc.dram_tensor("tap_" + name, shape, dtype,
                                           kind="ExternalOutput").ap()
            return tap_aps[name]
        return None

    from contextlib import ExitStack
    with ExitStack() as st:
        _allocs = []

        BASE = 24 * KB

        def at(name, shape, dt, off_b):
            per = int(np.prod(shape[1:])) * (2 if dt == BF16 else 4)
            _allocs.append((name, off_b, off_b + per))
            assert BASE + off_b + per <= 224 * KB, f"{name} over budget"
            return nc.alloc_sbuf_tensor_at(name, shape, dt, offset=BASE + off_b)

        k_fm = at("k_fm", [128, DT, NB], BF16, 0)
        h_buf = at("h_buf", [128, IT, NO], BF16, 0)

        xn_fm = at("xn_fm", [128, 2, DT, NO], BF16, 64 * KB)
        q_fm = xn_fm[:, 1]
        xbuf = [at(f"xbuf{i}", [128, D], F32, (64 + 8 * i) * KB) for i in range(2)]
        xbuf1 = [at("xbuf1a", [128, D], F32, 161 * KB + 512),
                 at("xbuf1b", [128, D], F32, 169 * KB + 512)]
        xb2 = [at(f"xb2{i}", [128, D], F32, (64 + 8 * i) * KB)
               for i in range(2)]
        stgP1 = [at(f"stgP1{i}", [128, D], BF16, (148 + 4 * i) * KB)
                 for i in range(2)]
        vevq = [at(f"vevq{i}", [128, 512], BF16, (146 + i) * KB)
                for i in range(2)]
        x1t = at("x1t", [128, D], F32, 80 * KB)
        sqdump = at("sqdump", [128, D], F32, 80 * KB)
        stg = at("stg", [128, D], BF16, 88 * KB)
        stg2 = at("stg2", [128, D], BF16, 92 * KB)
        s2n = at("s2n", [128, I_FFN], BF16, 88 * KB)

        x4buf = [at(f"x4buf{i}", [128, D], F32, (32 + 8 * i) * KB)
                 for i in range(2)]
        x1t4 = [at(f"x1t4{i}", [128, D], F32, (48 + 8 * i) * KB)
                for i in range(2)]
        og_fm4 = at("og_fm4", [128, DT, NO], BF16, 64 * KB)
        stg4 = [at(f"stg4{i}", [128, D], BF16, (161 + 4 * i) * KB + 512)
                for i in range(2)]
        osu = at("osu", [128, D], F32, 169 * KB + 512)
        poob = at("poob", [128, 1024], F32, 161 * KB + 512)
        wpb = [at(f"wpb{dc}", [128, D], BF16,
                  (4 * dc if dc < 8 else 128 + 4 * (dc - 8)) * KB)
               for dc in range(DT)]
        stq = at("stq", [128, 16, 512], BF16, 64 * KB)
        vq = at("vq", [128, 16, 512], BF16, 80 * KB)
        og_fm = at("og_fm", [128, DT, NO], BF16, 96 * KB)
        xq_fm = at("xq_fm", [128, DT, NO], BF16, 96 * KB)
        down_fm = at("down_fm", [128, DT, NO], BF16, 100 * KB)

        wbuf = [at(f"wbuf{i}", [128, 2 * D], BF16, (128 + 8 * i) * KB)
                for i in range(2)]
        gtile = at("gtile", [128, 4, D], BF16, 128 * KB)
        vq2 = at("vq2", [128, 16, 512], BF16, 144 * KB)
        vstg = [at(f"vstg{i}", [128, 512], BF16, (144 + i) * KB) for i in range(2)]
        t1qb = [at(f"t1q{i}", [128, D], BF16, (144 + 4 * i) * KB) for i in range(4)]
        uwb = at("uwb", [128, D], F32, 88 * KB)
        w2b = [at(f"w2b{i}", [128, I_FFN], BF16, (132 + 11 * i) * KB)
               for i in range(2)]
        t2q = at("t2q", [128, IT * 128], BF16, 161 * KB + 512)

        ident = at("ident", [128, 128], BF16, 160 * KB)
        ones1 = at("ones1", [1, 512], BF16, 160 * KB + 256)
        ogst = [at(f"ogst{i}", [128, 512], BF16, 165 * KB + 256 + i * KB)
                for i in range(2)]
        masks3 = at("masks3", [128, 16, 512], BF16, 167 * KB + 256 + 256)
        gsil = at("gsil", [128, NO], F32, 167 * KB + 512)
        hacc = at("hacc", [128, NO], BF16, 172 * KB)
        rbc = at("rbc", [128, NO], F32, 128 * KB)
        rrowb = at("rrowb", [128, 1024], BF16, 176 * KB)
        wuq = [at(f"wuq{i}", [128, I_FFN // 4], F32,
                   (172 if i == 0 else 178) * KB + 512) for i in range(2)]

        iota = at("iota", [128, 512], F32, 184 * KB)
        sv = at("sv", [128, 1024], F32, 186 * KB)
        ost = at("ost", [128, D], F32, 190 * KB)
        mofb = at("mofb", [128, 24], F32, 198 * KB)
        wsc = at("wsc", [128, 64], F32, 198 * KB + 128)
        wscb = at("wscb", [128, 16], BF16, 178 * KB)
        wsc2 = at("wsc2", [128, 16], F32, 178 * KB + 128)
        mgb = at("mgb", [128, 2], F32, 178 * KB + 320)

        ps_mm = [st.enter_context(nc.psum_tensor(f"ps_mm{i}", [128, 512], F32))
                 for i in range(4)]
        ps_tr = [st.enter_context(nc.psum_tensor(f"ps_tr{i}", [128, 1024], BF16))
                 for i in range(2)]
        ps_ex = [st.enter_context(nc.psum_tensor(f"ps_ex{i}", [128, 512], F32))
                 for i in range(2)]

        p = P(nc, st, ["qs", "qsa", "qsb", "qga", "qgb", "qp", "qo",
                       "ten", "vec", "sca", "gps"])
        block = st.enter_context(nc.Block())

        class _B:

            def vector(self, f):
                block.vector(lambda v: (p.sw(v, "vec"), f(v))[1])

            def scalar(self, f):
                block.scalar(lambda s_: (p.sw(s_, "sca"), f(s_))[1])

            def tensor(self, f):
                block.tensor(lambda t: (p.sw(t, "ten"), f(t))[1])

            def gpsimd(self, f):
                block.gpsimd(lambda g: (p.sw(g, "gps"), f(g))[1])

            def sync(self, f):
                block.sync(f)

        blk = _B()

        SUM = sv[:, 0:16]
        SQ = sv[:, 16:32]
        M0 = sv[:, 32:48]
        E0 = sv[:, 48:64]
        V0 = sv[:, 64:80]
        T0 = sv[:, 80:96]
        R0 = sv[:, 96:112]
        S0 = sv[:, 112:128]
        N1 = sv[:, 128:144]
        N2 = sv[:, 144:160]
        SCL = sv[:, 160:176]
        BIA = sv[:, 176:192]
        AM = sv[:, 192:200]
        SX = sv[:, 200:208]
        RX = sv[:, 208:216]
        SH = sv[:, 216:224]
        SHS = sv[:, 224:232]
        RH = sv[:, 232:240]
        TMPA = sv[:, 240:248]

        def WSG(i):
            return wsc[:, i % 8:i % 8 + 1]

        def WRG(i):
            return wsc[:, 8 + i % 8:8 + i % 8 + 1]

        def WSU(i):
            return wsc[:, 16 + i % 8:16 + i % 8 + 1]

        def WRU(i):
            return wsc[:, 24 + i % 8:24 + i % 8 + 1]

        def WS2(i):
            return wsc[:, 32 + i % 8:32 + i % 8 + 1]

        def WR2(i):
            return wsc[:, 40 + i % 8:40 + i % 8 + 1]

        @blk.sync
        def _(sync):
            p.inc(sync.dma_start(out=mofb[:], in_=moff), "qs", 16)
        moff_v = p.cnt["qs"]

        @blk.vector
        def _(v):
            v.memset(ident[:], 0.0)
            p.inc(v.memset(ones1[:], 1.0), "vec")
        vec_ms = p.cnt["vec"]

        @blk.gpsimd
        def _(g):
            p.w(g, "vec", vec_ms)
            p.inc(g.affine_select(out=ident[:], in_=ident[:],
                                  compare_op=OP.not_equal, fill=1.0, base=0,
                                  pattern=[[-1, 128]], channel_multiplier=1),
                  "gps")
            p.inc(g.iota(out=iota[:], pattern=[[1, 512]], base=0,
                         channel_multiplier=-1,
                         allow_small_or_imprecise_dtypes=True), "gps")
        gps_p0 = p.cnt["gps"]

        if "iota" in taps:
            ap = tap("iota", [128, 512], F32)

            @blk.sync
            def _(sync):
                p.w(sync, "gps", gps_p0)
                p.inc(sync.dma_start(out=ap, in_=iota[:]), "qo", 16)

        qsa0, qsb0 = p.cnt["qsa"], p.cnt["qsb"]

        def _ldsem(r):
            return "qsa" if r % 2 == 0 else "qsb"

        def _ldval(r, base_a=qsa0, base_b=qsb0):
            return (base_a if r % 2 == 0 else base_b) + 16 * (r // 2 + 1)

        ORD1 = list(range(8, 16)) + list(range(8))
        vec_10, sca_10, ten_10 = p.cnt["vec"], p.cnt["sca"], p.cnt["ten"]
        qs_10 = p.cnt["qs"]
        TK1 = {}
        c = vec_10
        for i in range(8):
            c += 1; TK1["SUMr", i] = c
        c += 4; TK1["R0p"] = c
        c += 6; TK1["BIAp"] = c
        for i in range(8, 16):
            c += 1; TK1["SUMr", i] = c
            c += 1; TK1["evac", i - 8] = c
        c += 4; TK1["R0o"] = c
        c += 6; TK1["BIAo"] = c
        for j in range(8, 16):
            c += 1; TK1["evac", j] = c
        vec_P1e = c
        c = sca_10
        for i in range(8):
            c += 1; TK1["SQ", i] = c
        c += 1; TK1["sqp"] = c
        for i in range(8, 16):
            c += 1; TK1["SQ", i] = c
            c += 1; TK1["apply", i - 8] = c
        c += 1; TK1["sqo"] = c
        for j in range(8, 16):
            c += 1; TK1["apply", j] = c
        c = ten_10
        for j in range(NT):
            c += 1; TK1["tr", j] = c

        @blk.sync
        def _(sync):
            for i, r in enumerate(ORD1):
                if i >= 2:
                    p.w(sync, "vec", TK1["SUMr", i - 2])
                    p.w(sync, "sca", TK1["SQ", i - 2])
                p.inc(sync.dma_start(out=xbuf1[i % 2][:],
                                     in_=x_in[r * 128:(r + 1) * 128, :]),
                      _ldsem(i), 16)

        def issue_b(s, i):
            if i >= NT:
                return
            buf = xb2[i % 2] if i < 8 else xbuf1[i % 2]
            if i == 8:
                p.w(s, "vec", TK1["SUMr", 14])
                p.w(s, "sca", TK1["SQ", 14])
            if i == 9:
                p.w(s, "vec", TK1["SUMr", 15])
                p.w(s, "sca", TK1["SQ", 15])
            if i >= 2:
                p.w(s, "sca", TK1["apply", i - 2])
            r = ORD1[i]
            p.inc(s.dma_start(out=buf[:],
                              in_=x_in[r * 128:(r + 1) * 128, :]), "qs", 16)

        def scale_batch(v, lo, hi, kR):
            sl = slice(8 * lo // 8, hi)

            def cc(t_):
                return t_[:, lo:hi]

            v.tensor_scalar(out=cc(M0), in0=cc(SUM), scalar1=1.0 / D,
                            scalar2=None, op0=OP.mult)
            v.tensor_scalar(out=cc(E0), in0=cc(SQ), scalar1=1.0 / D,
                            scalar2=None, op0=OP.mult)
            p.chain(v.tensor_mul(cc(V0), cc(M0), cc(M0)), v, "vec")
            p.chain(v.tensor_sub(cc(V0), cc(E0), cc(V0)), v, "vec")
            p.chain(v.tensor_scalar(out=cc(T0), in0=cc(V0),
                                    scalar1=1.0 + RMS_EPS,
                                    scalar2=RMS_EPS * LN1_EPS,
                                    op0=OP.mult, op1=OP.add), v, "vec")
            p.inc(v.reciprocal(cc(R0), cc(T0)), "vec")
            assert p.cnt["vec"] == TK1[kR]

        def newton_batch(v, lo, hi, kB):
            def cc(t_):
                return t_[:, lo:hi]

            p.chain(v.tensor_mul(cc(N1), cc(S0), cc(S0)), v, "vec")
            p.chain(v.tensor_mul(cc(N1), cc(T0), cc(N1)), v, "vec")
            p.chain(v.tensor_scalar(out=cc(N1), in0=cc(N1), scalar1=-0.5,
                                    scalar2=1.5, op0=OP.mult, op1=OP.add),
                    v, "vec")
            p.chain(v.tensor_mul(cc(SCL), cc(S0), cc(N1)), v, "vec")
            p.chain(v.tensor_mul(cc(N2), cc(M0), cc(SCL)), v, "vec")
            p.inc(v.tensor_scalar(out=cc(BIA), in0=cc(N2), scalar1=-1.0,
                                  scalar2=None, op0=OP.mult), "vec")
            assert p.cnt["vec"] == TK1[kB]

        @blk.vector
        def _(v):
            for i in range(8):
                p.w(v, _ldsem(i), _ldval(i))
                inst = v.tensor_reduce(out=SUM[:, ORD1[i]:ORD1[i] + 1],
                                       in_=xbuf1[i % 2][:], axis=AX,
                                       op=OP.add)
                if i == 7:
                    p.chain(inst, v, "vec")
                else:
                    p.inc(inst, "vec")
                assert p.cnt["vec"] == TK1["SUMr", i]
            p.w(v, "sca", TK1["SQ", 7])
            scale_batch(v, 8, 16, "R0p")
            p.w(v, "sca", TK1["sqp"])
            newton_batch(v, 8, 16, "BIAp")
            for i in range(8, 16):
                p.w(v, _ldsem(i), _ldval(i))
                inst = v.tensor_reduce(out=SUM[:, ORD1[i]:ORD1[i] + 1],
                                       in_=xbuf1[i % 2][:], axis=AX,
                                       op=OP.add)
                if i == 15:
                    p.chain(inst, v, "vec")
                else:
                    p.inc(inst, "vec")
                assert p.cnt["vec"] == TK1["SUMr", i]
                k = i - 8
                p.w(v, "ten", TK1["tr", k])
                last = None
                rp = ORD1[k]
                for half in range(2):
                    last = v.tensor_copy(
                        out=xn_fm[:, rp // 8, half * 8:(half + 1) * 8,
                                  (rp % 8) * 128:(rp % 8 + 1) * 128],
                        in_=ps_tr[half][:].rearrange("p (c f) -> p c f", c=8))
                p.inc(last, "vec")
                assert p.cnt["vec"] == TK1["evac", k]
            p.w(v, "sca", TK1["SQ", 15])
            scale_batch(v, 0, 8, "R0o")
            p.w(v, "sca", TK1["sqo"])
            newton_batch(v, 0, 8, "BIAo")
            for j in range(8, 16):
                p.w(v, "ten", TK1["tr", j])
                last = None
                rp = ORD1[j]
                for half in range(2):
                    last = v.tensor_copy(
                        out=xn_fm[:, rp // 8, half * 8:(half + 1) * 8,
                                  (rp % 8) * 128:(rp % 8 + 1) * 128],
                        in_=ps_tr[half][:].rearrange("p (c f) -> p c f", c=8))
                p.inc(last, "vec")
                assert p.cnt["vec"] == TK1["evac", j]
            assert p.cnt["vec"] == vec_P1e

        @blk.scalar
        def _(s):
            issue_b(s, 0)
            issue_b(s, 1)
            for i in range(8):
                p.w(s, _ldsem(i), _ldval(i))
                p.sw(s, "sca")
                p.inc(s.activation(out=ost[:], in_=xbuf1[i % 2][:],
                                   func=AF.Square,
                                   accum_out=SQ[:, ORD1[i]:ORD1[i] + 1]),
                      "sca")
                assert p.cnt["sca"] == TK1["SQ", i]
            p.w(s, "vec", TK1["R0p"])
            p.inc(s.activation(out=S0[:, 8:16], in_=R0[:, 8:16],
                               func=AF.Sqrt), "sca")
            assert p.cnt["sca"] == TK1["sqp"]
            for i in range(8, 16):
                p.w(s, _ldsem(i), _ldval(i))
                p.sw(s, "sca")
                p.inc(s.activation(out=ost[:], in_=xbuf1[i % 2][:],
                                   func=AF.Square,
                                   accum_out=SQ[:, ORD1[i]:ORD1[i] + 1]),
                      "sca")
                assert p.cnt["sca"] == TK1["SQ", i]
                k = i - 8
                rp = ORD1[k]
                if k == 0:
                    p.w(s, "vec", TK1["BIAp"])
                p.w(s, "qs", qs_10 + 16 * (k + 1))
                if k >= 2:
                    p.w(s, "ten", TK1["tr", k - 2])
                p.inc(s.activation(out=stgP1[k % 2][:], in_=xb2[k % 2][:],
                                   func=AF.Identity,
                                   scale=SCL[:, rp:rp + 1],
                                   bias=BIA[:, rp:rp + 1]), "sca")
                assert p.cnt["sca"] == TK1["apply", k]
                issue_b(s, k + 2)
            p.w(s, "vec", TK1["R0o"])
            p.inc(s.activation(out=S0[:, 0:8], in_=R0[:, 0:8],
                               func=AF.Sqrt), "sca")
            assert p.cnt["sca"] == TK1["sqo"]
            for j in range(8, 16):
                rp = ORD1[j]
                if j == 8:
                    p.w(s, "vec", TK1["BIAo"])
                p.w(s, "qs", qs_10 + 16 * (j + 1))
                p.w(s, "ten", TK1["tr", j - 2])
                p.inc(s.activation(out=stgP1[j % 2][:], in_=xbuf1[j % 2][:],
                                   func=AF.Identity,
                                   scale=SCL[:, rp:rp + 1],
                                   bias=BIA[:, rp:rp + 1]), "sca")
                assert p.cnt["sca"] == TK1["apply", j]
                issue_b(s, j + 2)

        @blk.tensor
        def _(t):
            p.w(t, "gps", gps_p0)
            for j in range(NT):
                p.w(t, "sca", TK1["apply", j])
                if j >= 1:
                    p.w(t, "vec", TK1["evac", j - 1])
                last = None
                for dc in range(DT):
                    last = t.transpose(
                        ps_tr[dc // 8][:, (dc % 8) * 128:(dc % 8 + 1) * 128],
                        stgP1[j % 2][:, dc * 128:(dc + 1) * 128], ident[:])
                p.inc(last, "ten")
                assert p.cnt["ten"] == TK1["tr", j]

        if "xn_scr" in taps:
            ap = tap("xn_scr", [128, 2 * DT * NO], BF16)

            @blk.sync
            def _(sync):
                p.w(sync, "vec", vec_P1e)
                p.inc(sync.dma_start(
                    out=ap, in_=xn_fm[:].rearrange("p a b c -> p (a b c)")),
                    "qo", 16)

        def emit_proj_fm(wsrc, dst_fm, sp, act, dst_base=None, first=False):
            qga0, qgb0 = p.cnt["qga"], p.cnt["qgb"]
            sca_0, ten_0 = p.cnt["sca"], p.cnt["ten"]
            db = 1024 * sp if dst_base is None else dst_base

            def _wsem(grp):
                return "qga" if grp % 2 == 0 else "qgb"

            def _wval(grp):
                return (qga0 if grp % 2 == 0 else qgb0) + 16 * (grp // 2 + 1)

            @blk.gpsimd
            def _(g):
                for grp in range(8):
                    if grp >= 2:
                        p.w(g, "ten", ten_0 + 4 * (grp - 1))
                    elif not first:
                        p.w(g, "ten", ten_0 - (4 if grp == 0 else 0))
                    p.inc(g.dma_start(out=wbuf[grp % 2][:], in_=wsrc[grp]),
                          _wsem(grp), 16)

            @blk.tensor
            def _(t):
                for ot in range(DT):
                    grp, half = ot // 2, ot % 2
                    if half == 0:
                        p.w(t, _wsem(grp), _wval(grp))
                    for sl in range(2):
                        u = 2 * ot + sl
                        if u >= 2:
                            p.w(t, "sca", sca_0 + u - 1)
                        ps = ps_mm[u % 2]
                        last = None
                        for dc in range(DT):
                            last = t.matmul(ps[:],
                                            wbuf[grp % 2][:, half * D + dc * 128:
                                                          half * D + (dc + 1) * 128],
                                            xn_fm[:, sp, dc, sl * 512:
                                                  (sl + 1) * 512],
                                            start=(dc == 0), stop=(dc == DT - 1))
                        p.inc(last, "ten")

            @blk.scalar
            def _(s):
                for u in range(32):
                    p.w(s, "ten", ten_0 + u + 1)
                    ot, sl = u // 2, u % 2
                    p.inc(s.activation(out=dst_fm[:, ot, db + 512 * sl:
                                               db + 512 * (sl + 1)],
                                       in_=ps_mm[u % 2][:], func=act), "sca")

        def emit_proj_tm(wsrc, scr, sp, act, base_tile):
            qga0, qgb0 = p.cnt["qga"], p.cnt["qgb"]
            sca_0, ten_0, vec_0 = p.cnt["sca"], p.cnt["ten"], p.cnt["vec"]
            qp_0 = p.cnt["qp"]

            def _wsem(grp):
                return "qga" if grp % 2 == 0 else "qgb"

            def _wval(grp):
                return (qga0 if grp % 2 == 0 else qgb0) + 16 * (grp // 2 + 1)

            @blk.gpsimd
            def _(g):
                for grp in range(8):
                    if grp >= 2:
                        p.w(g, "ten", ten_0 + 8 * (grp - 1))
                    else:
                        p.w(g, "ten", ten_0 - (4 if grp == 0 else 0))
                    p.inc(g.dma_start(out=wbuf[grp % 2][:], in_=wsrc[grp]),
                          _wsem(grp), 16)

            @blk.tensor
            def _(t):
                for ot in range(DT):
                    grp, half = ot // 2, ot % 2
                    if half == 0:
                        p.w(t, _wsem(grp), _wval(grp))
                    for sl in range(2):
                        u = 2 * ot + sl
                        if u >= 2:
                            p.w(t, "sca", sca_0 + u - 1)
                        ps = ps_mm[u % 2]
                        last = None
                        for dc in range(DT):
                            last = t.matmul(ps[:],
                                            wbuf[grp % 2][:, half * D + dc * 128:
                                                          half * D + (dc + 1) * 128],
                                            xn_fm[:, sp, dc, sl * 512:
                                                  (sl + 1) * 512],
                                            start=(dc == 0), stop=(dc == DT - 1))
                        p.inc(last, "ten")
                        p.w(t, "sca", sca_0 + u + 1)
                        if u >= 2:
                            p.w(t, "vec", vec_0 + u - 1)
                        for q4 in range(4):
                            last = t.transpose(
                                ps_tr[u % 2][:, q4 * 128:(q4 + 1) * 128],
                                vstg[u % 2][:, q4 * 128:(q4 + 1) * 128],
                                ident[:])
                        p.inc(last, "ten")

            @blk.vector
            def _(v):
                for u in range(32):
                    p.w(v, "ten", ten_0 + 2 * u + 2)
                    if u < 2:
                        p.w(v, "qp", qp_0)
                    else:
                        p.w(v, "qp", qp_0 + 64 * (u - 1))
                    p.inc(v.tensor_copy(out=vevq[u % 2][:],
                                        in_=ps_tr[u % 2][:, 0:512]), "vec")

            @blk.scalar
            def _(s):
                for u in range(33):
                    if u < 32:
                        p.w(s, "ten", ten_0 + 2 * u + 1)
                        p.inc(s.activation(out=vstg[u % 2][:],
                                           in_=ps_mm[u % 2][:], func=act),
                              "sca")
                    if u >= 1:
                        up = u - 1
                        ot, sl = up // 2, up % 2
                        p.w(s, "vec", vec_0 + up + 1)
                        for q4 in range(4):
                            p.inc(s.dma_start(
                                out=scr[base_tile + 4 * sl + q4, ot // 4, :,
                                        (ot % 4) * 128:(ot % 4 + 1) * 128],
                                in_=vevq[up % 2][:, q4 * 128:(q4 + 1) * 128]),
                                "qp", 16)

        sp_marks = {}
        if phase_limit >= 2:
            for sp in (1, 0):
                @blk.tensor
                def _(t, sp=sp):
                    p.w(t, "vec", TK1["evac", 7 if sp == 1 else 15])

                emit_proj_fm(wk, k_fm, sp, AF.Silu, first=(sp == 1))
                if phase_limit >= 3:
                    emit_proj_tm(wv, v_scr, sp, AF.Silu, base_tile=8 * sp)
                if sp == 0 and phase_limit >= 4:
                    emit_proj_tm(wg, g_scr, 0, AF.Sigmoid, base_tile=0)
                if sp == 0 and phase_limit >= 5:
                    qp_now = p.cnt["qp"]

                    @blk.scalar
                    def _(s_, qp_now=qp_now):
                        p.w(s_, "qp", qp_now)
                    emit_proj_fm(wq, q_fm, 0, AF.Silu, dst_base=0)
                sp_marks[("ten_end", sp)] = p.cnt["ten"]

        sca_P2, qp_P2 = p.cnt["sca"], p.cnt["qp"]

        if "k_fm" in taps and phase_limit >= 2:
            ap = tap("k_fm", [128, DT * NB], BF16)

            @blk.sync
            def _(sync):
                p.w(sync, "sca", sca_P2)
                p.inc(sync.dma_start(
                    out=ap, in_=k_fm[:].rearrange("p a b -> p (a b)")), "qo", 16)
        if "q_fm" in taps and phase_limit >= 5:
            ap = tap("q_fm", [128, DT * NO], BF16)

            @blk.sync
            def _(sync):
                p.w(sync, "sca", sca_P2)
                p.inc(sync.dma_start(
                    out=ap, in_=q_fm[:].rearrange("p a b -> p (a b)")), "qo", 16)
        if "v_tm" in taps and phase_limit >= 3:
            ap = tap("v_tm", [NT * 4 * 128, 512], BF16)

            @blk.sync
            def _(sync):
                p.w(sync, "qp", qp_P2)
                p.inc(sync.dma_start(
                    out=ap, in_=v_scr.rearrange("a e b c -> (a e b) c")),
                    "qo", 16)
        if "g_tm" in taps and phase_limit >= 4:
            ap = tap("g_tm", [NOT_ * 4 * 128, 512], BF16)

            @blk.sync
            def _(sync):
                p.w(sync, "qp", qp_P2)
                p.inc(sync.dma_start(
                    out=ap, in_=g_scr.rearrange("a e b c -> (a e b) c")), "qo", 16)

        qp_og = p.cnt["qp"]
        p3m = {}
        if phase_limit >= 6:
            for ib, jset in ((0, JSET_A), (1, JSET_B)):
                nj = len(jset)
                vec_mk0 = p.cnt["vec"]

                @blk.vector
                def _(v, ib=ib, jset=jset):
                    if ib == 0:
                        p.w(v, "qs", moff_v)
                        p.w(v, "gps", gps_p0)
                        p.w(v, "qp", qp_P2)
                    last = None
                    for idx in range(len(jset)):
                        c = idx if ib == 0 else 8 + idx
                        last = v.tensor_scalar(out=masks3[:, idx, :], in0=iota[:],
                                               scalar1=mofb[:, c:c + 1],
                                               scalar2=0.0,
                                               op0=OP.add, op1=OP.is_ge)
                    p.inc(last, "vec")
                vec_mk = p.cnt["vec"]

                ten_s0, vec_s0 = p.cnt["ten"], p.cnt["vec"]

                @blk.tensor
                def _(t, ib=ib, jset=jset, ten_s0=ten_s0, vec_s0=vec_s0,
                      vec_mk=vec_mk):
                    p.w(t, "vec", vec_mk)
                    if ib == 0:
                        p.w(t, "sca", sca_P2)
                    for idx, jt in enumerate(jset):
                        if idx >= 2:
                            p.w(t, "vec", vec_s0 + idx - 1)
                        ps = ps_mm[idx % 2]
                        last = None
                        for dc in range(DT):
                            last = t.matmul(ps[:],
                                            k_fm[:, dc, jt * 128:(jt + 1) * 128],
                                            q_fm[:, dc, ib * 512:(ib + 1) * 512],
                                            start=(dc == 0), stop=(dc == DT - 1))
                        p.inc(last, "ten")

                if ib == 1:
                    p3m["kdone"] = p.cnt["ten"]

                @blk.vector
                def _(v, jset=jset, ten_s0=ten_s0):
                    for idx in range(len(jset)):
                        p.w(v, "ten", ten_s0 + idx + 1)
                        p.inc(v.tensor_tensor(out=stq[:, idx, :],
                                              in0=ps_mm[idx % 2][:],
                                              in1=masks3[:, idx, :],
                                              op=OP.mult), "vec")
                vec_s = p.cnt["vec"]

                qs_g0 = p.cnt["qs"]

                vec_now = p.cnt["vec"]

                @blk.sync
                def _(sync, ib=ib, vec_now=vec_now):
                    if ib == 0:
                        p.w(sync, "qp", qp_P2)
                        p.w(sync, "sca", sca_P2)
                    else:
                        p.w(sync, "vec", vec_now)
                    for it in range(4):
                        for eh in range(4):
                            p.inc(sync.dma_start(
                                out=gtile[:, it, eh * 512:(eh + 1) * 512],
                                in_=g_scr[4 * ib + it, eh]), "qs", 16)
                qs_g = p.cnt["qs"]

                ten_v0, vec_v0 = p.cnt["ten"], p.cnt["vec"]
                qsa_v0, qsb_v0 = p.cnt["qsa"], p.cnt["qsb"]
                qp_o0 = p.cnt["qp"]

                @blk.sync
                def _(sync, ib=ib, jset=jset, vec_v0=vec_v0):
                    nj = len(jset)
                    if ib == 0:
                        p.w(sync, "sca", sca_P2)
                    for eh in range(4):
                        if eh >= 2:
                            p.w(sync, "vec", vec_v0 + 4 * (eh - 1))
                        else:
                            p.w(sync, "vec", vec_v0)
                        buf = vq if eh % 2 == 0 else vq2
                        for idx, jt in enumerate(jset):
                            p.inc(sync.dma_start(
                                out=buf[:, idx, :],
                                in_=v_scr[jt, eh]),
                                "qsa" if eh % 2 == 0 else "qsb", 16)

                @blk.tensor
                def _(t, jset=jset, ten_v0=ten_v0, vec_s=vec_s,
                      qsa_v0=qsa_v0, qsb_v0=qsb_v0, vec_v0=vec_v0):
                    nj = len(jset)
                    p.w(t, "vec", vec_s)
                    for eh in range(4):
                        p.w(t, "qsa" if eh % 2 == 0 else "qsb",
                            (qsa_v0 if eh % 2 == 0 else qsb_v0)
                            + 16 * nj * (eh // 2 + 1))
                        buf = vq if eh % 2 == 0 else vq2
                        for it in range(4):
                            u = 4 * eh + it
                            if u >= 2:
                                p.w(t, "vec", vec_v0 + u - 1)
                            ps = ps_mm[2 + u % 2]
                            last = None
                            for idx in range(nj):
                                last = t.matmul(ps[:],
                                                stq[:, idx, it * 128:(it + 1) * 128],
                                                buf[:, idx, :],
                                                start=(idx == 0),
                                                stop=(idx == nj - 1))
                            p.inc(last, "ten")

                dbg_ap = None
                if ib == 0 and "dbgo" in taps:
                    dbg_ap = tap("dbgo", [128, 3 * 8192], F32)
                    dbg_bf = tap("dbgb", [128, 2 * 8192], BF16)

                @blk.vector
                def _(v, ib=ib, ten_v0=ten_v0, qs_g=qs_g, qp_o0=qp_o0,
                      jset=jset):
                    p.w(v, "qs", qs_g)
                    for eh in range(4):
                        for it in range(4):
                            u = 4 * eh + it
                            p.w(v, "ten", ten_v0 + u + 1)
                            if u >= 2:
                                p.w(v, "qp", qp_o0 + 16 * u)
                            else:
                                p.w(v, "qp", qp_o0)
                            if ib == 0 and eh == 3 and it == 0 and dbg_ap is not None:
                                v.tensor_copy(out=ost[:, 0:512],
                                              in_=ps_mm[2 + u % 2][:])
                            p.inc(v.tensor_tensor(
                                out=ogst[u % 2][:],
                                in0=ps_mm[2 + u % 2][:],
                                in1=gtile[:, it, eh * 512:(eh + 1) * 512],
                                op=OP.mult), "vec")

                @blk.scalar
                def _(s, ib=ib, vec_v0=vec_v0):
                    for eh in range(4):
                        for it in range(4):
                            u = 4 * eh + it
                            p.w(s, "vec", vec_v0 + u + 1)
                            p.inc(s.dma_start(
                                out=og_scr[4 * ib + it, :, eh * 512:(eh + 1) * 512],
                                in_=ogst[u % 2][:]), "qp", 16)

                if dbg_ap is not None and ib == 0:
                    @blk.sync
                    def _(sync, vec_v0=vec_v0):
                        p.w(sync, "vec", vec_v0 + 16)
                        p.inc(sync.dma_start(out=dbg_ap[:, 0:512],
                                             in_=ost[:, 0:512]), "qo", 16)
                        p.inc(sync.dma_start(
                            out=dbg_bf[:, 0:8 * 512],
                            in_=stq[:, 0:8, :].rearrange("p a b -> p (a b)")),
                            "qo", 16)
                        p.inc(sync.dma_start(
                            out=dbg_bf[:, 8192:8192 + 8 * 512],
                            in_=vq2[:, 0:8, :].rearrange("p a b -> p (a b)")),
                            "qo", 16)
            qp_og = p.cnt["qp"]

            if "og" in taps:
                ap = tap("og", [NOT_ * 128, D], BF16)

                @blk.sync
                def _(sync):
                    p.w(sync, "qp", qp_og)
                    p.inc(sync.dma_start(
                        out=ap, in_=og_scr.rearrange("a b c -> (a b) c")), "qo", 16)

        vec_xq = p.cnt["vec"]
        qp_x1 = p.cnt["qp"]
        if phase_limit >= 7:
            ten_a0, vec_a0 = p.cnt["ten"], p.cnt["vec"]
            vec_P3e = p.cnt["vec"]
            ten_P3e = p.cnt["ten"]
            qga_w0, qgb_w0 = p.cnt["qga"], p.cnt["qgb"]
            qsa_a0, qsb_a0 = p.cnt["qsa"], p.cnt["qsb"]

            @blk.gpsimd
            def _(g):
                for dc in range(DT):
                    if dc < 8:
                        p.w(g, "ten", p3m.get("kdone", ten_P3e))
                    else:
                        p.w(g, "ten", ten_P3e)
                        p.w(g, "vec", vec_P3e)
                    p.inc(g.dma_start(out=wpb[dc][:], in_=wp[dc]),
                          "qga" if dc % 2 == 0 else "qgb", 16)

            @blk.sync
            def _(sync):
                p.w(sync, "qp", qp_og)
                for it in range(NOT_):
                    if it >= 2:
                        p.w(sync, "ten", ten_a0 + (it - 1))
                    p.inc(sync.dma_start(out=stg4[it % 2][:],
                                         in_=og_scr[it]), _ldsem(it), 16)

            @blk.tensor
            def _(t):
                for it in range(NOT_):
                    p.w(t, _ldsem(it), _ldval(it, qsa_a0, qsb_a0))
                    if it >= 1:
                        p.w(t, "vec", vec_a0 + it)
                    last = None
                    for ec in range(DT):
                        last = t.transpose(
                            ps_tr[ec // 8][:, (ec % 8) * 128:(ec % 8 + 1) * 128],
                            stg4[it % 2][:, ec * 128:(ec + 1) * 128], ident[:])
                    p.inc(last, "ten")

            @blk.vector
            def _(v):
                for it in range(NOT_):
                    p.w(v, "ten", ten_a0 + it + 1)
                    last = None
                    for half in range(2):
                        last = v.tensor_copy(
                            out=og_fm4[:, half * 8:(half + 1) * 8,
                                       it * 128:(it + 1) * 128],
                            in_=ps_tr[half][:].rearrange("p (c f) -> p c f", c=8))
                    p.inc(last, "vec")
            vec_ogfm = p.cnt["vec"]
            ten_a1 = p.cnt["ten"]

            ten_b0, vec_b0, sca_b0 = p.cnt["ten"], p.cnt["vec"], p.cnt["sca"]
            qsa_b0, qsb_b0 = p.cnt["qsa"], p.cnt["qsb"]
            qp_b0 = p.cnt["qp"]

            TK4 = {}
            c = vec_b0
            for k in range(NOT_ + 1):
                if k < NOT_:
                    for ch in range(4):
                        c += 1; TK4["va", k, ch] = c
                    c += 1; TK4["SUMr", k] = c
                if k >= 1:
                    c += 1; TK4["evQ", k - 1] = c
                if k < NOT_:
                    c += 4; TK4["R0", k] = c
                    c += 6; TK4["BIA", k] = c
                    c += 3; TK4["RX", k] = c
                    c += 1; TK4["r2b", k] = c
            vec_b_end = c
            c = sca_b0
            for k in range(NOT_):
                c += 1; TK4["SQ", k] = c
                c += 1; TK4["sqrt", k] = c
                c += 1; TK4["apply", k] = c
                c += 2; TK4["r2a", k] = c
            c = ten_b0
            for k in range(NOT_ + 1):
                if k < NOT_:
                    for ch in range(4):
                        c += 1; TK4["mmB", k, ch] = c
                if k >= 1:
                    c += 1; TK4["trQ", k - 1] = c
            ten_b_end = c

            @blk.sync
            def _(sync):
                p.w(sync, "ten", ten_P3e)
                for k in range(NOT_):
                    if k >= 2:
                        p.w(sync, "vec", TK4["va", k - 2, 3])
                    p.inc(sync.dma_start(out=x4buf[k % 2][:],
                                         in_=x_in[k * 128:(k + 1) * 128, :]),
                          _ldsem(k), 16)

            @blk.tensor
            def _(t):
                p.w(t, "vec", vec_ogfm)
                for k in range(NOT_ + 1):
                    if k < NOT_:
                        for ch in range(4):
                            if k >= 1:
                                p.w(t, "vec", TK4["va", k - 1, ch])
                            last = None
                            for dc in range(DT):
                                if k == 0 and ch == 0:
                                    p.w(t, "qga" if dc % 2 == 0 else "qgb",
                                        (qga_w0 if dc % 2 == 0 else qgb_w0)
                                        + 16 * (dc // 2 + 1))
                                last = t.matmul(
                                    ps_mm[ch][:],
                                    og_fm4[:, dc, k * 128:(k + 1) * 128],
                                    wpb[dc][:, ch * 512:(ch + 1) * 512],
                                    start=(dc == 0), stop=(dc == DT - 1))
                            p.inc(last, "ten")
                            assert p.cnt["ten"] == TK4["mmB", k, ch]
                    if k >= 1:
                        p.w(t, "vec", TK4["r2b", k - 1])
                        if k >= 2:
                            p.w(t, "vec", TK4["evQ", k - 2])
                        last = None
                        for ec in range(DT):
                            last = t.transpose(
                                ps_tr[ec // 8][:, (ec % 8) * 128:
                                               (ec % 8 + 1) * 128],
                                stg4[(k - 1) % 2][:, ec * 128:(ec + 1) * 128],
                                ident[:])
                        p.inc(last, "ten")
                        assert p.cnt["ten"] == TK4["trQ", k - 1]

            @blk.vector
            def _(v):
                v.memset(mgb[:, 0:1], MAGIC)
                v.memset(mgb[:, 1:2], -MAGIC)
                for k in range(NOT_ + 1):
                    if k < NOT_:
                        p.w(v, _ldsem(k), _ldval(k, qsa_b0, qsb_b0))
                        if k >= 2:
                            p.w(v, "qp", qp_b0 + 16 * (k - 1))
                        for ch in range(4):
                            p.w(v, "ten", TK4["mmB", k, ch])
                            inst = v.tensor_tensor(
                                out=x1t4[k % 2][:, ch * 512:(ch + 1) * 512],
                                in0=ps_mm[ch][:],
                                in1=x4buf[k % 2][:, ch * 512:(ch + 1) * 512],
                                op=OP.add)
                            if ch < 3:
                                p.inc(inst, "vec")
                            else:
                                p.chain(inst, v, "vec")
                            assert p.cnt["vec"] == TK4["va", k, ch]
                        p.inc(v.tensor_reduce(out=SUM[:, k:k + 1],
                                              in_=x1t4[k % 2][:], axis=AX,
                                              op=OP.add), "vec")
                        assert p.cnt["vec"] == TK4["SUMr", k]
                    if k >= 1:
                        p.w(v, "ten", TK4["trQ", k - 1])
                        last = None
                        for half in range(2):
                            last = v.tensor_copy(
                                out=xq_fm[:, half * 8:(half + 1) * 8,
                                          (k - 1) * 128:k * 128],
                                in_=ps_tr[half][:].rearrange(
                                    "p (c f) -> p c f", c=8))
                        p.inc(last, "vec")
                        assert p.cnt["vec"] == TK4["evQ", k - 1]
                    if k < NOT_:
                        p.w(v, "sca", TK4["SQ", k])
                        v.tensor_scalar(out=M0[:, k:k + 1], in0=SUM[:, k:k + 1],
                                        scalar1=1.0 / D, scalar2=None,
                                        op0=OP.mult)
                        v.tensor_scalar(out=E0[:, k:k + 1], in0=SQ[:, k:k + 1],
                                        scalar1=1.0 / D, scalar2=None,
                                        op0=OP.mult)
                        p.chain(v.tensor_mul(V0[:, k:k + 1], M0[:, k:k + 1],
                                             M0[:, k:k + 1]), v, "vec")
                        p.chain(v.tensor_sub(V0[:, k:k + 1], E0[:, k:k + 1],
                                             V0[:, k:k + 1]), v, "vec")
                        p.chain(v.tensor_scalar(out=T0[:, k:k + 1],
                                                in0=V0[:, k:k + 1],
                                                scalar1=1.0, scalar2=LN1_EPS,
                                                op0=OP.mult, op1=OP.add),
                                v, "vec")
                        p.inc(v.reciprocal(R0[:, k:k + 1], T0[:, k:k + 1]),
                              "vec")
                        assert p.cnt["vec"] == TK4["R0", k]
                        p.w(v, "sca", TK4["sqrt", k])
                        p.chain(v.tensor_mul(N1[:, k:k + 1], S0[:, k:k + 1],
                                             S0[:, k:k + 1]), v, "vec")
                        p.chain(v.tensor_mul(N1[:, k:k + 1], T0[:, k:k + 1],
                                             N1[:, k:k + 1]), v, "vec")
                        p.chain(v.tensor_scalar(out=N1[:, k:k + 1],
                                                in0=N1[:, k:k + 1],
                                                scalar1=-0.5, scalar2=1.5,
                                                op0=OP.mult, op1=OP.add),
                                v, "vec")
                        p.chain(v.tensor_mul(SCL[:, k:k + 1], S0[:, k:k + 1],
                                             N1[:, k:k + 1]), v, "vec")
                        p.chain(v.tensor_mul(N2[:, k:k + 1], M0[:, k:k + 1],
                                             SCL[:, k:k + 1]), v, "vec")
                        p.inc(v.tensor_scalar(out=BIA[:, k:k + 1],
                                              in0=N2[:, k:k + 1], scalar1=-1.0,
                                              scalar2=None, op0=OP.mult),
                              "vec")
                        assert p.cnt["vec"] == TK4["BIA", k]
                        p.w(v, "sca", TK4["apply", k])
                        p.chain(v.tensor_reduce(out=AM[:, k:k + 1], in_=osu[:],
                                                axis=AX, op=OP.max,
                                                apply_absolute_value=True),
                                v, "vec")
                        p.chain(v.tensor_scalar(out=SX[:, k:k + 1],
                                                in0=AM[:, k:k + 1],
                                                scalar1=RQMAX, scalar2=QEPS,
                                                op0=OP.mult, op1=OP.max),
                                v, "vec")
                        p.inc(v.reciprocal(RX[:, k:k + 1], SX[:, k:k + 1]),
                              "vec")
                        assert p.cnt["vec"] == TK4["RX", k]
                        p.w(v, "sca", TK4["r2a", k])
                        if k >= 2:
                            p.w(v, "ten", TK4["trQ", k - 2])
                        else:
                            p.w(v, "ten", ten_a1)
                        p.inc(v.tensor_scalar(out=stg4[k % 2][:], in0=osu[:],
                                              scalar1=SX[:, k:k + 1],
                                              scalar2=None, op0=OP.mult),
                              "vec")
                        assert p.cnt["vec"] == TK4["r2b", k]
                assert p.cnt["vec"] == vec_b_end

            @blk.scalar
            def _(s):
                for k in range(NOT_):
                    p.w(s, "vec", TK4["va", k, 3])
                    p.inc(s.activation(out=ost[:], in_=x1t4[k % 2][:],
                                       func=AF.Square,
                                       accum_out=SQ[:, k:k + 1]), "sca")
                    assert p.cnt["sca"] == TK4["SQ", k]
                    p.w(s, "vec", TK4["R0", k])
                    p.inc(s.activation(out=S0[:, k:k + 1], in_=R0[:, k:k + 1],
                                       func=AF.Sqrt), "sca")
                    assert p.cnt["sca"] == TK4["sqrt", k]
                    p.w(s, "vec", TK4["BIA", k])
                    if k >= 1:
                        p.w(s, "vec", TK4["r2b", k - 1])
                    p.inc(s.activation(out=osu[:], in_=x1t4[k % 2][:],
                                       func=AF.Identity, scale=SCL[:, k:k + 1],
                                       bias=BIA[:, k:k + 1]), "sca")
                    assert p.cnt["sca"] == TK4["apply", k]
                    p.w(s, "vec", TK4["RX", k])
                    p.chain(s.activation(out=ost[:], in_=osu[:],
                                         func=AF.Identity,
                                         scale=RX[:, k:k + 1],
                                         bias=mgb[:, 0:1]), s, "sca")
                    p.inc(s.activation(out=osu[:], in_=ost[:],
                                       func=AF.Identity, bias=mgb[:, 1:2]),
                          "sca")
                    assert p.cnt["sca"] == TK4["r2a", k]
                    p.inc(s.dma_start(out=x1_scr[k], in_=x1t4[k % 2][:]),
                          "qp", 16)
            qp_x1 = p.cnt["qp"]

            if "x1" in taps:
                ap = tap("x1", [NOT_ * 128, D], F32)

                @blk.sync
                def _(sync):
                    p.w(sync, "qp", qp_x1)
                    p.inc(sync.dma_start(
                        out=ap, in_=x1_scr.rearrange("a b c -> (a b) c")), "qo", 16)

            assert p.cnt["ten"] == ten_b_end
            vec_xq = p.cnt["vec"]

            if "xq" in taps:
                ap = tap("xq", [128, DT * NO], BF16)

                @blk.sync
                def _(sync):
                    p.w(sync, "vec", vec_xq)
                    p.inc(sync.dma_start(
                        out=ap, in_=xq_fm[:].rearrange("p a b -> p (a b)")), "qo", 16)

        vec_thq = p.cnt["vec"]
        if phase_limit >= 8:
            qga_f0, qgb_f0 = p.cnt["qga"], p.cnt["qgb"]
            ten_f0, vec_f0, sca_f0 = p.cnt["ten"], p.cnt["vec"], p.cnt["sca"]

            def _p_wsem(ppi):
                return "qga" if ppi % 2 == 0 else "qgb"

            def _p_wval(ppi):
                return (qga_f0 if ppi % 2 == 0 else qgb_f0) + 16 * (ppi // 2 + 1)

            def dslot(ppi, which):
                off = (ppi % 2) * 256 + which * 128
                return rrowb[:, off:off + 128]

            def WBG(i):
                return wscb[:, 2 * (i % 8):2 * (i % 8) + 1]

            def WBU(i):
                return wscb[:, 2 * (i % 8) + 1:2 * (i % 8) + 2]

            def WSGc(i):
                return wsc2[:, 2 * (i % 8):2 * (i % 8) + 1]

            def WSUc(i):
                return wsc2[:, 2 * (i % 8) + 1:2 * (i % 8) + 2]

            def t1q(ppi, which):
                return t1qb[2 * (ppi % 2) + which]

            TK = {}
            c = sca_f0
            for it_ in range(IT + 1):
                if it_ < IT:
                    c += 1; TK["gAa2", it_] = c
                if it_ >= 1:
                    c += 1; TK["silu", it_ - 1] = c
                if it_ < IT:
                    c += 1; TK["gBa2", it_] = c
                    c += 1; TK["gAa3", it_] = c
                    c += 1; TK["gBa3", it_] = c
                    c += 1; TK["uAa2", it_] = c
                    c += 1; TK["uBa2", it_] = c
                    c += 1; TK["uAa3", it_] = c
                    c += 1; TK["uBa3", it_] = c
            sca_P5e = c
            c = vec_f0
            for it_ in range(IT + 2):
                if it_ < IT:
                    c += 1; TK["Ag", it_] = c
                    c += 1; TK["Au", it_] = c
                if it_ >= 2:
                    c += 1; TK["E", it_ - 2] = c
                    c += 1; TK["F1", it_ - 2] = c
                    c += 1; TK["F2", it_ - 2] = c
            vec_P5e = c
            c = ten_f0
            for it_ in range(IT + 1):
                if it_ < IT:
                    c += 1; TK["trgA", it_] = c
                if it_ >= 1:
                    c += 1; TK["mmg", it_ - 1] = c
                if it_ < IT:
                    c += 1; TK["trgB", it_] = c
                if it_ >= 1:
                    c += 1; TK["mmu", it_ - 1] = c
                if it_ < IT:
                    c += 1; TK["truA", it_] = c
                    c += 1; TK["truB", it_] = c
            ten_P5e = c

            @blk.gpsimd
            def _(g):
                for j in range(IT):
                    if j >= 2:
                        p.w(g, "ten", TK["truB", j - 2])
                        p.w(g, "vec", TK["Au", j - 2])
                    else:
                        p.w(g, "ten", ten_f0)
                        p.w(g, "vec", vec_xq)
                    p.inc(g.dma_start(out=wbuf[j % 2][:], in_=w1[j]),
                          _p_wsem(j), 16)

            @blk.vector
            def _(v):
                p.w(v, "vec", vec_xq)
                v.memset(hacc[:], 0.0)
                for it_ in range(IT + 2):
                    if it_ < IT:
                        j = it_
                        p.w(v, _p_wsem(j), _p_wval(j))
                        if j >= 8:
                            p.w(v, "sca", TK["silu", j - 8])
                        wbg = wbuf[j % 2][:, 0:D]
                        wbu = wbuf[j % 2][:, D:2 * D]
                        v.tensor_reduce(out=TMPA[:, 0:1], in_=wbg, axis=AX,
                                        op=OP.max, apply_absolute_value=True)
                        v.tensor_reduce(out=TMPA[:, 1:2], in_=wbu, axis=AX,
                                        op=OP.max, apply_absolute_value=True)
                        v.tensor_scalar(out=WSG(j), in0=TMPA[:, 0:1],
                                        scalar1=RQMAX, scalar2=QEPS,
                                        op0=OP.mult, op1=OP.max)
                        v.tensor_scalar(out=WSU(j), in0=TMPA[:, 1:2],
                                        scalar1=RQMAX, scalar2=QEPS,
                                        op0=OP.mult, op1=OP.max)
                        v.reciprocal(WRG(j), WSG(j))
                        v.reciprocal(WRU(j), WSU(j))
                        v.tensor_scalar(out=WBG(j), in0=WRG(j), scalar1=1.0,
                                        scalar2=None, op0=OP.mult)
                        v.tensor_scalar(out=WBU(j), in0=WRU(j), scalar1=1.0,
                                        scalar2=None, op0=OP.mult)
                        v.reciprocal(WSGc(j), WBG(j))
                        v.reciprocal(WSUc(j), WBU(j))
                        p.inc(v.tensor_scalar(out=dslot(j, 0), in0=ident[:],
                                              scalar1=WRG(j), scalar2=None,
                                              op0=OP.mult), "vec")
                        p.inc(v.tensor_scalar(out=dslot(j, 1), in0=ident[:],
                                              scalar1=WRU(j), scalar2=None,
                                              op0=OP.mult), "vec")
                        assert p.cnt["vec"] == TK["Au", j]
                    if it_ >= 2:
                        j = it_ - 2
                        p.w(v, "ten", TK["mmu", j])
                        p.w(v, "sca", TK["silu", j])
                        last = None
                        for half in range(2):
                            last = v.scalar_tensor_tensor(
                                out=h_buf[:, j, half * 512:(half + 1) * 512],
                                in0=ps_mm[2 + half][:], scalar=WSUc(j),
                                in1=gsil[:, half * 512:(half + 1) * 512],
                                op0=OP.mult, op1=OP.mult)
                        p.chain(last, v, "vec")
                        assert p.cnt["vec"] == TK["E", j]
                        p.chain(v.tensor_max(hacc[:], hacc[:], h_buf[:, j, :]),
                                v, "vec")
                        p.inc(v.scalar_tensor_tensor(out=hacc[:],
                                                     in0=h_buf[:, j, :],
                                                     scalar=-1.0, in1=hacc[:],
                                                     op0=OP.mult, op1=OP.max),
                              "vec")

            def emit_tr(t, j, which, hB):
                wbw = wbuf[j % 2][:, which * D:(which + 1) * D]
                last = None
                for cc in range(8):
                    hc = 8 * hB + cc
                    last = t.matmul(
                        ps_ex[cc // 4][:, (cc % 4) * 128:(cc % 4 + 1) * 128],
                        wbw[:, hc * 128:(hc + 1) * 128],
                        dslot(j, which), start=True, stop=True)
                p.inc(last, "ten")

            def emit_mm(t, j, which):
                last = None
                for half in range(2):
                    ps = ps_mm[2 * which + half]
                    for hc in range(DT):
                        last = t.matmul(
                            ps[:],
                            t1q(j, which)[:, hc * 128:(hc + 1) * 128],
                            xq_fm[:, hc, half * 512:(half + 1) * 512],
                            start=(hc == 0), stop=(hc == DT - 1))
                p.inc(last, "ten")

            @blk.tensor
            def _(t):
                for it_ in range(IT + 1):
                    if it_ < IT:
                        if it_ >= 1:
                            p.w(t, "sca", TK["uBa2", it_ - 1])
                        p.w(t, "vec", TK["Au", it_])
                        emit_tr(t, it_, 0, 0)
                        assert p.cnt["ten"] == TK["trgA", it_]
                    if it_ >= 1:
                        p.w(t, "sca", TK["gBa3", it_ - 1])
                        emit_mm(t, it_ - 1, 0)
                        assert p.cnt["ten"] == TK["mmg", it_ - 1]
                    if it_ < IT:
                        p.w(t, "sca", TK["gAa2", it_])
                        emit_tr(t, it_, 0, 1)
                        assert p.cnt["ten"] == TK["trgB", it_]
                    if it_ >= 1:
                        p.w(t, "sca", TK["uBa3", it_ - 1])
                        if it_ >= 2:
                            p.w(t, "vec", TK["E", it_ - 2])
                        emit_mm(t, it_ - 1, 1)
                        assert p.cnt["ten"] == TK["mmu", it_ - 1]
                    if it_ < IT:
                        p.w(t, "sca", TK["gBa2", it_])
                        emit_tr(t, it_, 1, 0)
                        assert p.cnt["ten"] == TK["truA", it_]
                        p.w(t, "sca", TK["uAa2", it_])
                        emit_tr(t, it_, 1, 1)
                        assert p.cnt["ten"] == TK["truB", it_]

            def emit_r12(s, hB, k2):
                base = 1024 * hB
                s.activation(out=uwb[:, base:base + 512],
                             in_=ps_ex[0][:], func=AF.Identity,
                             bias=mgb[:, 0:1])
                p.inc(s.activation(out=uwb[:, base + 512:base + 1024],
                                   in_=ps_ex[1][:], func=AF.Identity,
                                   bias=mgb[:, 0:1]), "sca")
                assert p.cnt["sca"] == k2

            def emit_r3(s, j, which, hB, k3):
                base = 1024 * hB
                p.inc(s.activation(out=t1q(j, which)[:, base:base + 1024],
                                   in_=uwb[:, base:base + 1024],
                                   func=AF.Identity, bias=mgb[:, 1:2]), "sca")
                assert p.cnt["sca"] == k3

            @blk.scalar
            def _(s):
                for it_ in range(IT + 1):
                    if it_ < IT:
                        p.w(s, "ten", TK["trgA", it_])
                        emit_r12(s, 0, TK["gAa2", it_])
                    if it_ >= 1:
                        j = it_ - 1
                        p.w(s, "ten", TK["mmg", j])
                        if j >= 1:
                            p.w(s, "vec", TK["E", j - 1])
                        last = None
                        for half in range(2):
                            last = s.activation(
                                out=gsil[:, half * 512:(half + 1) * 512],
                                in_=ps_mm[half][:], func=AF.Silu,
                                scale=WSGc(j))
                        p.inc(last, "sca")
                        assert p.cnt["sca"] == TK["silu", j]
                    if it_ < IT:
                        p.w(s, "ten", TK["trgB", it_])
                        emit_r12(s, 1, TK["gBa2", it_])
                        emit_r3(s, it_, 0, 0, TK["gAa3", it_])
                        emit_r3(s, it_, 0, 1, TK["gBa3", it_])
                        p.w(s, "ten", TK["truA", it_])
                        emit_r12(s, 0, TK["uAa2", it_])
                        p.w(s, "ten", TK["truB", it_])
                        emit_r12(s, 1, TK["uBa2", it_])
                        emit_r3(s, it_, 1, 0, TK["uAa3", it_])
                        emit_r3(s, it_, 1, 1, TK["uBa3", it_])

            assert p.cnt["vec"] == vec_P5e and p.cnt["ten"] == ten_P5e \
                and p.cnt["sca"] == sca_P5e
            vec_P5 = p.cnt["vec"]
            ten_P5 = p.cnt["ten"]

            @blk.tensor
            def _(t):
                p.w(t, "vec", vec_P5)
                last = None
                for bk in range(8):
                    last = t.transpose(ps_tr[0][:, bk * 128:(bk + 1) * 128],
                                       hacc[:, bk * 128:(bk + 1) * 128], ident[:])
                p.inc(last, "ten")
            ten_sh1 = p.cnt["ten"]

            @blk.vector
            def _(v):
                p.w(v, "ten", ten_sh1)
                last = None
                for bk in range(8):
                    last = v.tensor_reduce(out=SH[:, bk:bk + 1],
                                           in_=ps_tr[0][:, bk * 128:(bk + 1) * 128],
                                           axis=AX, op=OP.max)
                p.chain(last, v, "vec")
                p.chain(v.tensor_scalar(out=SHS[:, 0:8], in0=SH[:, 0:8],
                                        scalar1=RQMAX, scalar2=QEPS,
                                        op0=OP.mult, op1=OP.max), v, "vec")
                p.inc(v.reciprocal(RH[:, 0:8], SHS[:, 0:8]), "vec")
            vec_sh1 = p.cnt["vec"]

            @blk.vector
            def _(v):
                p.w(v, "vec", vec_sh1)
                p.chain(v.memset(rrowb[:, 0:128], 0.0), v, "vec")
                p.inc(v.tensor_copy(out=rrowb[:, 0:8], in_=RH[:, 0:8]), "vec")
            vec_rr1 = p.cnt["vec"]

            @blk.tensor
            def _(t):
                p.w(t, "vec", vec_rr1)
                p.inc(t.transpose(ps_tr[0][:, 0:128], rrowb[:, 0:128],
                                  ident[:]), "ten")
            ten_rr = p.cnt["ten"]

            @blk.vector
            def _(v):
                p.w(v, "ten", ten_rr)
                p.inc(v.tensor_copy(out=stg[0:8, 0:128],
                                    in_=ps_tr[0][0:8, 0:128]), "vec")
            vec_rr2 = p.cnt["vec"]

            @blk.sync
            def _(sync):
                p.w(sync, "vec", vec_rr2)
                p.inc(sync.dma_start(out=rrowb[0:1, 0:1024],
                                     in_=stg[0:8, 0:128]), "qp", 16)
            qp_rr = p.cnt["qp"]

            @blk.tensor
            def _(t):
                p.w(t, "qp", qp_rr)
                last = None
                for hf in range(2):
                    last = t.matmul(ps_mm[hf][:], ones1[0:1, 0:128],
                                    rrowb[0:1, hf * 512:(hf + 1) * 512],
                                    start=True, stop=True)
                p.inc(last, "ten")
            ten_rbc = p.cnt["ten"]

            vec_rbc0 = p.cnt["vec"] + 1
            qga_d0, qgb_d0 = p.cnt["qga"], p.cnt["qgb"]
            ten_d0, sca_d0 = p.cnt["ten"], p.cnt["sca"]
            GRP_SZ = [8, 8, 8, 8, 8, 4]
            QN = [0, 1, 2, 2, 3, 3]
            Q4 = I_FFN // 4

            TK6 = {}
            c = vec_rbc0
            c += 4; TK6["A", 0] = c
            for j in range(IT):
                c += 1; TK6["thq", j] = c
            for ht in range(DT):
                if ht + 1 < DT:
                    c += 4; TK6["A", ht + 1] = c
                for gg in range(6):
                    c += 1; TK6["evac", ht, gg] = c
            vec_P6e = c
            c = sca_d0
            for ht in range(DT):
                for q in range(4):
                    c += 1; TK6["B", ht, q] = c
                if ht >= 1:
                    c += 1; TK6["dfm", ht - 1] = c
            c += 1; TK6["dfm", DT - 1] = c
            sca_P6e = c
            c = ten_d0
            for ht in range(DT):
                for gg in range(6):
                    c += 1; TK6["tr", ht, gg] = c
                c += 1; TK6["mm", ht] = c
            ten_P6e = c
            vec_thq = TK6["thq", IT - 1]

            @blk.gpsimd
            def _(g):
                for ht in range(DT):
                    if ht >= 2:
                        p.w(g, "sca", TK6["B", ht - 2, 3])
                        p.w(g, "vec", TK6["A", ht - 2])
                    else:
                        p.w(g, "ten", TK["mmu", IT - 1])
                        p.w(g, "vec", TK["Au", IT - 1])
                    p.inc(g.dma_start(out=w2b[ht % 2][:], in_=w2[ht]),
                          "qga" if ht % 2 == 0 else "qgb", 16)

            def emit_A(v, ht):
                p.w(v, "qga" if ht % 2 == 0 else "qgb",
                    (qga_d0 if ht % 2 == 0 else qgb_d0) + 16 * (ht // 2 + 1))
                if ht >= 8:
                    p.w(v, "sca", TK6["dfm", ht - 8])
                wb2 = w2b[ht % 2]
                v.tensor_reduce(out=TMPA[:, 0:1], in_=wb2[:, 0:I_FFN // 2],
                                axis=AX, op=OP.max, apply_absolute_value=True)
                p.chain(v.tensor_reduce(out=TMPA[:, 1:2],
                                        in_=wb2[:, I_FFN // 2:],
                                        axis=AX, op=OP.max,
                                        apply_absolute_value=True), v, "vec")
                p.chain(v.tensor_max(TMPA[:, 2:3], TMPA[:, 0:1],
                                     TMPA[:, 1:2]), v, "vec")
                p.chain(v.tensor_scalar(out=WS2(ht), in0=TMPA[:, 2:3],
                                        scalar1=RQMAX, scalar2=QEPS,
                                        op0=OP.mult, op1=OP.max), v, "vec")
                p.inc(v.reciprocal(WR2(ht), WS2(ht)), "vec")
                assert p.cnt["vec"] == TK6["A", ht]

            @blk.vector
            def _(v):
                p.w(v, "ten", ten_rbc)
                v.tensor_copy(out=rbc[:, 0:512], in_=ps_mm[0][:])
                p.chain(v.tensor_copy(out=rbc[:, 512:1024], in_=ps_mm[1][:]),
                        v, "vec")
                assert p.cnt["vec"] == vec_rbc0
                emit_A(v, 0)
                tsc = [ost[:, 0:1024], ost[:, 1024:2048]]
                for j in range(IT + 1):
                    if j < IT:
                        v.tensor_tensor(out=tsc[j % 2], in0=h_buf[:, j, :],
                                        in1=rbc[:], op=OP.mult)
                    if j >= 1:
                        p.inc(v.tensor_scalar(out=h_buf[:, j - 1, :],
                                              in0=tsc[(j - 1) % 2],
                                              scalar1=MAGIC, scalar2=MAGIC,
                                              op0=OP.add, op1=OP.subtract),
                              "vec")
                        assert p.cnt["vec"] == TK6["thq", j - 1]
                for ht in range(DT):
                    if ht + 1 < DT:
                        emit_A(v, ht + 1)
                    for gg, sz in enumerate(GRP_SZ):
                        p.w(v, "ten", TK6["tr", ht, gg])
                        off = sum(GRP_SZ[:gg])
                        p.inc(v.tensor_copy(
                            out=t2q[:, off * 128:(off + sz) * 128],
                            in_=ps_tr[gg % 2][:, 0:sz * 128]), "vec")
                        assert p.cnt["vec"] == TK6["evac", ht, gg]
                assert p.cnt["vec"] == vec_P6e

            @blk.scalar
            def _(s):
                for ht in range(DT):
                    p.w(s, "vec", TK6["A", ht])
                    p.w(s, "qga" if ht % 2 == 0 else "qgb",
                        (qga_d0 if ht % 2 == 0 else qgb_d0) + 16 * (ht // 2 + 1))
                    if ht >= 1:
                        p.w(s, "ten", TK6["tr", ht - 1, 5])
                    wb2 = w2b[ht % 2]
                    def a1(qq):
                        s.activation(out=wuq[qq % 2][:],
                                     in_=wb2[:, qq * Q4:(qq + 1) * Q4],
                                     func=AF.Identity, scale=WR2(ht),
                                     bias=mgb[:, 0:1])

                    def a2(qq):
                        p.inc(s.activation(out=s2n[:, qq * Q4:(qq + 1) * Q4],
                                           in_=wuq[qq % 2][:],
                                           func=AF.Identity,
                                           bias=mgb[:, 1:2]), "sca")

                    a1(0); a1(1); a2(0); a1(2); a2(1); a1(3); a2(2); a2(3)
                    assert p.cnt["sca"] == TK6["B", ht, 3]
                    if ht >= 1:
                        p.w(s, "ten", TK6["mm", ht - 1])
                        last = None
                        for half in range(2):
                            last = s.activation(
                                out=down_fm[:, ht - 1,
                                            half * 512:(half + 1) * 512],
                                in_=ps_mm[half][:], func=AF.Identity,
                                scale=WS2(ht - 1))
                        p.inc(last, "sca")
                        assert p.cnt["sca"] == TK6["dfm", ht - 1]
                p.w(s, "ten", TK6["mm", DT - 1])
                last = None
                for half in range(2):
                    last = s.activation(
                        out=down_fm[:, DT - 1, half * 512:(half + 1) * 512],
                        in_=ps_mm[half][:], func=AF.Identity,
                        scale=WS2(DT - 1))
                p.inc(last, "sca")
                assert p.cnt["sca"] == sca_P6e

            @blk.tensor
            def _(t):
                for ht in range(DT):
                    off = 0
                    for gg, sz in enumerate(GRP_SZ):
                        p.w(t, "sca", TK6["B", ht, QN[gg]])
                        if gg >= 2:
                            p.w(t, "vec", TK6["evac", ht, gg - 2])
                        elif ht >= 1:
                            p.w(t, "vec", TK6["evac", ht - 1, gg + 4])
                        last = None
                        for j in range(sz):
                            ic = off + j
                            last = t.transpose(
                                ps_tr[gg % 2][:, j * 128:(j + 1) * 128],
                                s2n[:, ic * 128:(ic + 1) * 128], ident[:])
                        p.inc(last, "ten")
                        assert p.cnt["ten"] == TK6["tr", ht, gg]
                        off += sz
                    p.w(t, "vec", TK6["evac", ht, 5])
                    if ht >= 1:
                        p.w(t, "sca", TK6["dfm", ht - 1])
                    last = None
                    for half in range(2):
                        ps = ps_mm[half]
                        for ic in range(IT):
                            if ht == 0 and half == 0:
                                p.w(t, "vec", TK6["thq", ic])
                            last = t.matmul(
                                ps[:], t2q[:, ic * 128:(ic + 1) * 128],
                                h_buf[:, ic, half * 512:(half + 1) * 512],
                                start=(ic == 0), stop=(ic == IT - 1))
                    p.inc(last, "ten")
                    assert p.cnt["ten"] == TK6["mm", ht]
            ten_P6 = p.cnt["ten"]

            if "thq" in taps:
                ap = tap("thq", [128, IT * NO], BF16)

                @blk.sync
                def _(sync):
                    p.w(sync, "vec", vec_thq)
                    p.inc(sync.dma_start(
                        out=ap, in_=h_buf[:].rearrange("p a b -> p (a b)")),
                        "qo", 16)

            sca_P6 = p.cnt["sca"]

            ten_fin0, vec_fin0 = p.cnt["ten"], p.cnt["vec"]
            qsa_f0, qsb_f0 = p.cnt["qsa"], p.cnt["qsb"]
            qo_f0 = p.cnt["qo"]
            obuf = [ost, osu]

            @blk.sync
            def _(sync):
                p.w(sync, "sca", sca_P6)
                p.w(sync, "qp", qp_x1)
                for nt in range(NOT_):
                    if nt >= 2:
                        p.w(sync, "vec", vec_fin0 + 2 * (nt - 2) + 2)
                    p.inc(sync.dma_start(out=xbuf[nt % 2][:], in_=x1_scr[nt]),
                          _ldsem(nt), 16)

            @blk.scalar
            def _(s):
                for nt in range(NOT_):
                    p.w(s, "vec", vec_fin0 + 2 * nt + 2)
                    p.inc(s.dma_start(out=out[nt * 128:(nt + 1) * 128, :],
                                      in_=obuf[nt % 2][:]), "qo", 16)

            @blk.tensor
            def _(t):
                p.w(t, "sca", sca_P6)
                for nt in range(NOT_):
                    if nt >= 1:
                        p.w(t, "vec", vec_fin0 + 2 * (nt - 1) + 1)
                    last = None
                    for ht in range(DT):
                        last = t.transpose(
                            ps_tr[ht // 8][:, (ht % 8) * 128:(ht % 8 + 1) * 128],
                            down_fm[:, ht, nt * 128:(nt + 1) * 128], ident[:])
                    p.inc(last, "ten")

            @blk.vector
            def _(v):
                for nt in range(NOT_):
                    p.w(v, "ten", ten_fin0 + nt + 1)
                    p.w(v, _ldsem(nt), _ldval(nt, qsa_f0, qsb_f0))
                    if nt >= 2:
                        p.w(v, "qo", qo_f0 + 16 * (nt - 1))
                    last = None
                    for half in range(2):
                        last = v.tensor_scalar(
                            out=obuf[nt % 2][:, half * 1024:(half + 1) * 1024],
                            in0=ps_tr[half][:],
                            scalar1=SHS[:, nt:nt + 1], scalar2=None,
                            op0=OP.mult)
                    p.chain(last, v, "vec")
                    p.inc(v.tensor_add(obuf[nt % 2][:], obuf[nt % 2][:],
                                       xbuf[nt % 2][:]), "vec")



        @blk.sync
        def _(sync):
            for nm in ("qs", "qsa", "qsb", "qga", "qgb", "qp", "qo"):
                p.w(sync, nm, p.cnt[nm])

    return nc, tap_aps


def _tile_w(W):
    O, Din = W.shape
    A = W.reshape(O // 256, 2, 128, Din // 128, 128)
    A = A.transpose(0, 4, 1, 3, 2)
    return np.ascontiguousarray(A.reshape(O // 256, 128, 2 * Din))


def _perm_for_core(c):
    if c % 2 == 0:
        return (0, 3), (1, 2)
    return (1, 2), (0, 3)


def _moff_for_core(c):
    own, partner = _perm_for_core(c)
    blocks = [own[0], own[1], partner[0], partner[1]]
    gstart = [512 * b for b in blocks]
    offs = []
    for jt in JSET_A:
        offs.append(gstart[0] - (gstart[jt // 4] + 128 * (jt % 4)))
    for jt in JSET_B:
        offs.append(gstart[1] - (gstart[jt // 4] + 128 * (jt % 4)))
    m = np.zeros((128, 24), np.float32)
    m[:] = np.array(offs, np.float32)[None, :]
    return m


def make_inputs_for_core(c, X, W):
    own, partner = _perm_for_core(c)
    xb = X[c // 2]
    rows = np.concatenate([xb[512 * blk:512 * (blk + 1)]
                           for blk in (own + partner)], axis=0)
    d = {"x_in": np.ascontiguousarray(rows), "moff": _moff_for_core(c)}
    d.update(W)
    return d


def prep_weights(L0_attn_Wq, L0_attn_Wk, L0_attn_Wv, L0_attn_Wg, L0_attn_Wp,
                 L0_ffn_w_gate_up, L0_ffn_w_down):
    return {
        "wq": _tile_w(np.asarray(L0_attn_Wq, np.float32)),
        "wk": _tile_w(np.asarray(L0_attn_Wk, np.float32)),
        "wv": _tile_w(np.asarray(L0_attn_Wv, np.float32)),
        "wg": _tile_w(np.asarray(L0_attn_Wg, np.float32)),
        "wp": np.ascontiguousarray(
            np.asarray(L0_attn_Wp, np.float32).T.reshape(DT, 128, D)),
        "w1": np.ascontiguousarray(
            np.asarray(L0_ffn_w_gate_up, np.float32).reshape(2, IT, 128, D)
            .transpose(1, 2, 0, 3).reshape(IT, 128, 2 * D)),
        "w2": np.ascontiguousarray(
            np.asarray(L0_ffn_w_down, np.float32).reshape(DT, 128, I_FFN)),
    }


def kernel(L0_attn_x, L0_attn_Wq, L0_attn_Wk, L0_attn_Wv, L0_attn_Wg,
           L0_attn_Wp, L0_ffn_w_gate_up, L0_ffn_w_down):
    X = np.ascontiguousarray(np.asarray(L0_attn_x, np.float32))
    W = prep_weights(L0_attn_Wq, L0_attn_Wk, L0_attn_Wv, L0_attn_Wg,
                     L0_attn_Wp, L0_ffn_w_gate_up, L0_ffn_w_down)
    nc, _ = build()
    in_maps = [make_inputs_for_core(c, X, W) for c in range(8)]
    res = run_bass_kernel_spmd(nc, in_maps, core_ids=list(range(8)))
    outp = np.empty((4, 2048, 2048), np.float32)
    for c in range(8):
        own, _ = _perm_for_core(c)
        o = np.asarray(res.results[c]["out"], np.float32)
        b = c // 2
        outp[b, 512 * own[0]:512 * (own[0] + 1)] = o[0:512]
        outp[b, 512 * own[1]:512 * (own[1] + 1)] = o[512:1024]
    return outp


# version 14
# speedup vs baseline: 1.1454x; 1.0086x over previous
import numpy as np
import ml_dtypes
import concourse.bass as bass
import concourse.mybir as mybir
from concourse.bass_utils import run_bass_kernel_spmd

F32 = mybir.dt.float32
BF16 = mybir.dt.bfloat16
OP = mybir.AluOpType
AF = mybir.ActivationFunctionType
AX = mybir.AxisListType.X

MAGIC = 12582912.0
LN1_EPS = 1e-5
RMS_EPS = float(np.finfo(np.float32).eps)
RQMAX = 1.0 / 127.0
QEPS = 1e-8

D = 2048
NB = 2048
NO = 1024
DT = 16
NT = 16
NOT_ = 8
I_FFN = 5632
IT = 44
KB = 1024

JSET_A = [0, 1, 2, 3, 8, 9, 10, 11]
JSET_B = list(range(16))


class P:
    def __init__(self, nc, st, names):
        self.sem = {}
        self.cnt = {}
        for n in names:
            self.sem[n] = st.enter_context(nc.semaphore(n))
            self.cnt[n] = 0

    def inc(self, inst, name, k=1):
        inst.then_inc(self.sem[name], k)
        self.cnt[name] += k
        return self.cnt[name]

    def tick(self, eng, name):
        eng.sem_inc(self.sem[name], 1)
        self.cnt[name] += 1
        return self.cnt[name]

    def w(self, eng, name, v):
        if v > 0:
            eng.wait_ge(self.sem[name], v)

    def chain(self, inst, eng, name):
        self.inc(inst, name)
        eng.wait_ge(self.sem[name], self.cnt[name])

    def sw(self, eng, name):
        if self.cnt[name] > 0:
            eng.wait_ge(self.sem[name], self.cnt[name])


def build(taps=(), phase_limit=99):
    taps = set(taps)
    nc = bass.Bass("TRN2", target_bir_lowering=False,
                   detect_race_conditions=False)

    x_in = nc.dram_tensor("x_in", [NB, D], F32, kind="ExternalInput").ap()
    wq = nc.dram_tensor("wq", [8, 128, 2 * D], F32, kind="ExternalInput").ap()
    wk = nc.dram_tensor("wk", [8, 128, 2 * D], F32, kind="ExternalInput").ap()
    wv = nc.dram_tensor("wv", [8, 128, 2 * D], F32, kind="ExternalInput").ap()
    wg = nc.dram_tensor("wg", [8, 128, 2 * D], F32, kind="ExternalInput").ap()
    wp = nc.dram_tensor("wp", [DT, 128, D], F32, kind="ExternalInput").ap()
    w1 = nc.dram_tensor("w1", [IT, 128, 2 * D], F32, kind="ExternalInput").ap()
    w2 = nc.dram_tensor("w2", [DT, 128, I_FFN], F32, kind="ExternalInput").ap()
    moff = nc.dram_tensor("moff", [128, 24], F32, kind="ExternalInput").ap()

    out = nc.dram_tensor("out", [NO, D], F32, kind="ExternalOutput").ap()

    xn_scr = nc.dram_tensor("xn_scr", [4, 128, DT * 512], BF16).ap()
    v_scr = nc.dram_tensor("v_scr", [NT, 4, 128, 512], BF16).ap()
    g_scr = nc.dram_tensor("g_scr", [NOT_, 4, 128, 512], BF16).ap()
    og_scr = nc.dram_tensor("og_scr", [NOT_, 128, D], BF16).ap()
    x1_scr = nc.dram_tensor("x1_scr", [NOT_, 128, D], F32).ap()

    tap_aps = {}

    def tap(name, shape, dtype=F32):
        if name in taps:
            tap_aps[name] = nc.dram_tensor("tap_" + name, shape, dtype,
                                           kind="ExternalOutput").ap()
            return tap_aps[name]
        return None

    from contextlib import ExitStack
    with ExitStack() as st:
        _allocs = []

        BASE = 24 * KB

        def at(name, shape, dt, off_b):
            per = int(np.prod(shape[1:])) * (2 if dt == BF16 else 4)
            _allocs.append((name, off_b, off_b + per))
            assert BASE + off_b + per <= 224 * KB, f"{name} over budget"
            return nc.alloc_sbuf_tensor_at(name, shape, dt, offset=BASE + off_b)

        k_fm = at("k_fm", [128, DT, NB], BF16, 0)
        h_buf = at("h_buf", [128, IT, NO], BF16, 0)

        xn_fm = at("xn_fm", [128, 2, DT, NO], BF16, 64 * KB)
        q_fm = xn_fm[:, 1]
        xbuf = [at(f"xbuf{i}", [128, D], F32, (64 + 8 * i) * KB) for i in range(2)]
        xbuf1 = [at("xbuf1a", [128, D], F32, 161 * KB + 512),
                 at("xbuf1b", [128, D], F32, 169 * KB + 512)]
        xb2 = [at(f"xb2{i}", [128, D], F32, (64 + 8 * i) * KB)
               for i in range(2)]
        stgP1 = [at(f"stgP1{i}", [128, D], BF16, (148 + 4 * i) * KB)
                 for i in range(2)]
        vevq = [at(f"vevq{i}", [128, 512], BF16, (146 + i) * KB)
                for i in range(2)]
        x1t = at("x1t", [128, D], F32, 80 * KB)
        sqdump = at("sqdump", [128, D], F32, 80 * KB)
        stg = at("stg", [128, D], BF16, 88 * KB)
        stg2 = at("stg2", [128, D], BF16, 92 * KB)
        s2n = at("s2n", [128, I_FFN], BF16, 88 * KB)

        x4buf = [at(f"x4buf{i}", [128, D], F32, (32 + 8 * i) * KB)
                 for i in range(2)]
        x1t4 = [at(f"x1t4{i}", [128, D], F32, (48 + 8 * i) * KB)
                for i in range(2)]
        og_fm4 = at("og_fm4", [128, DT, NO], BF16, 64 * KB)
        stg4 = [at(f"stg4{i}", [128, D], BF16, (161 + 4 * i) * KB + 512)
                for i in range(2)]
        osu = at("osu", [128, D], F32, 169 * KB + 512)
        poob = at("poob", [128, 1024], F32, 161 * KB + 512)
        wpb = [at(f"wpb{dc}", [128, D], BF16,
                  (4 * dc if dc < 8 else 128 + 4 * (dc - 8)) * KB)
               for dc in range(DT)]
        stq = at("stq", [128, 16, 512], BF16, 64 * KB)
        vq = at("vq", [128, 16, 512], BF16, 80 * KB)
        og_fm = at("og_fm", [128, DT, NO], BF16, 96 * KB)
        xq_fm = at("xq_fm", [128, DT, NO], BF16, 96 * KB)
        down_fm = at("down_fm", [128, DT, NO], BF16, 100 * KB)

        wbuf = [at(f"wbuf{i}", [128, 2 * D], BF16, (128 + 8 * i) * KB)
                for i in range(2)]
        gtile = at("gtile", [128, 4, D], BF16, 128 * KB)
        vq2 = at("vq2", [128, 16, 512], BF16, 144 * KB)
        vstg = [at(f"vstg{i}", [128, 512], BF16, (144 + i) * KB) for i in range(2)]
        t1qb = [at(f"t1q{i}", [128, D], BF16, (144 + 4 * i) * KB) for i in range(4)]
        uwb = at("uwb", [128, D], F32, 88 * KB)
        w2b = [at(f"w2b{i}", [128, I_FFN], BF16, (132 + 11 * i) * KB)
               for i in range(2)]
        t2q = at("t2q", [128, IT * 128], BF16, 161 * KB + 512)

        ident = at("ident", [128, 128], BF16, 160 * KB)
        ones1 = at("ones1", [1, 512], BF16, 160 * KB + 256)
        ogst = [at(f"ogst{i}", [128, 512], BF16, 165 * KB + 256 + i * KB)
                for i in range(2)]
        masks3 = at("masks3", [128, 16, 512], BF16, 167 * KB + 256 + 256)
        gsil = at("gsil", [128, NO], F32, 167 * KB + 512)
        hacc = at("hacc", [128, NO], BF16, 172 * KB)
        rbc = at("rbc", [128, NO], F32, 128 * KB)
        rrowb = at("rrowb", [128, 1024], BF16, 176 * KB)
        wuq = [at(f"wuq{i}", [128, I_FFN // 4], F32,
                   (172 if i == 0 else 178) * KB + 512) for i in range(2)]

        iota = at("iota", [128, 512], F32, 184 * KB)
        sv = at("sv", [128, 1024], F32, 186 * KB)
        ost = at("ost", [128, D], F32, 190 * KB)
        mofb = at("mofb", [128, 24], F32, 198 * KB)
        wsc = at("wsc", [128, 64], F32, 198 * KB + 128)
        wscb = at("wscb", [128, 16], BF16, 178 * KB)
        wsc2 = at("wsc2", [128, 16], F32, 178 * KB + 128)
        mgb = at("mgb", [128, 2], F32, 178 * KB + 320)

        ps_mm = [st.enter_context(nc.psum_tensor(f"ps_mm{i}", [128, 512], F32))
                 for i in range(4)]
        ps_tr = [st.enter_context(nc.psum_tensor(f"ps_tr{i}", [128, 1024], BF16))
                 for i in range(2)]
        ps_ex = [st.enter_context(nc.psum_tensor(f"ps_ex{i}", [128, 512], F32))
                 for i in range(2)]

        p = P(nc, st, ["qs", "qsa", "qsb", "qga", "qgb", "qg2", "qp", "qo",
                       "ten", "vec", "sca", "gps"])
        block = st.enter_context(nc.Block())

        class _B:

            def vector(self, f):
                block.vector(lambda v: (p.sw(v, "vec"), f(v))[1])

            def scalar(self, f):
                block.scalar(lambda s_: (p.sw(s_, "sca"), f(s_))[1])

            def tensor(self, f):
                block.tensor(lambda t: (p.sw(t, "ten"), f(t))[1])

            def gpsimd(self, f):
                block.gpsimd(lambda g: (p.sw(g, "gps"), f(g))[1])

            def sync(self, f):
                block.sync(f)

        blk = _B()

        SUM = sv[:, 0:16]
        SQ = sv[:, 16:32]
        M0 = sv[:, 32:48]
        E0 = sv[:, 48:64]
        V0 = sv[:, 64:80]
        T0 = sv[:, 80:96]
        R0 = sv[:, 96:112]
        S0 = sv[:, 112:128]
        N1 = sv[:, 128:144]
        N2 = sv[:, 144:160]
        SCL = sv[:, 160:176]
        BIA = sv[:, 176:192]
        AM = sv[:, 192:200]
        SX = sv[:, 200:208]
        RX = sv[:, 208:216]
        SH = sv[:, 216:224]
        SHS = sv[:, 224:232]
        RH = sv[:, 232:240]
        TMPA = sv[:, 240:248]

        def WSG(i):
            return wsc[:, i % 8:i % 8 + 1]

        def WRG(i):
            return wsc[:, 8 + i % 8:8 + i % 8 + 1]

        def WSU(i):
            return wsc[:, 16 + i % 8:16 + i % 8 + 1]

        def WRU(i):
            return wsc[:, 24 + i % 8:24 + i % 8 + 1]

        def WS2(i):
            return wsc[:, 32 + i % 8:32 + i % 8 + 1]

        def WR2(i):
            return wsc[:, 40 + i % 8:40 + i % 8 + 1]

        @blk.sync
        def _(sync):
            p.inc(sync.dma_start(out=mofb[:], in_=moff), "qs", 16)
        moff_v = p.cnt["qs"]

        @blk.vector
        def _(v):
            v.memset(ident[:], 0.0)
            p.inc(v.memset(ones1[:], 1.0), "vec")
        vec_ms = p.cnt["vec"]

        @blk.gpsimd
        def _(g):
            p.w(g, "vec", vec_ms)
            p.inc(g.affine_select(out=ident[:], in_=ident[:],
                                  compare_op=OP.not_equal, fill=1.0, base=0,
                                  pattern=[[-1, 128]], channel_multiplier=1),
                  "gps")
            p.inc(g.iota(out=iota[:], pattern=[[1, 512]], base=0,
                         channel_multiplier=-1,
                         allow_small_or_imprecise_dtypes=True), "gps")
        gps_p0 = p.cnt["gps"]

        if "iota" in taps:
            ap = tap("iota", [128, 512], F32)

            @blk.sync
            def _(sync):
                p.w(sync, "gps", gps_p0)
                p.inc(sync.dma_start(out=ap, in_=iota[:]), "qo", 16)

        qsa0, qsb0 = p.cnt["qsa"], p.cnt["qsb"]

        def _ldsem(r):
            return "qsa" if r % 2 == 0 else "qsb"

        def _ldval(r, base_a=qsa0, base_b=qsb0):
            return (base_a if r % 2 == 0 else base_b) + 16 * (r // 2 + 1)

        ORD1 = list(range(8, 16)) + list(range(8))
        vec_10, sca_10, ten_10 = p.cnt["vec"], p.cnt["sca"], p.cnt["ten"]
        qs_10 = p.cnt["qs"]
        TK1 = {}
        c = vec_10
        for i in range(8):
            c += 1; TK1["SUMr", i] = c
        c += 4; TK1["R0p"] = c
        c += 6; TK1["BIAp"] = c
        for i in range(8, 16):
            c += 1; TK1["SUMr", i] = c
            c += 1; TK1["evac", i - 8] = c
        c += 4; TK1["R0o"] = c
        c += 6; TK1["BIAo"] = c
        for j in range(8, 16):
            c += 1; TK1["evac", j] = c
        vec_P1e = c
        c = sca_10
        for i in range(8):
            c += 1; TK1["SQ", i] = c
        c += 1; TK1["sqp"] = c
        for i in range(8, 16):
            c += 1; TK1["SQ", i] = c
            c += 1; TK1["apply", i - 8] = c
        c += 1; TK1["sqo"] = c
        for j in range(8, 16):
            c += 1; TK1["apply", j] = c
        c = ten_10
        for j in range(NT):
            c += 1; TK1["tr", j] = c

        @blk.sync
        def _(sync):
            for i, r in enumerate(ORD1):
                if i >= 2:
                    p.w(sync, "vec", TK1["SUMr", i - 2])
                    p.w(sync, "sca", TK1["SQ", i - 2])
                p.inc(sync.dma_start(out=xbuf1[i % 2][:],
                                     in_=x_in[r * 128:(r + 1) * 128, :]),
                      _ldsem(i), 16)

        def issue_b(s, i):
            if i >= NT:
                return
            buf = xb2[i % 2] if i < 8 else xbuf1[i % 2]
            if i == 8:
                p.w(s, "vec", TK1["SUMr", 14])
                p.w(s, "sca", TK1["SQ", 14])
            if i == 9:
                p.w(s, "vec", TK1["SUMr", 15])
                p.w(s, "sca", TK1["SQ", 15])
            if i >= 2:
                p.w(s, "sca", TK1["apply", i - 2])
            r = ORD1[i]
            p.inc(s.dma_start(out=buf[:],
                              in_=x_in[r * 128:(r + 1) * 128, :]), "qs", 16)

        def scale_batch(v, lo, hi, kR):
            sl = slice(8 * lo // 8, hi)

            def cc(t_):
                return t_[:, lo:hi]

            v.tensor_scalar(out=cc(M0), in0=cc(SUM), scalar1=1.0 / D,
                            scalar2=None, op0=OP.mult)
            v.tensor_scalar(out=cc(E0), in0=cc(SQ), scalar1=1.0 / D,
                            scalar2=None, op0=OP.mult)
            p.chain(v.tensor_mul(cc(V0), cc(M0), cc(M0)), v, "vec")
            p.chain(v.tensor_sub(cc(V0), cc(E0), cc(V0)), v, "vec")
            p.chain(v.tensor_scalar(out=cc(T0), in0=cc(V0),
                                    scalar1=1.0 + RMS_EPS,
                                    scalar2=RMS_EPS * LN1_EPS,
                                    op0=OP.mult, op1=OP.add), v, "vec")
            p.inc(v.reciprocal(cc(R0), cc(T0)), "vec")
            assert p.cnt["vec"] == TK1[kR]

        def newton_batch(v, lo, hi, kB):
            def cc(t_):
                return t_[:, lo:hi]

            p.chain(v.tensor_mul(cc(N1), cc(S0), cc(S0)), v, "vec")
            p.chain(v.tensor_mul(cc(N1), cc(T0), cc(N1)), v, "vec")
            p.chain(v.tensor_scalar(out=cc(N1), in0=cc(N1), scalar1=-0.5,
                                    scalar2=1.5, op0=OP.mult, op1=OP.add),
                    v, "vec")
            p.chain(v.tensor_mul(cc(SCL), cc(S0), cc(N1)), v, "vec")
            p.chain(v.tensor_mul(cc(N2), cc(M0), cc(SCL)), v, "vec")
            p.inc(v.tensor_scalar(out=cc(BIA), in0=cc(N2), scalar1=-1.0,
                                  scalar2=None, op0=OP.mult), "vec")
            assert p.cnt["vec"] == TK1[kB]

        @blk.vector
        def _(v):
            for i in range(8):
                p.w(v, _ldsem(i), _ldval(i))
                inst = v.tensor_reduce(out=SUM[:, ORD1[i]:ORD1[i] + 1],
                                       in_=xbuf1[i % 2][:], axis=AX,
                                       op=OP.add)
                if i == 7:
                    p.chain(inst, v, "vec")
                else:
                    p.inc(inst, "vec")
                assert p.cnt["vec"] == TK1["SUMr", i]
            p.w(v, "sca", TK1["SQ", 7])
            scale_batch(v, 8, 16, "R0p")
            p.w(v, "sca", TK1["sqp"])
            newton_batch(v, 8, 16, "BIAp")
            for i in range(8, 16):
                p.w(v, _ldsem(i), _ldval(i))
                inst = v.tensor_reduce(out=SUM[:, ORD1[i]:ORD1[i] + 1],
                                       in_=xbuf1[i % 2][:], axis=AX,
                                       op=OP.add)
                if i == 15:
                    p.chain(inst, v, "vec")
                else:
                    p.inc(inst, "vec")
                assert p.cnt["vec"] == TK1["SUMr", i]
                k = i - 8
                p.w(v, "ten", TK1["tr", k])
                last = None
                rp = ORD1[k]
                for half in range(2):
                    last = v.tensor_copy(
                        out=xn_fm[:, rp // 8, half * 8:(half + 1) * 8,
                                  (rp % 8) * 128:(rp % 8 + 1) * 128],
                        in_=ps_tr[half][:].rearrange("p (c f) -> p c f", c=8))
                p.inc(last, "vec")
                assert p.cnt["vec"] == TK1["evac", k]
            p.w(v, "sca", TK1["SQ", 15])
            scale_batch(v, 0, 8, "R0o")
            p.w(v, "sca", TK1["sqo"])
            newton_batch(v, 0, 8, "BIAo")
            for j in range(8, 16):
                p.w(v, "ten", TK1["tr", j])
                last = None
                rp = ORD1[j]
                for half in range(2):
                    last = v.tensor_copy(
                        out=xn_fm[:, rp // 8, half * 8:(half + 1) * 8,
                                  (rp % 8) * 128:(rp % 8 + 1) * 128],
                        in_=ps_tr[half][:].rearrange("p (c f) -> p c f", c=8))
                p.inc(last, "vec")
                assert p.cnt["vec"] == TK1["evac", j]
            assert p.cnt["vec"] == vec_P1e

        @blk.scalar
        def _(s):
            issue_b(s, 0)
            issue_b(s, 1)
            for i in range(8):
                p.w(s, _ldsem(i), _ldval(i))
                p.sw(s, "sca")
                p.inc(s.activation(out=ost[:], in_=xbuf1[i % 2][:],
                                   func=AF.Square,
                                   accum_out=SQ[:, ORD1[i]:ORD1[i] + 1]),
                      "sca")
                assert p.cnt["sca"] == TK1["SQ", i]
            p.w(s, "vec", TK1["R0p"])
            p.inc(s.activation(out=S0[:, 8:16], in_=R0[:, 8:16],
                               func=AF.Sqrt), "sca")
            assert p.cnt["sca"] == TK1["sqp"]
            for i in range(8, 16):
                p.w(s, _ldsem(i), _ldval(i))
                p.sw(s, "sca")
                p.inc(s.activation(out=ost[:], in_=xbuf1[i % 2][:],
                                   func=AF.Square,
                                   accum_out=SQ[:, ORD1[i]:ORD1[i] + 1]),
                      "sca")
                assert p.cnt["sca"] == TK1["SQ", i]
                k = i - 8
                rp = ORD1[k]
                if k == 0:
                    p.w(s, "vec", TK1["BIAp"])
                p.w(s, "qs", qs_10 + 16 * (k + 1))
                if k >= 2:
                    p.w(s, "ten", TK1["tr", k - 2])
                p.inc(s.activation(out=stgP1[k % 2][:], in_=xb2[k % 2][:],
                                   func=AF.Identity,
                                   scale=SCL[:, rp:rp + 1],
                                   bias=BIA[:, rp:rp + 1]), "sca")
                assert p.cnt["sca"] == TK1["apply", k]
                issue_b(s, k + 2)
            p.w(s, "vec", TK1["R0o"])
            p.inc(s.activation(out=S0[:, 0:8], in_=R0[:, 0:8],
                               func=AF.Sqrt), "sca")
            assert p.cnt["sca"] == TK1["sqo"]
            for j in range(8, 16):
                rp = ORD1[j]
                if j == 8:
                    p.w(s, "vec", TK1["BIAo"])
                p.w(s, "qs", qs_10 + 16 * (j + 1))
                p.w(s, "ten", TK1["tr", j - 2])
                p.inc(s.activation(out=stgP1[j % 2][:], in_=xbuf1[j % 2][:],
                                   func=AF.Identity,
                                   scale=SCL[:, rp:rp + 1],
                                   bias=BIA[:, rp:rp + 1]), "sca")
                assert p.cnt["sca"] == TK1["apply", j]
                issue_b(s, j + 2)

        @blk.tensor
        def _(t):
            p.w(t, "gps", gps_p0)
            for j in range(NT):
                p.w(t, "sca", TK1["apply", j])
                if j >= 1:
                    p.w(t, "vec", TK1["evac", j - 1])
                last = None
                for dc in range(DT):
                    last = t.transpose(
                        ps_tr[dc // 8][:, (dc % 8) * 128:(dc % 8 + 1) * 128],
                        stgP1[j % 2][:, dc * 128:(dc + 1) * 128], ident[:])
                p.inc(last, "ten")
                assert p.cnt["ten"] == TK1["tr", j]

        if "xn_scr" in taps:
            ap = tap("xn_scr", [128, 2 * DT * NO], BF16)

            @blk.sync
            def _(sync):
                p.w(sync, "vec", vec_P1e)
                p.inc(sync.dma_start(
                    out=ap, in_=xn_fm[:].rearrange("p a b c -> p (a b c)")),
                    "qo", 16)

        def emit_proj_fm(wsrc, dst_fm, sp, act, dst_base=None, first=False):
            qga0, qgb0 = p.cnt["qga"], p.cnt["qgb"]
            sca_0, ten_0 = p.cnt["sca"], p.cnt["ten"]
            db = 1024 * sp if dst_base is None else dst_base

            def _wsem(grp):
                return "qga" if grp % 2 == 0 else "qgb"

            def _wval(grp):
                return (qga0 if grp % 2 == 0 else qgb0) + 16 * (grp // 2 + 1)

            @blk.gpsimd
            def _(g):
                for grp in range(8):
                    if grp >= 2:
                        p.w(g, "ten", ten_0 + 4 * (grp - 1))
                    elif not first:
                        p.w(g, "ten", ten_0 - (4 if grp == 0 else 0))
                    p.inc(g.dma_start(out=wbuf[grp % 2][:], in_=wsrc[grp]),
                          _wsem(grp), 16)

            @blk.tensor
            def _(t):
                for ot in range(DT):
                    grp, half = ot // 2, ot % 2
                    if half == 0:
                        p.w(t, _wsem(grp), _wval(grp))
                    for sl in range(2):
                        u = 2 * ot + sl
                        if u >= 2:
                            p.w(t, "sca", sca_0 + u - 1)
                        ps = ps_mm[u % 2]
                        last = None
                        for dc in range(DT):
                            last = t.matmul(ps[:],
                                            wbuf[grp % 2][:, half * D + dc * 128:
                                                          half * D + (dc + 1) * 128],
                                            xn_fm[:, sp, dc, sl * 512:
                                                  (sl + 1) * 512],
                                            start=(dc == 0), stop=(dc == DT - 1))
                        p.inc(last, "ten")

            @blk.scalar
            def _(s):
                for u in range(32):
                    p.w(s, "ten", ten_0 + u + 1)
                    ot, sl = u // 2, u % 2
                    p.inc(s.activation(out=dst_fm[:, ot, db + 512 * sl:
                                               db + 512 * (sl + 1)],
                                       in_=ps_mm[u % 2][:], func=act), "sca")

        def emit_proj_tm(wsrc, scr, sp, act, base_tile):
            qga0, qgb0 = p.cnt["qga"], p.cnt["qgb"]
            sca_0, ten_0, vec_0 = p.cnt["sca"], p.cnt["ten"], p.cnt["vec"]
            qp_0 = p.cnt["qp"]

            def _wsem(grp):
                return "qga" if grp % 2 == 0 else "qgb"

            def _wval(grp):
                return (qga0 if grp % 2 == 0 else qgb0) + 16 * (grp // 2 + 1)

            @blk.gpsimd
            def _(g):
                for grp in range(8):
                    if grp >= 2:
                        p.w(g, "ten", ten_0 + 8 * (grp - 1))
                    else:
                        p.w(g, "ten", ten_0 - (4 if grp == 0 else 0))
                    p.inc(g.dma_start(out=wbuf[grp % 2][:], in_=wsrc[grp]),
                          _wsem(grp), 16)

            @blk.tensor
            def _(t):
                for ot in range(DT):
                    grp, half = ot // 2, ot % 2
                    if half == 0:
                        p.w(t, _wsem(grp), _wval(grp))
                    for sl in range(2):
                        u = 2 * ot + sl
                        if u >= 2:
                            p.w(t, "sca", sca_0 + u - 1)
                        ps = ps_mm[u % 2]
                        last = None
                        for dc in range(DT):
                            last = t.matmul(ps[:],
                                            wbuf[grp % 2][:, half * D + dc * 128:
                                                          half * D + (dc + 1) * 128],
                                            xn_fm[:, sp, dc, sl * 512:
                                                  (sl + 1) * 512],
                                            start=(dc == 0), stop=(dc == DT - 1))
                        p.inc(last, "ten")
                        p.w(t, "sca", sca_0 + u + 1)
                        if u >= 2:
                            p.w(t, "vec", vec_0 + u - 1)
                        for q4 in range(4):
                            last = t.transpose(
                                ps_tr[u % 2][:, q4 * 128:(q4 + 1) * 128],
                                vstg[u % 2][:, q4 * 128:(q4 + 1) * 128],
                                ident[:])
                        p.inc(last, "ten")

            @blk.vector
            def _(v):
                for u in range(32):
                    p.w(v, "ten", ten_0 + 2 * u + 2)
                    if u < 2:
                        p.w(v, "qp", qp_0)
                    else:
                        p.w(v, "qp", qp_0 + 64 * (u - 1))
                    p.inc(v.tensor_copy(out=vevq[u % 2][:],
                                        in_=ps_tr[u % 2][:, 0:512]), "vec")

            @blk.scalar
            def _(s):
                for u in range(33):
                    if u < 32:
                        p.w(s, "ten", ten_0 + 2 * u + 1)
                        p.inc(s.activation(out=vstg[u % 2][:],
                                           in_=ps_mm[u % 2][:], func=act),
                              "sca")
                    if u >= 1:
                        up = u - 1
                        ot, sl = up // 2, up % 2
                        p.w(s, "vec", vec_0 + up + 1)
                        for q4 in range(4):
                            p.inc(s.dma_start(
                                out=scr[base_tile + 4 * sl + q4, ot // 4, :,
                                        (ot % 4) * 128:(ot % 4 + 1) * 128],
                                in_=vevq[up % 2][:, q4 * 128:(q4 + 1) * 128]),
                                "qp", 16)

        sp_marks = {}
        if phase_limit >= 2:
            for sp in (1, 0):
                @blk.tensor
                def _(t, sp=sp):
                    p.w(t, "vec", TK1["evac", 7 if sp == 1 else 15])

                emit_proj_fm(wk, k_fm, sp, AF.Silu, first=(sp == 1))
                if phase_limit >= 3:
                    emit_proj_tm(wv, v_scr, sp, AF.Silu, base_tile=8 * sp)
                if sp == 0 and phase_limit >= 4:
                    emit_proj_tm(wg, g_scr, 0, AF.Sigmoid, base_tile=0)
                if sp == 0 and phase_limit >= 5:
                    qp_now = p.cnt["qp"]

                    @blk.scalar
                    def _(s_, qp_now=qp_now):
                        p.w(s_, "qp", qp_now)
                    emit_proj_fm(wq, q_fm, 0, AF.Silu, dst_base=0)
                sp_marks[("ten_end", sp)] = p.cnt["ten"]

        sca_P2, qp_P2 = p.cnt["sca"], p.cnt["qp"]

        if "k_fm" in taps and phase_limit >= 2:
            ap = tap("k_fm", [128, DT * NB], BF16)

            @blk.sync
            def _(sync):
                p.w(sync, "sca", sca_P2)
                p.inc(sync.dma_start(
                    out=ap, in_=k_fm[:].rearrange("p a b -> p (a b)")), "qo", 16)
        if "q_fm" in taps and phase_limit >= 5:
            ap = tap("q_fm", [128, DT * NO], BF16)

            @blk.sync
            def _(sync):
                p.w(sync, "sca", sca_P2)
                p.inc(sync.dma_start(
                    out=ap, in_=q_fm[:].rearrange("p a b -> p (a b)")), "qo", 16)
        if "v_tm" in taps and phase_limit >= 3:
            ap = tap("v_tm", [NT * 4 * 128, 512], BF16)

            @blk.sync
            def _(sync):
                p.w(sync, "qp", qp_P2)
                p.inc(sync.dma_start(
                    out=ap, in_=v_scr.rearrange("a e b c -> (a e b) c")),
                    "qo", 16)
        if "g_tm" in taps and phase_limit >= 4:
            ap = tap("g_tm", [NOT_ * 4 * 128, 512], BF16)

            @blk.sync
            def _(sync):
                p.w(sync, "qp", qp_P2)
                p.inc(sync.dma_start(
                    out=ap, in_=g_scr.rearrange("a e b c -> (a e b) c")), "qo", 16)

        qp_og = p.cnt["qp"]
        p3m = {}
        ten_P2e = p.cnt["ten"]
        if phase_limit >= 6:
            for ib, jset in ((0, JSET_A), (1, JSET_B)):
                nj = len(jset)
                vec_mk0 = p.cnt["vec"]

                @blk.vector
                def _(v, ib=ib, jset=jset):
                    if ib == 0:
                        p.w(v, "qs", moff_v)
                        p.w(v, "gps", gps_p0)
                        p.w(v, "qp", qp_P2)
                    last = None
                    for idx in range(len(jset)):
                        c = idx if ib == 0 else 8 + idx
                        last = v.tensor_scalar(out=masks3[:, idx, :], in0=iota[:],
                                               scalar1=mofb[:, c:c + 1],
                                               scalar2=0.0,
                                               op0=OP.add, op1=OP.is_ge)
                    p.inc(last, "vec")
                vec_mk = p.cnt["vec"]

                ten_s0, vec_s0 = p.cnt["ten"], p.cnt["vec"]

                @blk.tensor
                def _(t, ib=ib, jset=jset, ten_s0=ten_s0, vec_s0=vec_s0,
                      vec_mk=vec_mk):
                    p.w(t, "vec", vec_mk)
                    if ib == 0:
                        p.w(t, "sca", sca_P2)
                    for idx, jt in enumerate(jset):
                        if idx >= 2:
                            p.w(t, "vec", vec_s0 + idx - 1)
                        ps = ps_mm[idx % 2]
                        last = None
                        for dc in range(DT):
                            last = t.matmul(ps[:],
                                            k_fm[:, dc, jt * 128:(jt + 1) * 128],
                                            q_fm[:, dc, ib * 512:(ib + 1) * 512],
                                            start=(dc == 0), stop=(dc == DT - 1))
                        p.inc(last, "ten")

                if ib == 1:
                    p3m["kdone"] = p.cnt["ten"]

                @blk.vector
                def _(v, jset=jset, ten_s0=ten_s0):
                    for idx in range(len(jset)):
                        p.w(v, "ten", ten_s0 + idx + 1)
                        p.inc(v.tensor_tensor(out=stq[:, idx, :],
                                              in0=ps_mm[idx % 2][:],
                                              in1=masks3[:, idx, :],
                                              op=OP.mult), "vec")
                vec_s = p.cnt["vec"]

                qs_g0 = p.cnt["qs"]

                vec_now = p.cnt["vec"]
                ten_v0, vec_v0 = p.cnt["ten"], p.cnt["vec"]
                qsa_v0, qsb_v0 = p.cnt["qsa"], p.cnt["qsb"]
                qp_o0 = p.cnt["qp"]

                @blk.sync
                def _(sync, ib=ib, jset=jset, vec_v0=vec_v0):
                    nj = len(jset)
                    if ib == 0:
                        p.w(sync, "qp", qp_P2)
                        p.w(sync, "sca", sca_P2)
                    for eh in range(4):
                        if eh >= 2:
                            p.w(sync, "vec", vec_v0 + 4 * (eh - 1))
                        elif ib == 1:
                            p.w(sync, "ten", p3m["sv0_eh2" if eh == 0
                                                else "sv0_end"])
                        for idx, jt in enumerate(jset):
                            buf = vq if eh % 2 == 0 else vq2
                            p.inc(sync.dma_start(
                                out=buf[:, idx, :],
                                in_=v_scr[jt, eh]),
                                "qsa" if eh % 2 == 0 else "qsb", 16)

                @blk.gpsimd
                def _(g, ib=ib, vec_now=vec_now):
                    if ib == 0:
                        p.w(g, "qp", qp_P2)
                        p.w(g, "ten", ten_P2e)
                    else:
                        p.w(g, "vec", vec_now)
                    for it in range(4):
                        for eh in range(4):
                            p.inc(g.dma_start(
                                out=gtile[:, it, eh * 512:(eh + 1) * 512],
                                in_=g_scr[4 * ib + it, eh]), "qg2", 16)
                qs_g = p.cnt["qg2"]

                @blk.tensor
                def _(t, jset=jset, ten_v0=ten_v0, vec_s=vec_s,
                      qsa_v0=qsa_v0, qsb_v0=qsb_v0, vec_v0=vec_v0):
                    nj = len(jset)
                    p.w(t, "vec", vec_s)
                    for eh in range(4):
                        p.w(t, "qsa" if eh % 2 == 0 else "qsb",
                            (qsa_v0 if eh % 2 == 0 else qsb_v0)
                            + 16 * nj * (eh // 2 + 1))
                        buf = vq if eh % 2 == 0 else vq2
                        for it in range(4):
                            u = 4 * eh + it
                            if u >= 2:
                                p.w(t, "vec", vec_v0 + u - 1)
                            ps = ps_mm[2 + u % 2]
                            last = None
                            for idx in range(nj):
                                last = t.matmul(ps[:],
                                                stq[:, idx, it * 128:(it + 1) * 128],
                                                buf[:, idx, :],
                                                start=(idx == 0),
                                                stop=(idx == nj - 1))
                            p.inc(last, "ten")

                if ib == 0:
                    p3m["sv0_eh2"] = ten_v0 + 12
                    p3m["sv0_end"] = ten_v0 + 16

                dbg_ap = None
                if ib == 0 and "dbgo" in taps:
                    dbg_ap = tap("dbgo", [128, 3 * 8192], F32)
                    dbg_bf = tap("dbgb", [128, 2 * 8192], BF16)

                @blk.vector
                def _(v, ib=ib, ten_v0=ten_v0, qs_g=qs_g, qp_o0=qp_o0,
                      jset=jset):
                    p.w(v, "qg2", qs_g)
                    for eh in range(4):
                        for it in range(4):
                            u = 4 * eh + it
                            p.w(v, "ten", ten_v0 + u + 1)
                            if u >= 2:
                                p.w(v, "qp", qp_o0 + 16 * u)
                            else:
                                p.w(v, "qp", qp_o0)
                            if ib == 0 and eh == 3 and it == 0 and dbg_ap is not None:
                                v.tensor_copy(out=ost[:, 0:512],
                                              in_=ps_mm[2 + u % 2][:])
                            p.inc(v.tensor_tensor(
                                out=ogst[u % 2][:],
                                in0=ps_mm[2 + u % 2][:],
                                in1=gtile[:, it, eh * 512:(eh + 1) * 512],
                                op=OP.mult), "vec")

                @blk.scalar
                def _(s, ib=ib, vec_v0=vec_v0):
                    for eh in range(4):
                        for it in range(4):
                            u = 4 * eh + it
                            p.w(s, "vec", vec_v0 + u + 1)
                            p.inc(s.dma_start(
                                out=og_scr[4 * ib + it, :, eh * 512:(eh + 1) * 512],
                                in_=ogst[u % 2][:]), "qp", 16)

                if dbg_ap is not None and ib == 0:
                    @blk.sync
                    def _(sync, vec_v0=vec_v0):
                        p.w(sync, "vec", vec_v0 + 16)
                        p.inc(sync.dma_start(out=dbg_ap[:, 0:512],
                                             in_=ost[:, 0:512]), "qo", 16)
                        p.inc(sync.dma_start(
                            out=dbg_bf[:, 0:8 * 512],
                            in_=stq[:, 0:8, :].rearrange("p a b -> p (a b)")),
                            "qo", 16)
                        p.inc(sync.dma_start(
                            out=dbg_bf[:, 8192:8192 + 8 * 512],
                            in_=vq2[:, 0:8, :].rearrange("p a b -> p (a b)")),
                            "qo", 16)
            qp_og = p.cnt["qp"]

            if "og" in taps:
                ap = tap("og", [NOT_ * 128, D], BF16)

                @blk.sync
                def _(sync):
                    p.w(sync, "qp", qp_og)
                    p.inc(sync.dma_start(
                        out=ap, in_=og_scr.rearrange("a b c -> (a b) c")), "qo", 16)

        vec_xq = p.cnt["vec"]
        qp_x1 = p.cnt["qp"]
        if phase_limit >= 7:
            ten_a0, vec_a0 = p.cnt["ten"], p.cnt["vec"]
            vec_P3e = p.cnt["vec"]
            ten_P3e = p.cnt["ten"]
            qga_w0, qgb_w0 = p.cnt["qga"], p.cnt["qgb"]
            qsa_a0, qsb_a0 = p.cnt["qsa"], p.cnt["qsb"]

            @blk.gpsimd
            def _(g):
                for dc in range(DT):
                    if dc < 8:
                        p.w(g, "ten", p3m.get("kdone", ten_P3e))
                    else:
                        p.w(g, "ten", ten_P3e)
                        p.w(g, "vec", vec_P3e)
                    p.inc(g.dma_start(out=wpb[dc][:], in_=wp[dc]),
                          "qga" if dc % 2 == 0 else "qgb", 16)

            @blk.sync
            def _(sync):
                p.w(sync, "qp", qp_og)
                for it in range(NOT_):
                    if it >= 2:
                        p.w(sync, "ten", ten_a0 + (it - 1))
                    p.inc(sync.dma_start(out=stg4[it % 2][:],
                                         in_=og_scr[it]), _ldsem(it), 16)

            @blk.tensor
            def _(t):
                for it in range(NOT_):
                    p.w(t, _ldsem(it), _ldval(it, qsa_a0, qsb_a0))
                    if it >= 1:
                        p.w(t, "vec", vec_a0 + it)
                    last = None
                    for ec in range(DT):
                        last = t.transpose(
                            ps_tr[ec // 8][:, (ec % 8) * 128:(ec % 8 + 1) * 128],
                            stg4[it % 2][:, ec * 128:(ec + 1) * 128], ident[:])
                    p.inc(last, "ten")

            @blk.vector
            def _(v):
                for it in range(NOT_):
                    p.w(v, "ten", ten_a0 + it + 1)
                    last = None
                    for half in range(2):
                        last = v.tensor_copy(
                            out=og_fm4[:, half * 8:(half + 1) * 8,
                                       it * 128:(it + 1) * 128],
                            in_=ps_tr[half][:].rearrange("p (c f) -> p c f", c=8))
                    p.inc(last, "vec")
            vec_ogfm = p.cnt["vec"]
            ten_a1 = p.cnt["ten"]

            ten_b0, vec_b0, sca_b0 = p.cnt["ten"], p.cnt["vec"], p.cnt["sca"]
            qsa_b0, qsb_b0 = p.cnt["qsa"], p.cnt["qsb"]
            qp_b0 = p.cnt["qp"]

            TK4 = {}
            c = vec_b0
            for k in range(NOT_ + 1):
                if k < NOT_:
                    for ch in range(4):
                        c += 1; TK4["va", k, ch] = c
                    c += 1; TK4["SUMr", k] = c
                if k >= 1:
                    c += 1; TK4["evQ", k - 1] = c
                if k < NOT_:
                    c += 4; TK4["R0", k] = c
                    c += 6; TK4["BIA", k] = c
                    c += 3; TK4["RX", k] = c
                    c += 1; TK4["r2b", k] = c
            vec_b_end = c
            c = sca_b0
            for k in range(NOT_):
                c += 1; TK4["SQ", k] = c
                c += 1; TK4["sqrt", k] = c
                c += 1; TK4["apply", k] = c
                c += 2; TK4["r2a", k] = c
            c = ten_b0
            for k in range(NOT_ + 1):
                if k < NOT_:
                    for ch in range(4):
                        c += 1; TK4["mmB", k, ch] = c
                if k >= 1:
                    c += 1; TK4["trQ", k - 1] = c
            ten_b_end = c

            @blk.sync
            def _(sync):
                p.w(sync, "ten", ten_P3e)
                for k in range(NOT_):
                    if k >= 2:
                        p.w(sync, "vec", TK4["va", k - 2, 3])
                    p.inc(sync.dma_start(out=x4buf[k % 2][:],
                                         in_=x_in[k * 128:(k + 1) * 128, :]),
                          _ldsem(k), 16)

            @blk.tensor
            def _(t):
                p.w(t, "vec", vec_ogfm)
                for k in range(NOT_ + 1):
                    if k < NOT_:
                        for ch in range(4):
                            if k >= 1:
                                p.w(t, "vec", TK4["va", k - 1, ch])
                            last = None
                            for dc in range(DT):
                                if k == 0 and ch == 0:
                                    p.w(t, "qga" if dc % 2 == 0 else "qgb",
                                        (qga_w0 if dc % 2 == 0 else qgb_w0)
                                        + 16 * (dc // 2 + 1))
                                last = t.matmul(
                                    ps_mm[ch][:],
                                    og_fm4[:, dc, k * 128:(k + 1) * 128],
                                    wpb[dc][:, ch * 512:(ch + 1) * 512],
                                    start=(dc == 0), stop=(dc == DT - 1))
                            p.inc(last, "ten")
                            assert p.cnt["ten"] == TK4["mmB", k, ch]
                    if k >= 1:
                        p.w(t, "vec", TK4["r2b", k - 1])
                        if k >= 2:
                            p.w(t, "vec", TK4["evQ", k - 2])
                        last = None
                        for ec in range(DT):
                            last = t.transpose(
                                ps_tr[ec // 8][:, (ec % 8) * 128:
                                               (ec % 8 + 1) * 128],
                                stg4[(k - 1) % 2][:, ec * 128:(ec + 1) * 128],
                                ident[:])
                        p.inc(last, "ten")
                        assert p.cnt["ten"] == TK4["trQ", k - 1]

            @blk.vector
            def _(v):
                v.memset(mgb[:, 0:1], MAGIC)
                v.memset(mgb[:, 1:2], -MAGIC)
                for k in range(NOT_ + 1):
                    if k < NOT_:
                        p.w(v, _ldsem(k), _ldval(k, qsa_b0, qsb_b0))
                        if k >= 2:
                            p.w(v, "qp", qp_b0 + 16 * (k - 1))
                        for ch in range(4):
                            p.w(v, "ten", TK4["mmB", k, ch])
                            inst = v.tensor_tensor(
                                out=x1t4[k % 2][:, ch * 512:(ch + 1) * 512],
                                in0=ps_mm[ch][:],
                                in1=x4buf[k % 2][:, ch * 512:(ch + 1) * 512],
                                op=OP.add)
                            if ch < 3:
                                p.inc(inst, "vec")
                            else:
                                p.chain(inst, v, "vec")
                            assert p.cnt["vec"] == TK4["va", k, ch]
                        p.inc(v.tensor_reduce(out=SUM[:, k:k + 1],
                                              in_=x1t4[k % 2][:], axis=AX,
                                              op=OP.add), "vec")
                        assert p.cnt["vec"] == TK4["SUMr", k]
                    if k >= 1:
                        p.w(v, "ten", TK4["trQ", k - 1])
                        last = None
                        for half in range(2):
                            last = v.tensor_copy(
                                out=xq_fm[:, half * 8:(half + 1) * 8,
                                          (k - 1) * 128:k * 128],
                                in_=ps_tr[half][:].rearrange(
                                    "p (c f) -> p c f", c=8))
                        p.inc(last, "vec")
                        assert p.cnt["vec"] == TK4["evQ", k - 1]
                    if k < NOT_:
                        p.w(v, "sca", TK4["SQ", k])
                        v.tensor_scalar(out=M0[:, k:k + 1], in0=SUM[:, k:k + 1],
                                        scalar1=1.0 / D, scalar2=None,
                                        op0=OP.mult)
                        v.tensor_scalar(out=E0[:, k:k + 1], in0=SQ[:, k:k + 1],
                                        scalar1=1.0 / D, scalar2=None,
                                        op0=OP.mult)
                        p.chain(v.tensor_mul(V0[:, k:k + 1], M0[:, k:k + 1],
                                             M0[:, k:k + 1]), v, "vec")
                        p.chain(v.tensor_sub(V0[:, k:k + 1], E0[:, k:k + 1],
                                             V0[:, k:k + 1]), v, "vec")
                        p.chain(v.tensor_scalar(out=T0[:, k:k + 1],
                                                in0=V0[:, k:k + 1],
                                                scalar1=1.0, scalar2=LN1_EPS,
                                                op0=OP.mult, op1=OP.add),
                                v, "vec")
                        p.inc(v.reciprocal(R0[:, k:k + 1], T0[:, k:k + 1]),
                              "vec")
                        assert p.cnt["vec"] == TK4["R0", k]
                        p.w(v, "sca", TK4["sqrt", k])
                        p.chain(v.tensor_mul(N1[:, k:k + 1], S0[:, k:k + 1],
                                             S0[:, k:k + 1]), v, "vec")
                        p.chain(v.tensor_mul(N1[:, k:k + 1], T0[:, k:k + 1],
                                             N1[:, k:k + 1]), v, "vec")
                        p.chain(v.tensor_scalar(out=N1[:, k:k + 1],
                                                in0=N1[:, k:k + 1],
                                                scalar1=-0.5, scalar2=1.5,
                                                op0=OP.mult, op1=OP.add),
                                v, "vec")
                        p.chain(v.tensor_mul(SCL[:, k:k + 1], S0[:, k:k + 1],
                                             N1[:, k:k + 1]), v, "vec")
                        p.chain(v.tensor_mul(N2[:, k:k + 1], M0[:, k:k + 1],
                                             SCL[:, k:k + 1]), v, "vec")
                        p.inc(v.tensor_scalar(out=BIA[:, k:k + 1],
                                              in0=N2[:, k:k + 1], scalar1=-1.0,
                                              scalar2=None, op0=OP.mult),
                              "vec")
                        assert p.cnt["vec"] == TK4["BIA", k]
                        p.w(v, "sca", TK4["apply", k])
                        p.chain(v.tensor_reduce(out=AM[:, k:k + 1], in_=osu[:],
                                                axis=AX, op=OP.max,
                                                apply_absolute_value=True),
                                v, "vec")
                        p.chain(v.tensor_scalar(out=SX[:, k:k + 1],
                                                in0=AM[:, k:k + 1],
                                                scalar1=RQMAX, scalar2=QEPS,
                                                op0=OP.mult, op1=OP.max),
                                v, "vec")
                        p.inc(v.reciprocal(RX[:, k:k + 1], SX[:, k:k + 1]),
                              "vec")
                        assert p.cnt["vec"] == TK4["RX", k]
                        p.w(v, "sca", TK4["r2a", k])
                        if k >= 2:
                            p.w(v, "ten", TK4["trQ", k - 2])
                        else:
                            p.w(v, "ten", ten_a1)
                        p.inc(v.tensor_scalar(out=stg4[k % 2][:], in0=osu[:],
                                              scalar1=SX[:, k:k + 1],
                                              scalar2=None, op0=OP.mult),
                              "vec")
                        assert p.cnt["vec"] == TK4["r2b", k]
                assert p.cnt["vec"] == vec_b_end

            @blk.scalar
            def _(s):
                for k in range(NOT_):
                    p.w(s, "vec", TK4["va", k, 3])
                    p.inc(s.activation(out=ost[:], in_=x1t4[k % 2][:],
                                       func=AF.Square,
                                       accum_out=SQ[:, k:k + 1]), "sca")
                    assert p.cnt["sca"] == TK4["SQ", k]
                    p.w(s, "vec", TK4["R0", k])
                    p.inc(s.activation(out=S0[:, k:k + 1], in_=R0[:, k:k + 1],
                                       func=AF.Sqrt), "sca")
                    assert p.cnt["sca"] == TK4["sqrt", k]
                    p.w(s, "vec", TK4["BIA", k])
                    if k >= 1:
                        p.w(s, "vec", TK4["r2b", k - 1])
                    p.inc(s.activation(out=osu[:], in_=x1t4[k % 2][:],
                                       func=AF.Identity, scale=SCL[:, k:k + 1],
                                       bias=BIA[:, k:k + 1]), "sca")
                    assert p.cnt["sca"] == TK4["apply", k]
                    p.w(s, "vec", TK4["RX", k])
                    p.chain(s.activation(out=ost[:], in_=osu[:],
                                         func=AF.Identity,
                                         scale=RX[:, k:k + 1],
                                         bias=mgb[:, 0:1]), s, "sca")
                    p.inc(s.activation(out=osu[:], in_=ost[:],
                                       func=AF.Identity, bias=mgb[:, 1:2]),
                          "sca")
                    assert p.cnt["sca"] == TK4["r2a", k]
                    p.inc(s.dma_start(out=x1_scr[k], in_=x1t4[k % 2][:]),
                          "qp", 16)
            qp_x1 = p.cnt["qp"]

            if "x1" in taps:
                ap = tap("x1", [NOT_ * 128, D], F32)

                @blk.sync
                def _(sync):
                    p.w(sync, "qp", qp_x1)
                    p.inc(sync.dma_start(
                        out=ap, in_=x1_scr.rearrange("a b c -> (a b) c")), "qo", 16)

            assert p.cnt["ten"] == ten_b_end
            vec_xq = p.cnt["vec"]

            if "xq" in taps:
                ap = tap("xq", [128, DT * NO], BF16)

                @blk.sync
                def _(sync):
                    p.w(sync, "vec", vec_xq)
                    p.inc(sync.dma_start(
                        out=ap, in_=xq_fm[:].rearrange("p a b -> p (a b)")), "qo", 16)

        vec_thq = p.cnt["vec"]
        if phase_limit >= 8:
            qga_f0, qgb_f0 = p.cnt["qga"], p.cnt["qgb"]
            ten_f0, vec_f0, sca_f0 = p.cnt["ten"], p.cnt["vec"], p.cnt["sca"]

            def _p_wsem(ppi):
                return "qga" if ppi % 2 == 0 else "qgb"

            def _p_wval(ppi):
                return (qga_f0 if ppi % 2 == 0 else qgb_f0) + 16 * (ppi // 2 + 1)

            def dslot(ppi, which):
                off = (ppi % 2) * 256 + which * 128
                return rrowb[:, off:off + 128]

            def WBG(i):
                return wscb[:, 2 * (i % 8):2 * (i % 8) + 1]

            def WBU(i):
                return wscb[:, 2 * (i % 8) + 1:2 * (i % 8) + 2]

            def WSGc(i):
                return wsc2[:, 2 * (i % 8):2 * (i % 8) + 1]

            def WSUc(i):
                return wsc2[:, 2 * (i % 8) + 1:2 * (i % 8) + 2]

            def t1q(ppi, which):
                return t1qb[2 * (ppi % 2) + which]

            TK = {}
            c = sca_f0
            for it_ in range(IT + 1):
                if it_ < IT:
                    c += 1; TK["gAa2", it_] = c
                if it_ >= 1:
                    c += 1; TK["silu", it_ - 1] = c
                if it_ < IT:
                    c += 1; TK["gBa2", it_] = c
                    c += 1; TK["gAa3", it_] = c
                    c += 1; TK["gBa3", it_] = c
                    c += 1; TK["uAa2", it_] = c
                    c += 1; TK["uBa2", it_] = c
                    c += 1; TK["uAa3", it_] = c
                    c += 1; TK["uBa3", it_] = c
            sca_P5e = c
            c = vec_f0
            for it_ in range(IT + 2):
                if it_ < IT:
                    c += 1; TK["Ag", it_] = c
                    c += 1; TK["Au", it_] = c
                if it_ >= 2:
                    c += 1; TK["E", it_ - 2] = c
                    c += 1; TK["F1", it_ - 2] = c
                    c += 1; TK["F2", it_ - 2] = c
            vec_P5e = c
            c = ten_f0
            for it_ in range(IT + 1):
                if it_ < IT:
                    c += 1; TK["trgA", it_] = c
                if it_ >= 1:
                    c += 1; TK["mmg", it_ - 1] = c
                if it_ < IT:
                    c += 1; TK["trgB", it_] = c
                if it_ >= 1:
                    c += 1; TK["mmu", it_ - 1] = c
                if it_ < IT:
                    c += 1; TK["truA", it_] = c
                    c += 1; TK["truB", it_] = c
            ten_P5e = c

            @blk.gpsimd
            def _(g):
                for j in range(IT):
                    if j >= 2:
                        p.w(g, "ten", TK["truB", j - 2])
                        p.w(g, "vec", TK["Au", j - 2])
                    else:
                        p.w(g, "ten", ten_f0)
                        p.w(g, "vec", vec_xq)
                    p.inc(g.dma_start(out=wbuf[j % 2][:], in_=w1[j]),
                          _p_wsem(j), 16)

            @blk.vector
            def _(v):
                p.w(v, "vec", vec_xq)
                v.memset(hacc[:], 0.0)
                for it_ in range(IT + 2):
                    if it_ < IT:
                        j = it_
                        p.w(v, _p_wsem(j), _p_wval(j))
                        if j >= 8:
                            p.w(v, "sca", TK["silu", j - 8])
                        wbg = wbuf[j % 2][:, 0:D]
                        wbu = wbuf[j % 2][:, D:2 * D]
                        v.tensor_reduce(out=TMPA[:, 0:1], in_=wbg, axis=AX,
                                        op=OP.max, apply_absolute_value=True)
                        v.tensor_reduce(out=TMPA[:, 1:2], in_=wbu, axis=AX,
                                        op=OP.max, apply_absolute_value=True)
                        v.tensor_scalar(out=WSG(j), in0=TMPA[:, 0:1],
                                        scalar1=RQMAX, scalar2=QEPS,
                                        op0=OP.mult, op1=OP.max)
                        v.tensor_scalar(out=WSU(j), in0=TMPA[:, 1:2],
                                        scalar1=RQMAX, scalar2=QEPS,
                                        op0=OP.mult, op1=OP.max)
                        v.reciprocal(WRG(j), WSG(j))
                        v.reciprocal(WRU(j), WSU(j))
                        v.tensor_scalar(out=WBG(j), in0=WRG(j), scalar1=1.0,
                                        scalar2=None, op0=OP.mult)
                        v.tensor_scalar(out=WBU(j), in0=WRU(j), scalar1=1.0,
                                        scalar2=None, op0=OP.mult)
                        v.reciprocal(WSGc(j), WBG(j))
                        v.reciprocal(WSUc(j), WBU(j))
                        p.inc(v.tensor_scalar(out=dslot(j, 0), in0=ident[:],
                                              scalar1=WRG(j), scalar2=None,
                                              op0=OP.mult), "vec")
                        p.inc(v.tensor_scalar(out=dslot(j, 1), in0=ident[:],
                                              scalar1=WRU(j), scalar2=None,
                                              op0=OP.mult), "vec")
                        assert p.cnt["vec"] == TK["Au", j]
                    if it_ >= 2:
                        j = it_ - 2
                        p.w(v, "ten", TK["mmu", j])
                        p.w(v, "sca", TK["silu", j])
                        last = None
                        for half in range(2):
                            last = v.scalar_tensor_tensor(
                                out=h_buf[:, j, half * 512:(half + 1) * 512],
                                in0=ps_mm[2 + half][:], scalar=WSUc(j),
                                in1=gsil[:, half * 512:(half + 1) * 512],
                                op0=OP.mult, op1=OP.mult)
                        p.chain(last, v, "vec")
                        assert p.cnt["vec"] == TK["E", j]
                        p.chain(v.tensor_max(hacc[:], hacc[:], h_buf[:, j, :]),
                                v, "vec")
                        p.inc(v.scalar_tensor_tensor(out=hacc[:],
                                                     in0=h_buf[:, j, :],
                                                     scalar=-1.0, in1=hacc[:],
                                                     op0=OP.mult, op1=OP.max),
                              "vec")

            def emit_tr(t, j, which, hB):
                wbw = wbuf[j % 2][:, which * D:(which + 1) * D]
                last = None
                for cc in range(8):
                    hc = 8 * hB + cc
                    last = t.matmul(
                        ps_ex[cc // 4][:, (cc % 4) * 128:(cc % 4 + 1) * 128],
                        wbw[:, hc * 128:(hc + 1) * 128],
                        dslot(j, which), start=True, stop=True)
                p.inc(last, "ten")

            def emit_mm(t, j, which):
                last = None
                for half in range(2):
                    ps = ps_mm[2 * which + half]
                    for hc in range(DT):
                        last = t.matmul(
                            ps[:],
                            t1q(j, which)[:, hc * 128:(hc + 1) * 128],
                            xq_fm[:, hc, half * 512:(half + 1) * 512],
                            start=(hc == 0), stop=(hc == DT - 1))
                p.inc(last, "ten")

            @blk.tensor
            def _(t):
                for it_ in range(IT + 1):
                    if it_ < IT:
                        if it_ >= 1:
                            p.w(t, "sca", TK["uBa2", it_ - 1])
                        p.w(t, "vec", TK["Au", it_])
                        emit_tr(t, it_, 0, 0)
                        assert p.cnt["ten"] == TK["trgA", it_]
                    if it_ >= 1:
                        p.w(t, "sca", TK["gBa3", it_ - 1])
                        emit_mm(t, it_ - 1, 0)
                        assert p.cnt["ten"] == TK["mmg", it_ - 1]
                    if it_ < IT:
                        p.w(t, "sca", TK["gAa2", it_])
                        emit_tr(t, it_, 0, 1)
                        assert p.cnt["ten"] == TK["trgB", it_]
                    if it_ >= 1:
                        p.w(t, "sca", TK["uBa3", it_ - 1])
                        if it_ >= 2:
                            p.w(t, "vec", TK["E", it_ - 2])
                        emit_mm(t, it_ - 1, 1)
                        assert p.cnt["ten"] == TK["mmu", it_ - 1]
                    if it_ < IT:
                        p.w(t, "sca", TK["gBa2", it_])
                        emit_tr(t, it_, 1, 0)
                        assert p.cnt["ten"] == TK["truA", it_]
                        p.w(t, "sca", TK["uAa2", it_])
                        emit_tr(t, it_, 1, 1)
                        assert p.cnt["ten"] == TK["truB", it_]

            def emit_r12(s, hB, k2):
                base = 1024 * hB
                s.activation(out=uwb[:, base:base + 512],
                             in_=ps_ex[0][:], func=AF.Identity,
                             bias=mgb[:, 0:1])
                p.inc(s.activation(out=uwb[:, base + 512:base + 1024],
                                   in_=ps_ex[1][:], func=AF.Identity,
                                   bias=mgb[:, 0:1]), "sca")
                assert p.cnt["sca"] == k2

            def emit_r3(s, j, which, hB, k3):
                base = 1024 * hB
                p.inc(s.activation(out=t1q(j, which)[:, base:base + 1024],
                                   in_=uwb[:, base:base + 1024],
                                   func=AF.Identity, bias=mgb[:, 1:2]), "sca")
                assert p.cnt["sca"] == k3

            @blk.scalar
            def _(s):
                for it_ in range(IT + 1):
                    if it_ < IT:
                        p.w(s, "ten", TK["trgA", it_])
                        emit_r12(s, 0, TK["gAa2", it_])
                    if it_ >= 1:
                        j = it_ - 1
                        p.w(s, "ten", TK["mmg", j])
                        if j >= 1:
                            p.w(s, "vec", TK["E", j - 1])
                        last = None
                        for half in range(2):
                            last = s.activation(
                                out=gsil[:, half * 512:(half + 1) * 512],
                                in_=ps_mm[half][:], func=AF.Silu,
                                scale=WSGc(j))
                        p.inc(last, "sca")
                        assert p.cnt["sca"] == TK["silu", j]
                    if it_ < IT:
                        p.w(s, "ten", TK["trgB", it_])
                        emit_r12(s, 1, TK["gBa2", it_])
                        emit_r3(s, it_, 0, 0, TK["gAa3", it_])
                        emit_r3(s, it_, 0, 1, TK["gBa3", it_])
                        p.w(s, "ten", TK["truA", it_])
                        emit_r12(s, 0, TK["uAa2", it_])
                        p.w(s, "ten", TK["truB", it_])
                        emit_r12(s, 1, TK["uBa2", it_])
                        emit_r3(s, it_, 1, 0, TK["uAa3", it_])
                        emit_r3(s, it_, 1, 1, TK["uBa3", it_])

            assert p.cnt["vec"] == vec_P5e and p.cnt["ten"] == ten_P5e \
                and p.cnt["sca"] == sca_P5e
            vec_P5 = p.cnt["vec"]
            ten_P5 = p.cnt["ten"]

            @blk.tensor
            def _(t):
                p.w(t, "vec", vec_P5)
                last = None
                for bk in range(8):
                    last = t.transpose(ps_tr[0][:, bk * 128:(bk + 1) * 128],
                                       hacc[:, bk * 128:(bk + 1) * 128], ident[:])
                p.inc(last, "ten")
            ten_sh1 = p.cnt["ten"]

            @blk.vector
            def _(v):
                p.w(v, "ten", ten_sh1)
                last = None
                for bk in range(8):
                    last = v.tensor_reduce(out=SH[:, bk:bk + 1],
                                           in_=ps_tr[0][:, bk * 128:(bk + 1) * 128],
                                           axis=AX, op=OP.max)
                p.chain(last, v, "vec")
                p.chain(v.tensor_scalar(out=SHS[:, 0:8], in0=SH[:, 0:8],
                                        scalar1=RQMAX, scalar2=QEPS,
                                        op0=OP.mult, op1=OP.max), v, "vec")
                p.inc(v.reciprocal(RH[:, 0:8], SHS[:, 0:8]), "vec")
            vec_sh1 = p.cnt["vec"]

            @blk.vector
            def _(v):
                p.w(v, "vec", vec_sh1)
                p.chain(v.memset(rrowb[:, 0:128], 0.0), v, "vec")
                p.inc(v.tensor_copy(out=rrowb[:, 0:8], in_=RH[:, 0:8]), "vec")
            vec_rr1 = p.cnt["vec"]

            @blk.tensor
            def _(t):
                p.w(t, "vec", vec_rr1)
                p.inc(t.transpose(ps_tr[0][:, 0:128], rrowb[:, 0:128],
                                  ident[:]), "ten")
            ten_rr = p.cnt["ten"]

            @blk.vector
            def _(v):
                p.w(v, "ten", ten_rr)
                p.inc(v.tensor_copy(out=stg[0:8, 0:128],
                                    in_=ps_tr[0][0:8, 0:128]), "vec")
            vec_rr2 = p.cnt["vec"]

            @blk.sync
            def _(sync):
                p.w(sync, "vec", vec_rr2)
                p.inc(sync.dma_start(out=rrowb[0:1, 0:1024],
                                     in_=stg[0:8, 0:128]), "qp", 16)
            qp_rr = p.cnt["qp"]

            @blk.tensor
            def _(t):
                p.w(t, "qp", qp_rr)
                last = None
                for hf in range(2):
                    last = t.matmul(ps_mm[hf][:], ones1[0:1, 0:128],
                                    rrowb[0:1, hf * 512:(hf + 1) * 512],
                                    start=True, stop=True)
                p.inc(last, "ten")
            ten_rbc = p.cnt["ten"]

            vec_rbc0 = p.cnt["vec"] + 1
            qga_d0, qgb_d0 = p.cnt["qga"], p.cnt["qgb"]
            ten_d0, sca_d0 = p.cnt["ten"], p.cnt["sca"]
            GRP_SZ = [8, 8, 8, 8, 8, 4]
            QN = [0, 1, 2, 2, 3, 3]
            Q4 = I_FFN // 4

            TK6 = {}
            c = vec_rbc0
            c += 4; TK6["A", 0] = c
            for j in range(IT):
                c += 1; TK6["thq", j] = c
            for ht in range(DT):
                if ht + 1 < DT:
                    c += 4; TK6["A", ht + 1] = c
                for gg in range(6):
                    c += 1; TK6["evac", ht, gg] = c
            vec_P6e = c
            c = sca_d0
            for ht in range(DT):
                for q in range(4):
                    c += 1; TK6["B", ht, q] = c
                if ht >= 1:
                    c += 1; TK6["dfm", ht - 1] = c
            c += 1; TK6["dfm", DT - 1] = c
            sca_P6e = c
            c = ten_d0
            for ht in range(DT):
                for gg in range(6):
                    c += 1; TK6["tr", ht, gg] = c
                c += 1; TK6["mm", ht] = c
            ten_P6e = c
            vec_thq = TK6["thq", IT - 1]

            @blk.gpsimd
            def _(g):
                for ht in range(DT):
                    if ht >= 2:
                        p.w(g, "sca", TK6["B", ht - 2, 3])
                        p.w(g, "vec", TK6["A", ht - 2])
                    else:
                        p.w(g, "ten", TK["mmu", IT - 1])
                        p.w(g, "vec", TK["Au", IT - 1])
                    p.inc(g.dma_start(out=w2b[ht % 2][:], in_=w2[ht]),
                          "qga" if ht % 2 == 0 else "qgb", 16)

            def emit_A(v, ht):
                p.w(v, "qga" if ht % 2 == 0 else "qgb",
                    (qga_d0 if ht % 2 == 0 else qgb_d0) + 16 * (ht // 2 + 1))
                if ht >= 8:
                    p.w(v, "sca", TK6["dfm", ht - 8])
                wb2 = w2b[ht % 2]
                v.tensor_reduce(out=TMPA[:, 0:1], in_=wb2[:, 0:I_FFN // 2],
                                axis=AX, op=OP.max, apply_absolute_value=True)
                p.chain(v.tensor_reduce(out=TMPA[:, 1:2],
                                        in_=wb2[:, I_FFN // 2:],
                                        axis=AX, op=OP.max,
                                        apply_absolute_value=True), v, "vec")
                p.chain(v.tensor_max(TMPA[:, 2:3], TMPA[:, 0:1],
                                     TMPA[:, 1:2]), v, "vec")
                p.chain(v.tensor_scalar(out=WS2(ht), in0=TMPA[:, 2:3],
                                        scalar1=RQMAX, scalar2=QEPS,
                                        op0=OP.mult, op1=OP.max), v, "vec")
                p.inc(v.reciprocal(WR2(ht), WS2(ht)), "vec")
                assert p.cnt["vec"] == TK6["A", ht]

            @blk.vector
            def _(v):
                p.w(v, "ten", ten_rbc)
                v.tensor_copy(out=rbc[:, 0:512], in_=ps_mm[0][:])
                p.chain(v.tensor_copy(out=rbc[:, 512:1024], in_=ps_mm[1][:]),
                        v, "vec")
                assert p.cnt["vec"] == vec_rbc0
                emit_A(v, 0)
                tsc = [ost[:, 0:1024], ost[:, 1024:2048]]
                for j in range(IT + 1):
                    if j < IT:
                        v.tensor_tensor(out=tsc[j % 2], in0=h_buf[:, j, :],
                                        in1=rbc[:], op=OP.mult)
                    if j >= 1:
                        p.inc(v.tensor_scalar(out=h_buf[:, j - 1, :],
                                              in0=tsc[(j - 1) % 2],
                                              scalar1=MAGIC, scalar2=MAGIC,
                                              op0=OP.add, op1=OP.subtract),
                              "vec")
                        assert p.cnt["vec"] == TK6["thq", j - 1]
                for ht in range(DT):
                    if ht + 1 < DT:
                        emit_A(v, ht + 1)
                    for gg, sz in enumerate(GRP_SZ):
                        p.w(v, "ten", TK6["tr", ht, gg])
                        off = sum(GRP_SZ[:gg])
                        p.inc(v.tensor_copy(
                            out=t2q[:, off * 128:(off + sz) * 128],
                            in_=ps_tr[gg % 2][:, 0:sz * 128]), "vec")
                        assert p.cnt["vec"] == TK6["evac", ht, gg]
                assert p.cnt["vec"] == vec_P6e

            @blk.scalar
            def _(s):
                for ht in range(DT):
                    p.w(s, "vec", TK6["A", ht])
                    p.w(s, "qga" if ht % 2 == 0 else "qgb",
                        (qga_d0 if ht % 2 == 0 else qgb_d0) + 16 * (ht // 2 + 1))
                    if ht >= 1:
                        p.w(s, "ten", TK6["tr", ht - 1, 5])
                    wb2 = w2b[ht % 2]
                    def a1(qq):
                        s.activation(out=wuq[qq % 2][:],
                                     in_=wb2[:, qq * Q4:(qq + 1) * Q4],
                                     func=AF.Identity, scale=WR2(ht),
                                     bias=mgb[:, 0:1])

                    def a2(qq):
                        p.inc(s.activation(out=s2n[:, qq * Q4:(qq + 1) * Q4],
                                           in_=wuq[qq % 2][:],
                                           func=AF.Identity,
                                           bias=mgb[:, 1:2]), "sca")

                    a1(0); a1(1); a2(0); a1(2); a2(1); a1(3); a2(2); a2(3)
                    assert p.cnt["sca"] == TK6["B", ht, 3]
                    if ht >= 1:
                        p.w(s, "ten", TK6["mm", ht - 1])
                        last = None
                        for half in range(2):
                            last = s.activation(
                                out=down_fm[:, ht - 1,
                                            half * 512:(half + 1) * 512],
                                in_=ps_mm[half][:], func=AF.Identity,
                                scale=WS2(ht - 1))
                        p.inc(last, "sca")
                        assert p.cnt["sca"] == TK6["dfm", ht - 1]
                p.w(s, "ten", TK6["mm", DT - 1])
                last = None
                for half in range(2):
                    last = s.activation(
                        out=down_fm[:, DT - 1, half * 512:(half + 1) * 512],
                        in_=ps_mm[half][:], func=AF.Identity,
                        scale=WS2(DT - 1))
                p.inc(last, "sca")
                assert p.cnt["sca"] == sca_P6e

            @blk.tensor
            def _(t):
                for ht in range(DT):
                    off = 0
                    for gg, sz in enumerate(GRP_SZ):
                        p.w(t, "sca", TK6["B", ht, QN[gg]])
                        if gg >= 2:
                            p.w(t, "vec", TK6["evac", ht, gg - 2])
                        elif ht >= 1:
                            p.w(t, "vec", TK6["evac", ht - 1, gg + 4])
                        last = None
                        for j in range(sz):
                            ic = off + j
                            last = t.transpose(
                                ps_tr[gg % 2][:, j * 128:(j + 1) * 128],
                                s2n[:, ic * 128:(ic + 1) * 128], ident[:])
                        p.inc(last, "ten")
                        assert p.cnt["ten"] == TK6["tr", ht, gg]
                        off += sz
                    p.w(t, "vec", TK6["evac", ht, 5])
                    if ht >= 1:
                        p.w(t, "sca", TK6["dfm", ht - 1])
                    last = None
                    for half in range(2):
                        ps = ps_mm[half]
                        for ic in range(IT):
                            if ht == 0 and half == 0:
                                p.w(t, "vec", TK6["thq", ic])
                            last = t.matmul(
                                ps[:], t2q[:, ic * 128:(ic + 1) * 128],
                                h_buf[:, ic, half * 512:(half + 1) * 512],
                                start=(ic == 0), stop=(ic == IT - 1))
                    p.inc(last, "ten")
                    assert p.cnt["ten"] == TK6["mm", ht]
            ten_P6 = p.cnt["ten"]

            if "thq" in taps:
                ap = tap("thq", [128, IT * NO], BF16)

                @blk.sync
                def _(sync):
                    p.w(sync, "vec", vec_thq)
                    p.inc(sync.dma_start(
                        out=ap, in_=h_buf[:].rearrange("p a b -> p (a b)")),
                        "qo", 16)

            sca_P6 = p.cnt["sca"]

            ten_fin0, vec_fin0 = p.cnt["ten"], p.cnt["vec"]
            qsa_f0, qsb_f0 = p.cnt["qsa"], p.cnt["qsb"]
            qo_f0 = p.cnt["qo"]
            obuf = [ost, osu]

            @blk.sync
            def _(sync):
                p.w(sync, "sca", sca_P6)
                p.w(sync, "qp", qp_x1)
                for nt in range(NOT_):
                    if nt >= 2:
                        p.w(sync, "vec", vec_fin0 + 2 * (nt - 2) + 2)
                    p.inc(sync.dma_start(out=xbuf[nt % 2][:], in_=x1_scr[nt]),
                          _ldsem(nt), 16)

            @blk.scalar
            def _(s):
                for nt in range(NOT_):
                    p.w(s, "vec", vec_fin0 + 2 * nt + 2)
                    p.inc(s.dma_start(out=out[nt * 128:(nt + 1) * 128, :],
                                      in_=obuf[nt % 2][:]), "qo", 16)

            @blk.tensor
            def _(t):
                p.w(t, "sca", sca_P6)
                for nt in range(NOT_):
                    if nt >= 1:
                        p.w(t, "vec", vec_fin0 + 2 * (nt - 1) + 1)
                    last = None
                    for ht in range(DT):
                        last = t.transpose(
                            ps_tr[ht // 8][:, (ht % 8) * 128:(ht % 8 + 1) * 128],
                            down_fm[:, ht, nt * 128:(nt + 1) * 128], ident[:])
                    p.inc(last, "ten")

            @blk.vector
            def _(v):
                for nt in range(NOT_):
                    p.w(v, "ten", ten_fin0 + nt + 1)
                    p.w(v, _ldsem(nt), _ldval(nt, qsa_f0, qsb_f0))
                    if nt >= 2:
                        p.w(v, "qo", qo_f0 + 16 * (nt - 1))
                    last = None
                    for half in range(2):
                        last = v.tensor_scalar(
                            out=obuf[nt % 2][:, half * 1024:(half + 1) * 1024],
                            in0=ps_tr[half][:],
                            scalar1=SHS[:, nt:nt + 1], scalar2=None,
                            op0=OP.mult)
                    p.chain(last, v, "vec")
                    p.inc(v.tensor_add(obuf[nt % 2][:], obuf[nt % 2][:],
                                       xbuf[nt % 2][:]), "vec")



        @blk.sync
        def _(sync):
            for nm in ("qs", "qsa", "qsb", "qga", "qgb", "qg2", "qp", "qo"):
                p.w(sync, nm, p.cnt[nm])

    return nc, tap_aps


def _tile_w(W):
    O, Din = W.shape
    A = W.reshape(O // 256, 2, 128, Din // 128, 128)
    A = A.transpose(0, 4, 1, 3, 2)
    return np.ascontiguousarray(A.reshape(O // 256, 128, 2 * Din))


def _perm_for_core(c):
    if c % 2 == 0:
        return (0, 3), (1, 2)
    return (1, 2), (0, 3)


def _moff_for_core(c):
    own, partner = _perm_for_core(c)
    blocks = [own[0], own[1], partner[0], partner[1]]
    gstart = [512 * b for b in blocks]
    offs = []
    for jt in JSET_A:
        offs.append(gstart[0] - (gstart[jt // 4] + 128 * (jt % 4)))
    for jt in JSET_B:
        offs.append(gstart[1] - (gstart[jt // 4] + 128 * (jt % 4)))
    m = np.zeros((128, 24), np.float32)
    m[:] = np.array(offs, np.float32)[None, :]
    return m


def make_inputs_for_core(c, X, W):
    own, partner = _perm_for_core(c)
    xb = X[c // 2]
    rows = np.concatenate([xb[512 * blk:512 * (blk + 1)]
                           for blk in (own + partner)], axis=0)
    d = {"x_in": np.ascontiguousarray(rows), "moff": _moff_for_core(c)}
    d.update(W)
    return d


def prep_weights(L0_attn_Wq, L0_attn_Wk, L0_attn_Wv, L0_attn_Wg, L0_attn_Wp,
                 L0_ffn_w_gate_up, L0_ffn_w_down):
    return {
        "wq": _tile_w(np.asarray(L0_attn_Wq, np.float32)),
        "wk": _tile_w(np.asarray(L0_attn_Wk, np.float32)),
        "wv": _tile_w(np.asarray(L0_attn_Wv, np.float32)),
        "wg": _tile_w(np.asarray(L0_attn_Wg, np.float32)),
        "wp": np.ascontiguousarray(
            np.asarray(L0_attn_Wp, np.float32).T.reshape(DT, 128, D)),
        "w1": np.ascontiguousarray(
            np.asarray(L0_ffn_w_gate_up, np.float32).reshape(2, IT, 128, D)
            .transpose(1, 2, 0, 3).reshape(IT, 128, 2 * D)),
        "w2": np.ascontiguousarray(
            np.asarray(L0_ffn_w_down, np.float32).reshape(DT, 128, I_FFN)),
    }


def kernel(L0_attn_x, L0_attn_Wq, L0_attn_Wk, L0_attn_Wv, L0_attn_Wg,
           L0_attn_Wp, L0_ffn_w_gate_up, L0_ffn_w_down):
    X = np.ascontiguousarray(np.asarray(L0_attn_x, np.float32))
    W = prep_weights(L0_attn_Wq, L0_attn_Wk, L0_attn_Wv, L0_attn_Wg,
                     L0_attn_Wp, L0_ffn_w_gate_up, L0_ffn_w_down)
    nc, _ = build()
    in_maps = [make_inputs_for_core(c, X, W) for c in range(8)]
    res = run_bass_kernel_spmd(nc, in_maps, core_ids=list(range(8)))
    outp = np.empty((4, 2048, 2048), np.float32)
    for c in range(8):
        own, _ = _perm_for_core(c)
        o = np.asarray(res.results[c]["out"], np.float32)
        b = c // 2
        outp[b, 512 * own[0]:512 * (own[0] + 1)] = o[0:512]
        outp[b, 512 * own[1]:512 * (own[1] + 1)] = o[512:1024]
    return outp


# version 15
# speedup vs baseline: 1.2070x; 1.0538x over previous
import numpy as np
import ml_dtypes
import concourse.bass as bass
import concourse.mybir as mybir
from concourse.bass_utils import run_bass_kernel_spmd

F32 = mybir.dt.float32
BF16 = mybir.dt.bfloat16
OP = mybir.AluOpType
AF = mybir.ActivationFunctionType
AX = mybir.AxisListType.X

MAGIC = 12582912.0
LN1_EPS = 1e-5
RMS_EPS = float(np.finfo(np.float32).eps)
RQMAX = 1.0 / 127.0
QEPS = 1e-8

D = 2048
NB = 2048
NO = 1024
DT = 16
NT = 16
NOT_ = 8
I_FFN = 5632
IT = 44
KB = 1024

JSET_A = [0, 1, 2, 3, 8, 9, 10, 11]
JSET_B = list(range(16))


class P:
    def __init__(self, nc, st, names):
        self.sem = {}
        self.cnt = {}
        for n in names:
            self.sem[n] = st.enter_context(nc.semaphore(n))
            self.cnt[n] = 0

    def inc(self, inst, name, k=1):
        inst.then_inc(self.sem[name], k)
        self.cnt[name] += k
        return self.cnt[name]

    def tick(self, eng, name):
        eng.sem_inc(self.sem[name], 1)
        self.cnt[name] += 1
        return self.cnt[name]

    def w(self, eng, name, v):
        if v > 0:
            eng.wait_ge(self.sem[name], v)

    def chain(self, inst, eng, name):
        self.inc(inst, name)
        eng.wait_ge(self.sem[name], self.cnt[name])

    def sw(self, eng, name):
        if self.cnt[name] > 0:
            eng.wait_ge(self.sem[name], self.cnt[name])


def build(taps=(), phase_limit=99):
    taps = set(taps)
    nc = bass.Bass("TRN2", target_bir_lowering=False,
                   detect_race_conditions=False)

    x_in = nc.dram_tensor("x_in", [NB, D], F32, kind="ExternalInput").ap()
    wq = nc.dram_tensor("wq", [8, 128, 2 * D], F32, kind="ExternalInput").ap()
    wk = nc.dram_tensor("wk", [8, 128, 2 * D], F32, kind="ExternalInput").ap()
    wv = nc.dram_tensor("wv", [8, 128, 2 * D], F32, kind="ExternalInput").ap()
    wg = nc.dram_tensor("wg", [8, 128, 2 * D], F32, kind="ExternalInput").ap()
    wp = nc.dram_tensor("wp", [DT, 128, D], F32, kind="ExternalInput").ap()
    w1 = nc.dram_tensor("w1", [IT, 128, 2 * D], F32, kind="ExternalInput").ap()
    w2 = nc.dram_tensor("w2", [DT, 128, I_FFN], F32, kind="ExternalInput").ap()
    moff = nc.dram_tensor("moff", [128, 24], F32, kind="ExternalInput").ap()

    out = nc.dram_tensor("out", [NO, D], F32, kind="ExternalOutput").ap()

    xn_scr = nc.dram_tensor("xn_scr", [4, 128, DT * 512], BF16).ap()
    v_scr = nc.dram_tensor("v_scr", [NT, 4, 128, 512], BF16).ap()
    g_scr = nc.dram_tensor("g_scr", [NOT_, 4, 128, 512], BF16).ap()
    og_scr = nc.dram_tensor("og_scr", [NOT_, 128, D], BF16).ap()
    x1_scr = nc.dram_tensor("x1_scr", [NOT_, 128, D], F32).ap()

    tap_aps = {}

    def tap(name, shape, dtype=F32):
        if name in taps:
            tap_aps[name] = nc.dram_tensor("tap_" + name, shape, dtype,
                                           kind="ExternalOutput").ap()
            return tap_aps[name]
        return None

    from contextlib import ExitStack
    with ExitStack() as st:
        _allocs = []

        BASE = 24 * KB

        def at(name, shape, dt, off_b):
            per = int(np.prod(shape[1:])) * (2 if dt == BF16 else 4)
            _allocs.append((name, off_b, off_b + per))
            assert BASE + off_b + per <= 224 * KB, f"{name} over budget"
            return nc.alloc_sbuf_tensor_at(name, shape, dt, offset=BASE + off_b)

        k_fm = at("k_fm", [128, DT, NB], BF16, 0)
        h_buf = at("h_buf", [128, IT, NO], BF16, 0)

        xn_fm = at("xn_fm", [128, 2, DT, NO], BF16, 64 * KB)
        q_fm = xn_fm[:, 1]
        xbuf = [at(f"xbuf{i}", [128, D], F32, (64 + 8 * i) * KB) for i in range(2)]
        xbuf1 = [at("xbuf1a", [128, D], F32, 161 * KB + 512),
                 at("xbuf1b", [128, D], F32, 169 * KB + 512)]
        xb2 = [at(f"xb2{i}", [128, D], F32, (64 + 8 * i) * KB)
               for i in range(2)]
        stgP1 = [at(f"stgP1{i}", [128, D], BF16, (148 + 4 * i) * KB)
                 for i in range(2)]
        vevq = [at(f"vevq{i}", [128, 512], BF16, (146 + i) * KB)
                for i in range(2)]
        x1t = at("x1t", [128, D], F32, 80 * KB)
        sqdump = at("sqdump", [128, D], F32, 80 * KB)
        stg = at("stg", [128, D], BF16, 88 * KB)
        stg2 = at("stg2", [128, D], BF16, 92 * KB)
        s2n = at("s2n", [128, I_FFN], BF16, 88 * KB)

        x4buf = [at(f"x4buf{i}", [128, D], F32, (32 + 8 * i) * KB)
                 for i in range(2)]
        x1t4 = [at(f"x1t4{i}", [128, D], F32, (48 + 8 * i) * KB)
                for i in range(2)]
        og_fm4 = at("og_fm4", [128, DT, NO], BF16, 64 * KB)
        stg4 = [at(f"stg4{i}", [128, D], BF16, (161 + 4 * i) * KB + 512)
                for i in range(2)]
        osu = at("osu", [128, D], F32, 169 * KB + 512)
        poob = at("poob", [128, 1024], F32, 161 * KB + 512)
        wpb = [at(f"wpb{dc}", [128, D], BF16,
                  (4 * dc if dc < 8 else 128 + 4 * (dc - 8)) * KB)
               for dc in range(DT)]
        stq = at("stq", [128, 16, 512], BF16, 64 * KB)
        vq = at("vq", [128, 16, 512], BF16, 80 * KB)
        og_fm = at("og_fm", [128, DT, NO], BF16, 96 * KB)
        xq_fm = at("xq_fm", [128, DT, NO], BF16, 96 * KB)
        down_fm = at("down_fm", [128, DT, NO], BF16, 100 * KB)

        wbuf = [at(f"wbuf{i}", [128, 2 * D], BF16, (128 + 8 * i) * KB)
                for i in range(2)]
        gtile = at("gtile", [128, 4, D], BF16, 128 * KB)
        vq2 = at("vq2", [128, 16, 512], BF16, 144 * KB)
        vstg = [at(f"vstg{i}", [128, 512], BF16, (144 + i) * KB) for i in range(2)]
        t1qb = [at(f"t1q{i}", [128, D], BF16, (144 + 4 * i) * KB) for i in range(4)]
        uwb = at("uwb", [128, D], F32, 88 * KB)
        w2b = [at(f"w2b{i}", [128, I_FFN], BF16, (132 + 11 * i) * KB)
               for i in range(2)]
        t2q = at("t2q", [128, IT * 128], BF16, 161 * KB + 512)

        ident = at("ident", [128, 128], BF16, 160 * KB)
        ones1 = at("ones1", [1, 512], BF16, 160 * KB + 256)
        ogst = [at(f"ogst{i}", [128, 512], BF16, 165 * KB + 256 + i * KB)
                for i in range(2)]
        masks3 = at("masks3", [128, 16, 512], BF16, 167 * KB + 256 + 256)
        gsil = at("gsil", [128, NO], F32, 167 * KB + 512)
        hacc = at("hacc", [128, NO], BF16, 172 * KB)
        rbc = at("rbc", [128, NO], F32, 128 * KB)
        rrowb = at("rrowb", [128, 1024], BF16, 176 * KB)
        wuq = [at(f"wuq{i}", [128, I_FFN // 4], F32,
                   (172 if i == 0 else 178) * KB + 512) for i in range(2)]

        iota = at("iota", [128, 512], F32, 184 * KB)
        sv = at("sv", [128, 1024], F32, 186 * KB)
        ost = at("ost", [128, D], F32, 190 * KB)
        mofb = at("mofb", [128, 24], F32, 198 * KB)
        wsc = at("wsc", [128, 64], F32, 198 * KB + 128)
        wscb = at("wscb", [128, 16], BF16, 178 * KB)
        wsc2 = at("wsc2", [128, 16], F32, 178 * KB + 128)
        mgb = at("mgb", [128, 2], F32, 178 * KB + 320)

        ps_mm = [st.enter_context(nc.psum_tensor(f"ps_mm{i}", [128, 512], F32))
                 for i in range(4)]
        ps_tr = [st.enter_context(nc.psum_tensor(f"ps_tr{i}", [128, 1024], BF16))
                 for i in range(2)]
        ps_ex = [st.enter_context(nc.psum_tensor(f"ps_ex{i}", [128, 512], F32))
                 for i in range(2)]

        p = P(nc, st, ["qs", "qsa", "qsb", "qga", "qgb", "qg2", "qp", "qo",
                       "ten", "vec", "sca", "gps"])
        block = st.enter_context(nc.Block())

        class _B:

            def vector(self, f):
                block.vector(lambda v: (p.sw(v, "vec"), f(v))[1])

            def scalar(self, f):
                block.scalar(lambda s_: (p.sw(s_, "sca"), f(s_))[1])

            def tensor(self, f):
                block.tensor(lambda t: (p.sw(t, "ten"), f(t))[1])

            def gpsimd(self, f):
                block.gpsimd(lambda g: (p.sw(g, "gps"), f(g))[1])

            def sync(self, f):
                block.sync(f)

        blk = _B()

        SUM = sv[:, 0:16]
        SQ = sv[:, 16:32]
        M0 = sv[:, 32:48]
        E0 = sv[:, 48:64]
        V0 = sv[:, 64:80]
        T0 = sv[:, 80:96]
        R0 = sv[:, 96:112]
        S0 = sv[:, 112:128]
        N1 = sv[:, 128:144]
        N2 = sv[:, 144:160]
        SCL = sv[:, 160:176]
        BIA = sv[:, 176:192]
        AM = sv[:, 192:200]
        SX = sv[:, 200:208]
        RX = sv[:, 208:216]
        SH = sv[:, 216:224]
        SHS = sv[:, 224:232]
        RH = sv[:, 232:240]
        TMPA = sv[:, 240:248]

        def WSG(i):
            return wsc[:, i % 8:i % 8 + 1]

        def WRG(i):
            return wsc[:, 8 + i % 8:8 + i % 8 + 1]

        def WSU(i):
            return wsc[:, 16 + i % 8:16 + i % 8 + 1]

        def WRU(i):
            return wsc[:, 24 + i % 8:24 + i % 8 + 1]

        def WS2(i):
            return wsc[:, 32 + i % 8:32 + i % 8 + 1]

        def WR2(i):
            return wsc[:, 40 + i % 8:40 + i % 8 + 1]

        @blk.sync
        def _(sync):
            p.inc(sync.dma_start(out=mofb[:], in_=moff), "qs", 16)
        moff_v = p.cnt["qs"]

        @blk.vector
        def _(v):
            v.memset(ident[:], 0.0)
            p.inc(v.memset(ones1[:], 1.0), "vec")
        vec_ms = p.cnt["vec"]

        @blk.gpsimd
        def _(g):
            p.w(g, "vec", vec_ms)
            p.inc(g.affine_select(out=ident[:], in_=ident[:],
                                  compare_op=OP.not_equal, fill=1.0, base=0,
                                  pattern=[[-1, 128]], channel_multiplier=1),
                  "gps")
            p.inc(g.iota(out=iota[:], pattern=[[1, 512]], base=0,
                         channel_multiplier=-1,
                         allow_small_or_imprecise_dtypes=True), "gps")
        gps_p0 = p.cnt["gps"]

        if "iota" in taps:
            ap = tap("iota", [128, 512], F32)

            @blk.sync
            def _(sync):
                p.w(sync, "gps", gps_p0)
                p.inc(sync.dma_start(out=ap, in_=iota[:]), "qo", 16)

        qsa0, qsb0 = p.cnt["qsa"], p.cnt["qsb"]

        def _ldsem(r):
            return "qsa" if r % 2 == 0 else "qsb"

        def _ldval(r, base_a=qsa0, base_b=qsb0):
            return (base_a if r % 2 == 0 else base_b) + 16 * (r // 2 + 1)

        ORD1 = list(range(8, 16)) + list(range(8))
        vec_10, sca_10, ten_10 = p.cnt["vec"], p.cnt["sca"], p.cnt["ten"]
        qs_10 = p.cnt["qs"]
        TK1 = {}
        c = vec_10
        for i in range(8):
            c += 1; TK1["SUMr", i] = c
        c += 4; TK1["R0p"] = c
        c += 6; TK1["BIAp"] = c
        for i in range(8, 16):
            c += 1; TK1["SUMr", i] = c
            c += 1; TK1["evac", i - 8] = c
        c += 4; TK1["R0o"] = c
        c += 6; TK1["BIAo"] = c
        for j in range(8, 16):
            c += 1; TK1["evac", j] = c
        vec_P1e = c
        c = sca_10
        for i in range(8):
            c += 1; TK1["SQ", i] = c
        c += 1; TK1["sqp"] = c
        for i in range(8, 16):
            c += 1; TK1["SQ", i] = c
            c += 1; TK1["apply", i - 8] = c
        c += 1; TK1["sqo"] = c
        for j in range(8, 16):
            c += 1; TK1["apply", j] = c
        c = ten_10
        for j in range(NT):
            c += 1; TK1["tr", j] = c

        @blk.sync
        def _(sync):
            for i, r in enumerate(ORD1):
                if i >= 2:
                    p.w(sync, "vec", TK1["SUMr", i - 2])
                    p.w(sync, "sca", TK1["SQ", i - 2])
                p.inc(sync.dma_start(out=xbuf1[i % 2][:],
                                     in_=x_in[r * 128:(r + 1) * 128, :]),
                      _ldsem(i), 16)

        def issue_b(s, i):
            if i >= NT:
                return
            buf = xb2[i % 2] if i < 8 else xbuf1[i % 2]
            if i == 8:
                p.w(s, "vec", TK1["SUMr", 14])
                p.w(s, "sca", TK1["SQ", 14])
            if i == 9:
                p.w(s, "vec", TK1["SUMr", 15])
                p.w(s, "sca", TK1["SQ", 15])
            if i >= 2:
                p.w(s, "sca", TK1["apply", i - 2])
            r = ORD1[i]
            p.inc(s.dma_start(out=buf[:],
                              in_=x_in[r * 128:(r + 1) * 128, :]), "qs", 16)

        def scale_batch(v, lo, hi, kR):
            sl = slice(8 * lo // 8, hi)

            def cc(t_):
                return t_[:, lo:hi]

            v.tensor_scalar(out=cc(M0), in0=cc(SUM), scalar1=1.0 / D,
                            scalar2=None, op0=OP.mult)
            v.tensor_scalar(out=cc(E0), in0=cc(SQ), scalar1=1.0 / D,
                            scalar2=None, op0=OP.mult)
            p.chain(v.tensor_mul(cc(V0), cc(M0), cc(M0)), v, "vec")
            p.chain(v.tensor_sub(cc(V0), cc(E0), cc(V0)), v, "vec")
            p.chain(v.tensor_scalar(out=cc(T0), in0=cc(V0),
                                    scalar1=1.0 + RMS_EPS,
                                    scalar2=RMS_EPS * LN1_EPS,
                                    op0=OP.mult, op1=OP.add), v, "vec")
            p.inc(v.reciprocal(cc(R0), cc(T0)), "vec")
            assert p.cnt["vec"] == TK1[kR]

        def newton_batch(v, lo, hi, kB):
            def cc(t_):
                return t_[:, lo:hi]

            p.chain(v.tensor_mul(cc(N1), cc(S0), cc(S0)), v, "vec")
            p.chain(v.tensor_mul(cc(N1), cc(T0), cc(N1)), v, "vec")
            p.chain(v.tensor_scalar(out=cc(N1), in0=cc(N1), scalar1=-0.5,
                                    scalar2=1.5, op0=OP.mult, op1=OP.add),
                    v, "vec")
            p.chain(v.tensor_mul(cc(SCL), cc(S0), cc(N1)), v, "vec")
            p.chain(v.tensor_mul(cc(N2), cc(M0), cc(SCL)), v, "vec")
            p.inc(v.tensor_scalar(out=cc(BIA), in0=cc(N2), scalar1=-1.0,
                                  scalar2=None, op0=OP.mult), "vec")
            assert p.cnt["vec"] == TK1[kB]

        @blk.vector
        def _(v):
            for i in range(8):
                p.w(v, _ldsem(i), _ldval(i))
                inst = v.tensor_reduce(out=SUM[:, ORD1[i]:ORD1[i] + 1],
                                       in_=xbuf1[i % 2][:], axis=AX,
                                       op=OP.add)
                if i == 7:
                    p.chain(inst, v, "vec")
                else:
                    p.inc(inst, "vec")
                assert p.cnt["vec"] == TK1["SUMr", i]
            p.w(v, "sca", TK1["SQ", 7])
            scale_batch(v, 8, 16, "R0p")
            p.w(v, "sca", TK1["sqp"])
            newton_batch(v, 8, 16, "BIAp")
            for i in range(8, 16):
                p.w(v, _ldsem(i), _ldval(i))
                inst = v.tensor_reduce(out=SUM[:, ORD1[i]:ORD1[i] + 1],
                                       in_=xbuf1[i % 2][:], axis=AX,
                                       op=OP.add)
                if i == 15:
                    p.chain(inst, v, "vec")
                else:
                    p.inc(inst, "vec")
                assert p.cnt["vec"] == TK1["SUMr", i]
                k = i - 8
                p.w(v, "ten", TK1["tr", k])
                last = None
                rp = ORD1[k]
                for half in range(2):
                    last = v.tensor_copy(
                        out=xn_fm[:, rp // 8, half * 8:(half + 1) * 8,
                                  (rp % 8) * 128:(rp % 8 + 1) * 128],
                        in_=ps_tr[half][:].rearrange("p (c f) -> p c f", c=8))
                p.inc(last, "vec")
                assert p.cnt["vec"] == TK1["evac", k]
            p.w(v, "sca", TK1["SQ", 15])
            scale_batch(v, 0, 8, "R0o")
            p.w(v, "sca", TK1["sqo"])
            newton_batch(v, 0, 8, "BIAo")
            for j in range(8, 16):
                p.w(v, "ten", TK1["tr", j])
                last = None
                rp = ORD1[j]
                for half in range(2):
                    last = v.tensor_copy(
                        out=xn_fm[:, rp // 8, half * 8:(half + 1) * 8,
                                  (rp % 8) * 128:(rp % 8 + 1) * 128],
                        in_=ps_tr[half][:].rearrange("p (c f) -> p c f", c=8))
                p.inc(last, "vec")
                assert p.cnt["vec"] == TK1["evac", j]
            assert p.cnt["vec"] == vec_P1e

        @blk.scalar
        def _(s):
            issue_b(s, 0)
            issue_b(s, 1)
            for i in range(8):
                p.w(s, _ldsem(i), _ldval(i))
                p.sw(s, "sca")
                p.inc(s.activation(out=ost[:], in_=xbuf1[i % 2][:],
                                   func=AF.Square,
                                   accum_out=SQ[:, ORD1[i]:ORD1[i] + 1]),
                      "sca")
                assert p.cnt["sca"] == TK1["SQ", i]
            p.w(s, "vec", TK1["R0p"])
            p.inc(s.activation(out=S0[:, 8:16], in_=R0[:, 8:16],
                               func=AF.Sqrt), "sca")
            assert p.cnt["sca"] == TK1["sqp"]
            for i in range(8, 16):
                p.w(s, _ldsem(i), _ldval(i))
                p.sw(s, "sca")
                p.inc(s.activation(out=ost[:], in_=xbuf1[i % 2][:],
                                   func=AF.Square,
                                   accum_out=SQ[:, ORD1[i]:ORD1[i] + 1]),
                      "sca")
                assert p.cnt["sca"] == TK1["SQ", i]
                k = i - 8
                rp = ORD1[k]
                if k == 0:
                    p.w(s, "vec", TK1["BIAp"])
                p.w(s, "qs", qs_10 + 16 * (k + 1))
                if k >= 2:
                    p.w(s, "ten", TK1["tr", k - 2])
                p.inc(s.activation(out=stgP1[k % 2][:], in_=xb2[k % 2][:],
                                   func=AF.Identity,
                                   scale=SCL[:, rp:rp + 1],
                                   bias=BIA[:, rp:rp + 1]), "sca")
                assert p.cnt["sca"] == TK1["apply", k]
                issue_b(s, k + 2)
            p.w(s, "vec", TK1["R0o"])
            p.inc(s.activation(out=S0[:, 0:8], in_=R0[:, 0:8],
                               func=AF.Sqrt), "sca")
            assert p.cnt["sca"] == TK1["sqo"]
            for j in range(8, 16):
                rp = ORD1[j]
                if j == 8:
                    p.w(s, "vec", TK1["BIAo"])
                p.w(s, "qs", qs_10 + 16 * (j + 1))
                p.w(s, "ten", TK1["tr", j - 2])
                p.inc(s.activation(out=stgP1[j % 2][:], in_=xbuf1[j % 2][:],
                                   func=AF.Identity,
                                   scale=SCL[:, rp:rp + 1],
                                   bias=BIA[:, rp:rp + 1]), "sca")
                assert p.cnt["sca"] == TK1["apply", j]
                issue_b(s, j + 2)

        @blk.tensor
        def _(t):
            p.w(t, "gps", gps_p0)
            for j in range(NT):
                p.w(t, "sca", TK1["apply", j])
                if j >= 1:
                    p.w(t, "vec", TK1["evac", j - 1])
                last = None
                for dc in range(DT):
                    last = t.transpose(
                        ps_tr[dc // 8][:, (dc % 8) * 128:(dc % 8 + 1) * 128],
                        stgP1[j % 2][:, dc * 128:(dc + 1) * 128], ident[:])
                p.inc(last, "ten")
                assert p.cnt["ten"] == TK1["tr", j]

        if "xn_scr" in taps:
            ap = tap("xn_scr", [128, 2 * DT * NO], BF16)

            @blk.sync
            def _(sync):
                p.w(sync, "vec", vec_P1e)
                p.inc(sync.dma_start(
                    out=ap, in_=xn_fm[:].rearrange("p a b c -> p (a b c)")),
                    "qo", 16)

        def emit_proj_fm(wsrc, dst_fm, sp, act, dst_base=None, first=False):
            qga0, qgb0 = p.cnt["qga"], p.cnt["qgb"]
            sca_0, ten_0 = p.cnt["sca"], p.cnt["ten"]
            db = 1024 * sp if dst_base is None else dst_base

            def _wsem(grp):
                return "qga" if grp % 2 == 0 else "qgb"

            def _wval(grp):
                return (qga0 if grp % 2 == 0 else qgb0) + 16 * (grp // 2 + 1)

            @blk.gpsimd
            def _(g):
                for grp in range(8):
                    if grp >= 2:
                        p.w(g, "ten", ten_0 + 4 * (grp - 1))
                    elif not first:
                        p.w(g, "ten", ten_0 - (4 if grp == 0 else 0))
                    p.inc(g.dma_start(out=wbuf[grp % 2][:], in_=wsrc[grp]),
                          _wsem(grp), 16)

            @blk.tensor
            def _(t):
                for ot in range(DT):
                    grp, half = ot // 2, ot % 2
                    if half == 0:
                        p.w(t, _wsem(grp), _wval(grp))
                    for sl in range(2):
                        u = 2 * ot + sl
                        if u >= 2:
                            p.w(t, "sca", sca_0 + u - 1)
                        ps = ps_mm[u % 2]
                        last = None
                        for dc in range(DT):
                            last = t.matmul(ps[:],
                                            wbuf[grp % 2][:, half * D + dc * 128:
                                                          half * D + (dc + 1) * 128],
                                            xn_fm[:, sp, dc, sl * 512:
                                                  (sl + 1) * 512],
                                            start=(dc == 0), stop=(dc == DT - 1))
                        p.inc(last, "ten")

            @blk.scalar
            def _(s):
                for u in range(32):
                    p.w(s, "ten", ten_0 + u + 1)
                    ot, sl = u // 2, u % 2
                    p.inc(s.activation(out=dst_fm[:, ot, db + 512 * sl:
                                               db + 512 * (sl + 1)],
                                       in_=ps_mm[u % 2][:], func=act), "sca")

        def emit_proj_tm(wsrc, scr, sp, act, base_tile):
            qga0, qgb0 = p.cnt["qga"], p.cnt["qgb"]
            sca_0, ten_0, vec_0 = p.cnt["sca"], p.cnt["ten"], p.cnt["vec"]
            qp_0 = p.cnt["qp"]

            def _wsem(grp):
                return "qga" if grp % 2 == 0 else "qgb"

            def _wval(grp):
                return (qga0 if grp % 2 == 0 else qgb0) + 16 * (grp // 2 + 1)

            @blk.gpsimd
            def _(g):
                for grp in range(8):
                    if grp >= 2:
                        p.w(g, "ten", ten_0 + 8 * (grp - 1))
                    else:
                        p.w(g, "ten", ten_0 - (4 if grp == 0 else 0))
                    p.inc(g.dma_start(out=wbuf[grp % 2][:], in_=wsrc[grp]),
                          _wsem(grp), 16)

            def Mtick(u):
                return ten_0 + (1 if u == 0 else 2 * u)

            def Rtick(v):
                return ten_0 + (64 if v == 31 else 2 * v + 3)

            @blk.tensor
            def _(t):
                for u in range(33):
                    if u < 32:
                        ot, sl = u // 2, u % 2
                        grp, half = ot // 2, ot % 2
                        if half == 0 and sl == 0:
                            p.w(t, _wsem(grp), _wval(grp))
                        if u >= 2:
                            p.w(t, "sca", sca_0 + u - 1)
                        ps = ps_mm[u % 2]
                        last = None
                        for dc in range(DT):
                            last = t.matmul(ps[:],
                                            wbuf[grp % 2][:, half * D + dc * 128:
                                                          half * D + (dc + 1) * 128],
                                            xn_fm[:, sp, dc, sl * 512:
                                                  (sl + 1) * 512],
                                            start=(dc == 0), stop=(dc == DT - 1))
                        p.inc(last, "ten")
                        assert p.cnt["ten"] == Mtick(u)
                    if u >= 1:
                        v_ = u - 1
                        p.w(t, "sca", sca_0 + v_ + 1)
                        if v_ >= 2:
                            p.w(t, "vec", vec_0 + v_ - 1)
                        last = None
                        for q4 in range(4):
                            last = t.transpose(
                                ps_tr[v_ % 2][:, q4 * 128:(q4 + 1) * 128],
                                vstg[v_ % 2][:, q4 * 128:(q4 + 1) * 128],
                                ident[:])
                        p.inc(last, "ten")
                        assert p.cnt["ten"] == Rtick(v_)

            @blk.vector
            def _(v):
                for u in range(32):
                    p.w(v, "ten", Rtick(u))
                    if u < 2:
                        p.w(v, "qp", qp_0)
                    else:
                        p.w(v, "qp", qp_0 + 64 * (u - 1))
                    p.inc(v.tensor_copy(out=vevq[u % 2][:],
                                        in_=ps_tr[u % 2][:, 0:512]), "vec")

            @blk.scalar
            def _(s):
                for u in range(33):
                    if u < 32:
                        p.w(s, "ten", Mtick(u))
                        p.inc(s.activation(out=vstg[u % 2][:],
                                           in_=ps_mm[u % 2][:], func=act),
                              "sca")
                    if u >= 1:
                        up = u - 1
                        ot, sl = up // 2, up % 2
                        p.w(s, "vec", vec_0 + up + 1)
                        for q4 in range(4):
                            p.inc(s.dma_start(
                                out=scr[base_tile + 4 * sl + q4, ot // 4, :,
                                        (ot % 4) * 128:(ot % 4 + 1) * 128],
                                in_=vevq[up % 2][:, q4 * 128:(q4 + 1) * 128]),
                                "qp", 16)

        sp_marks = {}
        if phase_limit >= 2:
            for sp in (1, 0):
                @blk.tensor
                def _(t, sp=sp):
                    p.w(t, "vec", TK1["evac", 7 if sp == 1 else 15])

                emit_proj_fm(wk, k_fm, sp, AF.Silu, first=(sp == 1))
                if phase_limit >= 3:
                    emit_proj_tm(wv, v_scr, sp, AF.Silu, base_tile=8 * sp)
                if sp == 0 and phase_limit >= 4:
                    emit_proj_tm(wg, g_scr, 0, AF.Sigmoid, base_tile=0)
                if sp == 0 and phase_limit >= 5:
                    qp_now = p.cnt["qp"]

                    @blk.scalar
                    def _(s_, qp_now=qp_now):
                        p.w(s_, "qp", qp_now)
                    emit_proj_fm(wq, q_fm, 0, AF.Silu, dst_base=0)
                sp_marks[("ten_end", sp)] = p.cnt["ten"]

        sca_P2, qp_P2 = p.cnt["sca"], p.cnt["qp"]

        if "k_fm" in taps and phase_limit >= 2:
            ap = tap("k_fm", [128, DT * NB], BF16)

            @blk.sync
            def _(sync):
                p.w(sync, "sca", sca_P2)
                p.inc(sync.dma_start(
                    out=ap, in_=k_fm[:].rearrange("p a b -> p (a b)")), "qo", 16)
        if "q_fm" in taps and phase_limit >= 5:
            ap = tap("q_fm", [128, DT * NO], BF16)

            @blk.sync
            def _(sync):
                p.w(sync, "sca", sca_P2)
                p.inc(sync.dma_start(
                    out=ap, in_=q_fm[:].rearrange("p a b -> p (a b)")), "qo", 16)
        if "v_tm" in taps and phase_limit >= 3:
            ap = tap("v_tm", [NT * 4 * 128, 512], BF16)

            @blk.sync
            def _(sync):
                p.w(sync, "qp", qp_P2)
                p.inc(sync.dma_start(
                    out=ap, in_=v_scr.rearrange("a e b c -> (a e b) c")),
                    "qo", 16)
        if "g_tm" in taps and phase_limit >= 4:
            ap = tap("g_tm", [NOT_ * 4 * 128, 512], BF16)

            @blk.sync
            def _(sync):
                p.w(sync, "qp", qp_P2)
                p.inc(sync.dma_start(
                    out=ap, in_=g_scr.rearrange("a e b c -> (a e b) c")), "qo", 16)

        qp_og = p.cnt["qp"]
        p3m = {}
        ten_P2e = p.cnt["ten"]
        if phase_limit >= 6:
            for ib, jset in ((0, JSET_A), (1, JSET_B)):
                nj = len(jset)
                vec_mk0 = p.cnt["vec"]

                @blk.vector
                def _(v, ib=ib, jset=jset):
                    if ib == 0:
                        p.w(v, "qs", moff_v)
                        p.w(v, "gps", gps_p0)
                        p.w(v, "qp", qp_P2)
                    last = None
                    for idx in range(len(jset)):
                        c = idx if ib == 0 else 8 + idx
                        last = v.tensor_scalar(out=masks3[:, idx, :], in0=iota[:],
                                               scalar1=mofb[:, c:c + 1],
                                               scalar2=0.0,
                                               op0=OP.add, op1=OP.is_ge)
                    p.inc(last, "vec")
                vec_mk = p.cnt["vec"]

                ten_s0, vec_s0 = p.cnt["ten"], p.cnt["vec"]

                @blk.tensor
                def _(t, ib=ib, jset=jset, ten_s0=ten_s0, vec_s0=vec_s0,
                      vec_mk=vec_mk):
                    p.w(t, "vec", vec_mk)
                    if ib == 0:
                        p.w(t, "sca", sca_P2)
                    for idx, jt in enumerate(jset):
                        if idx >= 2:
                            p.w(t, "vec", vec_s0 + idx - 1)
                        ps = ps_mm[idx % 2]
                        last = None
                        for dc in range(DT):
                            last = t.matmul(ps[:],
                                            k_fm[:, dc, jt * 128:(jt + 1) * 128],
                                            q_fm[:, dc, ib * 512:(ib + 1) * 512],
                                            start=(dc == 0), stop=(dc == DT - 1))
                        p.inc(last, "ten")

                if ib == 1:
                    p3m["kdone"] = p.cnt["ten"]

                @blk.vector
                def _(v, jset=jset, ten_s0=ten_s0):
                    for idx in range(len(jset)):
                        p.w(v, "ten", ten_s0 + idx + 1)
                        p.inc(v.tensor_tensor(out=stq[:, idx, :],
                                              in0=ps_mm[idx % 2][:],
                                              in1=masks3[:, idx, :],
                                              op=OP.mult), "vec")
                vec_s = p.cnt["vec"]

                qs_g0 = p.cnt["qs"]

                vec_now = p.cnt["vec"]
                ten_v0, vec_v0 = p.cnt["ten"], p.cnt["vec"]
                qsa_v0, qsb_v0 = p.cnt["qsa"], p.cnt["qsb"]
                qp_o0 = p.cnt["qp"]

                @blk.sync
                def _(sync, ib=ib, jset=jset, vec_v0=vec_v0):
                    nj = len(jset)
                    if ib == 0:
                        p.w(sync, "qp", qp_P2)
                        p.w(sync, "sca", sca_P2)
                    for eh in range(4):
                        if eh >= 2:
                            p.w(sync, "vec", vec_v0 + 4 * (eh - 1))
                        elif ib == 1:
                            p.w(sync, "ten", p3m["sv0_eh2" if eh == 0
                                                else "sv0_end"])
                        for idx, jt in enumerate(jset):
                            buf = vq if eh % 2 == 0 else vq2
                            p.inc(sync.dma_start(
                                out=buf[:, idx, :],
                                in_=v_scr[jt, eh]),
                                "qsa" if eh % 2 == 0 else "qsb", 16)

                @blk.gpsimd
                def _(g, ib=ib, vec_now=vec_now):
                    if ib == 0:
                        p.w(g, "qp", qp_P2)
                        p.w(g, "ten", ten_P2e)
                    else:
                        p.w(g, "vec", vec_now)
                    for it in range(4):
                        for eh in range(4):
                            p.inc(g.dma_start(
                                out=gtile[:, it, eh * 512:(eh + 1) * 512],
                                in_=g_scr[4 * ib + it, eh]), "qg2", 16)
                qs_g = p.cnt["qg2"]

                @blk.tensor
                def _(t, jset=jset, ten_v0=ten_v0, vec_s=vec_s,
                      qsa_v0=qsa_v0, qsb_v0=qsb_v0, vec_v0=vec_v0):
                    nj = len(jset)
                    p.w(t, "vec", vec_s)
                    for eh in range(4):
                        p.w(t, "qsa" if eh % 2 == 0 else "qsb",
                            (qsa_v0 if eh % 2 == 0 else qsb_v0)
                            + 16 * nj * (eh // 2 + 1))
                        buf = vq if eh % 2 == 0 else vq2
                        for it in range(4):
                            u = 4 * eh + it
                            if u >= 2:
                                p.w(t, "vec", vec_v0 + u - 1)
                            ps = ps_mm[2 + u % 2]
                            last = None
                            for idx in range(nj):
                                last = t.matmul(ps[:],
                                                stq[:, idx, it * 128:(it + 1) * 128],
                                                buf[:, idx, :],
                                                start=(idx == 0),
                                                stop=(idx == nj - 1))
                            p.inc(last, "ten")

                if ib == 0:
                    p3m["sv0_eh2"] = ten_v0 + 12
                    p3m["sv0_end"] = ten_v0 + 16

                dbg_ap = None
                if ib == 0 and "dbgo" in taps:
                    dbg_ap = tap("dbgo", [128, 3 * 8192], F32)
                    dbg_bf = tap("dbgb", [128, 2 * 8192], BF16)

                @blk.vector
                def _(v, ib=ib, ten_v0=ten_v0, qs_g=qs_g, qp_o0=qp_o0,
                      jset=jset):
                    p.w(v, "qg2", qs_g)
                    for eh in range(4):
                        for it in range(4):
                            u = 4 * eh + it
                            p.w(v, "ten", ten_v0 + u + 1)
                            if u >= 2:
                                p.w(v, "qp", qp_o0 + 16 * u)
                            else:
                                p.w(v, "qp", qp_o0)
                            if ib == 0 and eh == 3 and it == 0 and dbg_ap is not None:
                                v.tensor_copy(out=ost[:, 0:512],
                                              in_=ps_mm[2 + u % 2][:])
                            p.inc(v.tensor_tensor(
                                out=ogst[u % 2][:],
                                in0=ps_mm[2 + u % 2][:],
                                in1=gtile[:, it, eh * 512:(eh + 1) * 512],
                                op=OP.mult), "vec")

                @blk.scalar
                def _(s, ib=ib, vec_v0=vec_v0):
                    for eh in range(4):
                        for it in range(4):
                            u = 4 * eh + it
                            p.w(s, "vec", vec_v0 + u + 1)
                            p.inc(s.dma_start(
                                out=og_scr[4 * ib + it, :, eh * 512:(eh + 1) * 512],
                                in_=ogst[u % 2][:]), "qp", 16)

                if dbg_ap is not None and ib == 0:
                    @blk.sync
                    def _(sync, vec_v0=vec_v0):
                        p.w(sync, "vec", vec_v0 + 16)
                        p.inc(sync.dma_start(out=dbg_ap[:, 0:512],
                                             in_=ost[:, 0:512]), "qo", 16)
                        p.inc(sync.dma_start(
                            out=dbg_bf[:, 0:8 * 512],
                            in_=stq[:, 0:8, :].rearrange("p a b -> p (a b)")),
                            "qo", 16)
                        p.inc(sync.dma_start(
                            out=dbg_bf[:, 8192:8192 + 8 * 512],
                            in_=vq2[:, 0:8, :].rearrange("p a b -> p (a b)")),
                            "qo", 16)
            qp_og = p.cnt["qp"]

            if "og" in taps:
                ap = tap("og", [NOT_ * 128, D], BF16)

                @blk.sync
                def _(sync):
                    p.w(sync, "qp", qp_og)
                    p.inc(sync.dma_start(
                        out=ap, in_=og_scr.rearrange("a b c -> (a b) c")), "qo", 16)

        vec_xq = p.cnt["vec"]
        qp_x1 = p.cnt["qp"]
        if phase_limit >= 7:
            ten_a0, vec_a0 = p.cnt["ten"], p.cnt["vec"]
            vec_P3e = p.cnt["vec"]
            ten_P3e = p.cnt["ten"]
            qga_w0, qgb_w0 = p.cnt["qga"], p.cnt["qgb"]
            qsa_a0, qsb_a0 = p.cnt["qsa"], p.cnt["qsb"]

            @blk.gpsimd
            def _(g):
                for dc in range(DT):
                    if dc < 8:
                        p.w(g, "ten", p3m.get("kdone", ten_P3e))
                    else:
                        p.w(g, "ten", ten_P3e)
                        p.w(g, "vec", vec_P3e)
                    p.inc(g.dma_start(out=wpb[dc][:], in_=wp[dc]),
                          "qga" if dc % 2 == 0 else "qgb", 16)

            @blk.sync
            def _(sync):
                p.w(sync, "qp", qp_og)
                for it in range(NOT_):
                    if it >= 2:
                        p.w(sync, "ten", ten_a0 + (it - 1))
                    p.inc(sync.dma_start(out=stg4[it % 2][:],
                                         in_=og_scr[it]), _ldsem(it), 16)

            @blk.tensor
            def _(t):
                for it in range(NOT_):
                    p.w(t, _ldsem(it), _ldval(it, qsa_a0, qsb_a0))
                    if it >= 1:
                        p.w(t, "vec", vec_a0 + it)
                    last = None
                    for ec in range(DT):
                        last = t.transpose(
                            ps_tr[ec // 8][:, (ec % 8) * 128:(ec % 8 + 1) * 128],
                            stg4[it % 2][:, ec * 128:(ec + 1) * 128], ident[:])
                    p.inc(last, "ten")

            @blk.vector
            def _(v):
                for it in range(NOT_):
                    p.w(v, "ten", ten_a0 + it + 1)
                    last = None
                    for half in range(2):
                        last = v.tensor_copy(
                            out=og_fm4[:, half * 8:(half + 1) * 8,
                                       it * 128:(it + 1) * 128],
                            in_=ps_tr[half][:].rearrange("p (c f) -> p c f", c=8))
                    p.inc(last, "vec")
            vec_ogfm = p.cnt["vec"]
            ten_a1 = p.cnt["ten"]

            ten_b0, vec_b0, sca_b0 = p.cnt["ten"], p.cnt["vec"], p.cnt["sca"]
            qsa_b0, qsb_b0 = p.cnt["qsa"], p.cnt["qsb"]
            qp_b0 = p.cnt["qp"]

            TK4 = {}
            c = vec_b0
            for k in range(NOT_ + 1):
                if k < NOT_:
                    for ch in range(4):
                        c += 1; TK4["va", k, ch] = c
                    c += 1; TK4["SUMr", k] = c
                if k >= 1:
                    c += 1; TK4["evQ", k - 1] = c
                if k < NOT_:
                    c += 4; TK4["R0", k] = c
                    c += 6; TK4["BIA", k] = c
                    c += 3; TK4["RX", k] = c
                    c += 1; TK4["r2b", k] = c
            vec_b_end = c
            c = sca_b0
            for k in range(NOT_):
                c += 1; TK4["SQ", k] = c
                c += 1; TK4["sqrt", k] = c
                c += 1; TK4["apply", k] = c
                c += 2; TK4["r2a", k] = c
            c = ten_b0
            for k in range(NOT_ + 1):
                if k < NOT_:
                    for ch in range(4):
                        c += 1; TK4["mmB", k, ch] = c
                if k >= 1:
                    c += 1; TK4["trQ", k - 1] = c
            ten_b_end = c

            @blk.sync
            def _(sync):
                p.w(sync, "ten", ten_P3e)
                for k in range(NOT_):
                    if k >= 2:
                        p.w(sync, "vec", TK4["va", k - 2, 3])
                    p.inc(sync.dma_start(out=x4buf[k % 2][:],
                                         in_=x_in[k * 128:(k + 1) * 128, :]),
                          _ldsem(k), 16)

            @blk.tensor
            def _(t):
                p.w(t, "vec", vec_ogfm)
                for k in range(NOT_ + 1):
                    if k < NOT_:
                        for ch in range(4):
                            if k >= 1:
                                p.w(t, "vec", TK4["va", k - 1, ch])
                            last = None
                            for dc in range(DT):
                                if k == 0 and ch == 0:
                                    p.w(t, "qga" if dc % 2 == 0 else "qgb",
                                        (qga_w0 if dc % 2 == 0 else qgb_w0)
                                        + 16 * (dc // 2 + 1))
                                last = t.matmul(
                                    ps_mm[ch][:],
                                    og_fm4[:, dc, k * 128:(k + 1) * 128],
                                    wpb[dc][:, ch * 512:(ch + 1) * 512],
                                    start=(dc == 0), stop=(dc == DT - 1))
                            p.inc(last, "ten")
                            assert p.cnt["ten"] == TK4["mmB", k, ch]
                    if k >= 1:
                        p.w(t, "vec", TK4["r2b", k - 1])
                        if k >= 2:
                            p.w(t, "vec", TK4["evQ", k - 2])
                        last = None
                        for ec in range(DT):
                            last = t.transpose(
                                ps_tr[ec // 8][:, (ec % 8) * 128:
                                               (ec % 8 + 1) * 128],
                                stg4[(k - 1) % 2][:, ec * 128:(ec + 1) * 128],
                                ident[:])
                        p.inc(last, "ten")
                        assert p.cnt["ten"] == TK4["trQ", k - 1]

            @blk.vector
            def _(v):
                v.memset(mgb[:, 0:1], MAGIC)
                v.memset(mgb[:, 1:2], -MAGIC)
                for k in range(NOT_ + 1):
                    if k < NOT_:
                        p.w(v, _ldsem(k), _ldval(k, qsa_b0, qsb_b0))
                        if k >= 2:
                            p.w(v, "qp", qp_b0 + 16 * (k - 1))
                        for ch in range(4):
                            p.w(v, "ten", TK4["mmB", k, ch])
                            inst = v.tensor_tensor(
                                out=x1t4[k % 2][:, ch * 512:(ch + 1) * 512],
                                in0=ps_mm[ch][:],
                                in1=x4buf[k % 2][:, ch * 512:(ch + 1) * 512],
                                op=OP.add)
                            if ch < 3:
                                p.inc(inst, "vec")
                            else:
                                p.chain(inst, v, "vec")
                            assert p.cnt["vec"] == TK4["va", k, ch]
                        p.inc(v.tensor_reduce(out=SUM[:, k:k + 1],
                                              in_=x1t4[k % 2][:], axis=AX,
                                              op=OP.add), "vec")
                        assert p.cnt["vec"] == TK4["SUMr", k]
                    if k >= 1:
                        p.w(v, "ten", TK4["trQ", k - 1])
                        last = None
                        for half in range(2):
                            last = v.tensor_copy(
                                out=xq_fm[:, half * 8:(half + 1) * 8,
                                          (k - 1) * 128:k * 128],
                                in_=ps_tr[half][:].rearrange(
                                    "p (c f) -> p c f", c=8))
                        p.inc(last, "vec")
                        assert p.cnt["vec"] == TK4["evQ", k - 1]
                    if k < NOT_:
                        p.w(v, "sca", TK4["SQ", k])
                        v.tensor_scalar(out=M0[:, k:k + 1], in0=SUM[:, k:k + 1],
                                        scalar1=1.0 / D, scalar2=None,
                                        op0=OP.mult)
                        v.tensor_scalar(out=E0[:, k:k + 1], in0=SQ[:, k:k + 1],
                                        scalar1=1.0 / D, scalar2=None,
                                        op0=OP.mult)
                        p.chain(v.tensor_mul(V0[:, k:k + 1], M0[:, k:k + 1],
                                             M0[:, k:k + 1]), v, "vec")
                        p.chain(v.tensor_sub(V0[:, k:k + 1], E0[:, k:k + 1],
                                             V0[:, k:k + 1]), v, "vec")
                        p.chain(v.tensor_scalar(out=T0[:, k:k + 1],
                                                in0=V0[:, k:k + 1],
                                                scalar1=1.0, scalar2=LN1_EPS,
                                                op0=OP.mult, op1=OP.add),
                                v, "vec")
                        p.inc(v.reciprocal(R0[:, k:k + 1], T0[:, k:k + 1]),
                              "vec")
                        assert p.cnt["vec"] == TK4["R0", k]
                        p.w(v, "sca", TK4["sqrt", k])
                        p.chain(v.tensor_mul(N1[:, k:k + 1], S0[:, k:k + 1],
                                             S0[:, k:k + 1]), v, "vec")
                        p.chain(v.tensor_mul(N1[:, k:k + 1], T0[:, k:k + 1],
                                             N1[:, k:k + 1]), v, "vec")
                        p.chain(v.tensor_scalar(out=N1[:, k:k + 1],
                                                in0=N1[:, k:k + 1],
                                                scalar1=-0.5, scalar2=1.5,
                                                op0=OP.mult, op1=OP.add),
                                v, "vec")
                        p.chain(v.tensor_mul(SCL[:, k:k + 1], S0[:, k:k + 1],
                                             N1[:, k:k + 1]), v, "vec")
                        p.chain(v.tensor_mul(N2[:, k:k + 1], M0[:, k:k + 1],
                                             SCL[:, k:k + 1]), v, "vec")
                        p.inc(v.tensor_scalar(out=BIA[:, k:k + 1],
                                              in0=N2[:, k:k + 1], scalar1=-1.0,
                                              scalar2=None, op0=OP.mult),
                              "vec")
                        assert p.cnt["vec"] == TK4["BIA", k]
                        p.w(v, "sca", TK4["apply", k])
                        p.chain(v.tensor_reduce(out=AM[:, k:k + 1], in_=osu[:],
                                                axis=AX, op=OP.max,
                                                apply_absolute_value=True),
                                v, "vec")
                        p.chain(v.tensor_scalar(out=SX[:, k:k + 1],
                                                in0=AM[:, k:k + 1],
                                                scalar1=RQMAX, scalar2=QEPS,
                                                op0=OP.mult, op1=OP.max),
                                v, "vec")
                        p.inc(v.reciprocal(RX[:, k:k + 1], SX[:, k:k + 1]),
                              "vec")
                        assert p.cnt["vec"] == TK4["RX", k]
                        p.w(v, "sca", TK4["r2a", k])
                        if k >= 2:
                            p.w(v, "ten", TK4["trQ", k - 2])
                        else:
                            p.w(v, "ten", ten_a1)
                        p.inc(v.tensor_scalar(out=stg4[k % 2][:], in0=osu[:],
                                              scalar1=SX[:, k:k + 1],
                                              scalar2=None, op0=OP.mult),
                              "vec")
                        assert p.cnt["vec"] == TK4["r2b", k]
                assert p.cnt["vec"] == vec_b_end

            @blk.scalar
            def _(s):
                for k in range(NOT_):
                    p.w(s, "vec", TK4["va", k, 3])
                    p.inc(s.activation(out=ost[:], in_=x1t4[k % 2][:],
                                       func=AF.Square,
                                       accum_out=SQ[:, k:k + 1]), "sca")
                    assert p.cnt["sca"] == TK4["SQ", k]
                    p.w(s, "vec", TK4["R0", k])
                    p.inc(s.activation(out=S0[:, k:k + 1], in_=R0[:, k:k + 1],
                                       func=AF.Sqrt), "sca")
                    assert p.cnt["sca"] == TK4["sqrt", k]
                    p.w(s, "vec", TK4["BIA", k])
                    if k >= 1:
                        p.w(s, "vec", TK4["r2b", k - 1])
                    p.inc(s.activation(out=osu[:], in_=x1t4[k % 2][:],
                                       func=AF.Identity, scale=SCL[:, k:k + 1],
                                       bias=BIA[:, k:k + 1]), "sca")
                    assert p.cnt["sca"] == TK4["apply", k]
                    p.w(s, "vec", TK4["RX", k])
                    p.chain(s.activation(out=ost[:], in_=osu[:],
                                         func=AF.Identity,
                                         scale=RX[:, k:k + 1],
                                         bias=mgb[:, 0:1]), s, "sca")
                    p.inc(s.activation(out=osu[:], in_=ost[:],
                                       func=AF.Identity, bias=mgb[:, 1:2]),
                          "sca")
                    assert p.cnt["sca"] == TK4["r2a", k]
                    p.inc(s.dma_start(out=x1_scr[k], in_=x1t4[k % 2][:]),
                          "qp", 16)
            qp_x1 = p.cnt["qp"]

            if "x1" in taps:
                ap = tap("x1", [NOT_ * 128, D], F32)

                @blk.sync
                def _(sync):
                    p.w(sync, "qp", qp_x1)
                    p.inc(sync.dma_start(
                        out=ap, in_=x1_scr.rearrange("a b c -> (a b) c")), "qo", 16)

            assert p.cnt["ten"] == ten_b_end
            vec_xq = p.cnt["vec"]

            if "xq" in taps:
                ap = tap("xq", [128, DT * NO], BF16)

                @blk.sync
                def _(sync):
                    p.w(sync, "vec", vec_xq)
                    p.inc(sync.dma_start(
                        out=ap, in_=xq_fm[:].rearrange("p a b -> p (a b)")), "qo", 16)

        vec_thq = p.cnt["vec"]
        if phase_limit >= 8:
            qga_f0, qgb_f0 = p.cnt["qga"], p.cnt["qgb"]
            ten_f0, vec_f0, sca_f0 = p.cnt["ten"], p.cnt["vec"], p.cnt["sca"]

            def _p_wsem(ppi):
                return "qga" if ppi % 2 == 0 else "qgb"

            def _p_wval(ppi):
                return (qga_f0 if ppi % 2 == 0 else qgb_f0) + 16 * (ppi // 2 + 1)

            def dslot(ppi, which):
                off = (ppi % 2) * 256 + which * 128
                return rrowb[:, off:off + 128]

            def WBG(i):
                return wscb[:, 2 * (i % 8):2 * (i % 8) + 1]

            def WBU(i):
                return wscb[:, 2 * (i % 8) + 1:2 * (i % 8) + 2]

            def WSGc(i):
                return wsc2[:, 2 * (i % 8):2 * (i % 8) + 1]

            def WSUc(i):
                return wsc2[:, 2 * (i % 8) + 1:2 * (i % 8) + 2]

            def t1q(ppi, which):
                return t1qb[2 * (ppi % 2) + which]

            TK = {}
            c = sca_f0
            for it_ in range(IT + 1):
                if it_ < IT:
                    c += 1; TK["gAa2", it_] = c
                if it_ >= 1:
                    c += 1; TK["silu", it_ - 1] = c
                if it_ < IT:
                    c += 1; TK["gBa2", it_] = c
                    c += 1; TK["gAa3", it_] = c
                    c += 1; TK["gBa3", it_] = c
                    c += 1; TK["uAa2", it_] = c
                    c += 1; TK["uBa2", it_] = c
                    c += 1; TK["uAa3", it_] = c
                    c += 1; TK["uBa3", it_] = c
            sca_P5e = c
            c = vec_f0
            for it_ in range(IT + 2):
                if it_ < IT:
                    c += 1; TK["Ag", it_] = c
                    c += 1; TK["Au", it_] = c
                if it_ >= 2:
                    c += 1; TK["E", it_ - 2] = c
                    c += 1; TK["F1", it_ - 2] = c
                    c += 1; TK["F2", it_ - 2] = c
            vec_P5e = c
            c = ten_f0
            for it_ in range(IT + 1):
                if it_ < IT:
                    c += 1; TK["trgA", it_] = c
                if it_ >= 1:
                    c += 1; TK["mmg", it_ - 1] = c
                if it_ < IT:
                    c += 1; TK["trgB", it_] = c
                if it_ >= 1:
                    c += 1; TK["mmu", it_ - 1] = c
                if it_ < IT:
                    c += 1; TK["truA", it_] = c
                    c += 1; TK["truB", it_] = c
            ten_P5e = c

            @blk.gpsimd
            def _(g):
                for j in range(IT):
                    if j >= 2:
                        p.w(g, "ten", TK["truB", j - 2])
                        p.w(g, "vec", TK["Au", j - 2])
                    else:
                        p.w(g, "ten", ten_f0)
                        p.w(g, "vec", vec_xq)
                    p.inc(g.dma_start(out=wbuf[j % 2][:], in_=w1[j]),
                          _p_wsem(j), 16)

            @blk.vector
            def _(v):
                p.w(v, "vec", vec_xq)
                v.memset(hacc[:], 0.0)
                for it_ in range(IT + 2):
                    if it_ < IT:
                        j = it_
                        p.w(v, _p_wsem(j), _p_wval(j))
                        if j >= 8:
                            p.w(v, "sca", TK["silu", j - 8])
                        wbg = wbuf[j % 2][:, 0:D]
                        wbu = wbuf[j % 2][:, D:2 * D]
                        v.tensor_reduce(out=TMPA[:, 0:1], in_=wbg, axis=AX,
                                        op=OP.max, apply_absolute_value=True)
                        v.tensor_reduce(out=TMPA[:, 1:2], in_=wbu, axis=AX,
                                        op=OP.max, apply_absolute_value=True)
                        v.tensor_scalar(out=WSG(j), in0=TMPA[:, 0:1],
                                        scalar1=RQMAX, scalar2=QEPS,
                                        op0=OP.mult, op1=OP.max)
                        v.tensor_scalar(out=WSU(j), in0=TMPA[:, 1:2],
                                        scalar1=RQMAX, scalar2=QEPS,
                                        op0=OP.mult, op1=OP.max)
                        v.reciprocal(WRG(j), WSG(j))
                        v.reciprocal(WRU(j), WSU(j))
                        v.tensor_scalar(out=WBG(j), in0=WRG(j), scalar1=1.0,
                                        scalar2=None, op0=OP.mult)
                        v.tensor_scalar(out=WBU(j), in0=WRU(j), scalar1=1.0,
                                        scalar2=None, op0=OP.mult)
                        v.reciprocal(WSGc(j), WBG(j))
                        v.reciprocal(WSUc(j), WBU(j))
                        p.inc(v.tensor_scalar(out=dslot(j, 0), in0=ident[:],
                                              scalar1=WRG(j), scalar2=None,
                                              op0=OP.mult), "vec")
                        p.inc(v.tensor_scalar(out=dslot(j, 1), in0=ident[:],
                                              scalar1=WRU(j), scalar2=None,
                                              op0=OP.mult), "vec")
                        assert p.cnt["vec"] == TK["Au", j]
                    if it_ >= 2:
                        j = it_ - 2
                        p.w(v, "ten", TK["mmu", j])
                        p.w(v, "sca", TK["silu", j])
                        last = None
                        for half in range(2):
                            last = v.scalar_tensor_tensor(
                                out=h_buf[:, j, half * 512:(half + 1) * 512],
                                in0=ps_mm[2 + half][:], scalar=WSUc(j),
                                in1=gsil[:, half * 512:(half + 1) * 512],
                                op0=OP.mult, op1=OP.mult)
                        p.chain(last, v, "vec")
                        assert p.cnt["vec"] == TK["E", j]
                        p.chain(v.tensor_max(hacc[:], hacc[:], h_buf[:, j, :]),
                                v, "vec")
                        p.inc(v.scalar_tensor_tensor(out=hacc[:],
                                                     in0=h_buf[:, j, :],
                                                     scalar=-1.0, in1=hacc[:],
                                                     op0=OP.mult, op1=OP.max),
                              "vec")

            def emit_tr(t, j, which, hB):
                wbw = wbuf[j % 2][:, which * D:(which + 1) * D]
                last = None
                for cc in range(8):
                    hc = 8 * hB + cc
                    last = t.matmul(
                        ps_ex[cc // 4][:, (cc % 4) * 128:(cc % 4 + 1) * 128],
                        wbw[:, hc * 128:(hc + 1) * 128],
                        dslot(j, which), start=True, stop=True)
                p.inc(last, "ten")

            def emit_mm(t, j, which):
                last = None
                for half in range(2):
                    ps = ps_mm[2 * which + half]
                    for hc in range(DT):
                        last = t.matmul(
                            ps[:],
                            t1q(j, which)[:, hc * 128:(hc + 1) * 128],
                            xq_fm[:, hc, half * 512:(half + 1) * 512],
                            start=(hc == 0), stop=(hc == DT - 1))
                p.inc(last, "ten")

            @blk.tensor
            def _(t):
                for it_ in range(IT + 1):
                    if it_ < IT:
                        if it_ >= 1:
                            p.w(t, "sca", TK["uBa2", it_ - 1])
                        p.w(t, "vec", TK["Au", it_])
                        emit_tr(t, it_, 0, 0)
                        assert p.cnt["ten"] == TK["trgA", it_]
                    if it_ >= 1:
                        p.w(t, "sca", TK["gBa3", it_ - 1])
                        emit_mm(t, it_ - 1, 0)
                        assert p.cnt["ten"] == TK["mmg", it_ - 1]
                    if it_ < IT:
                        p.w(t, "sca", TK["gAa2", it_])
                        emit_tr(t, it_, 0, 1)
                        assert p.cnt["ten"] == TK["trgB", it_]
                    if it_ >= 1:
                        p.w(t, "sca", TK["uBa3", it_ - 1])
                        if it_ >= 2:
                            p.w(t, "vec", TK["E", it_ - 2])
                        emit_mm(t, it_ - 1, 1)
                        assert p.cnt["ten"] == TK["mmu", it_ - 1]
                    if it_ < IT:
                        p.w(t, "sca", TK["gBa2", it_])
                        emit_tr(t, it_, 1, 0)
                        assert p.cnt["ten"] == TK["truA", it_]
                        p.w(t, "sca", TK["uAa2", it_])
                        emit_tr(t, it_, 1, 1)
                        assert p.cnt["ten"] == TK["truB", it_]

            def emit_r12(s, hB, k2):
                base = 1024 * hB
                s.activation(out=uwb[:, base:base + 512],
                             in_=ps_ex[0][:], func=AF.Identity,
                             bias=mgb[:, 0:1])
                p.inc(s.activation(out=uwb[:, base + 512:base + 1024],
                                   in_=ps_ex[1][:], func=AF.Identity,
                                   bias=mgb[:, 0:1]), "sca")
                assert p.cnt["sca"] == k2

            def emit_r3(s, j, which, hB, k3):
                base = 1024 * hB
                p.inc(s.activation(out=t1q(j, which)[:, base:base + 1024],
                                   in_=uwb[:, base:base + 1024],
                                   func=AF.Identity, bias=mgb[:, 1:2]), "sca")
                assert p.cnt["sca"] == k3

            @blk.scalar
            def _(s):
                for it_ in range(IT + 1):
                    if it_ < IT:
                        p.w(s, "ten", TK["trgA", it_])
                        emit_r12(s, 0, TK["gAa2", it_])
                    if it_ >= 1:
                        j = it_ - 1
                        p.w(s, "ten", TK["mmg", j])
                        if j >= 1:
                            p.w(s, "vec", TK["E", j - 1])
                        last = None
                        for half in range(2):
                            last = s.activation(
                                out=gsil[:, half * 512:(half + 1) * 512],
                                in_=ps_mm[half][:], func=AF.Silu,
                                scale=WSGc(j))
                        p.inc(last, "sca")
                        assert p.cnt["sca"] == TK["silu", j]
                    if it_ < IT:
                        p.w(s, "ten", TK["trgB", it_])
                        emit_r12(s, 1, TK["gBa2", it_])
                        emit_r3(s, it_, 0, 0, TK["gAa3", it_])
                        emit_r3(s, it_, 0, 1, TK["gBa3", it_])
                        p.w(s, "ten", TK["truA", it_])
                        emit_r12(s, 0, TK["uAa2", it_])
                        p.w(s, "ten", TK["truB", it_])
                        emit_r12(s, 1, TK["uBa2", it_])
                        emit_r3(s, it_, 1, 0, TK["uAa3", it_])
                        emit_r3(s, it_, 1, 1, TK["uBa3", it_])

            assert p.cnt["vec"] == vec_P5e and p.cnt["ten"] == ten_P5e \
                and p.cnt["sca"] == sca_P5e
            vec_P5 = p.cnt["vec"]
            ten_P5 = p.cnt["ten"]

            @blk.tensor
            def _(t):
                p.w(t, "vec", vec_P5)
                last = None
                for bk in range(8):
                    last = t.transpose(ps_tr[0][:, bk * 128:(bk + 1) * 128],
                                       hacc[:, bk * 128:(bk + 1) * 128], ident[:])
                p.inc(last, "ten")
            ten_sh1 = p.cnt["ten"]

            @blk.vector
            def _(v):
                p.w(v, "ten", ten_sh1)
                last = None
                for bk in range(8):
                    last = v.tensor_reduce(out=SH[:, bk:bk + 1],
                                           in_=ps_tr[0][:, bk * 128:(bk + 1) * 128],
                                           axis=AX, op=OP.max)
                p.chain(last, v, "vec")
                p.chain(v.tensor_scalar(out=SHS[:, 0:8], in0=SH[:, 0:8],
                                        scalar1=RQMAX, scalar2=QEPS,
                                        op0=OP.mult, op1=OP.max), v, "vec")
                p.inc(v.reciprocal(RH[:, 0:8], SHS[:, 0:8]), "vec")
            vec_sh1 = p.cnt["vec"]

            @blk.vector
            def _(v):
                p.w(v, "vec", vec_sh1)
                p.chain(v.memset(rrowb[:, 0:128], 0.0), v, "vec")
                p.inc(v.tensor_copy(out=rrowb[:, 0:8], in_=RH[:, 0:8]), "vec")
            vec_rr1 = p.cnt["vec"]

            @blk.tensor
            def _(t):
                p.w(t, "vec", vec_rr1)
                p.inc(t.transpose(ps_tr[0][:, 0:128], rrowb[:, 0:128],
                                  ident[:]), "ten")
            ten_rr = p.cnt["ten"]

            @blk.vector
            def _(v):
                p.w(v, "ten", ten_rr)
                p.inc(v.tensor_copy(out=stg[0:8, 0:128],
                                    in_=ps_tr[0][0:8, 0:128]), "vec")
            vec_rr2 = p.cnt["vec"]

            @blk.sync
            def _(sync):
                p.w(sync, "vec", vec_rr2)
                p.inc(sync.dma_start(out=rrowb[0:1, 0:1024],
                                     in_=stg[0:8, 0:128]), "qp", 16)
            qp_rr = p.cnt["qp"]

            @blk.tensor
            def _(t):
                p.w(t, "qp", qp_rr)
                last = None
                for hf in range(2):
                    last = t.matmul(ps_mm[hf][:], ones1[0:1, 0:128],
                                    rrowb[0:1, hf * 512:(hf + 1) * 512],
                                    start=True, stop=True)
                p.inc(last, "ten")
            ten_rbc = p.cnt["ten"]

            vec_rbc0 = p.cnt["vec"] + 1
            qga_d0, qgb_d0 = p.cnt["qga"], p.cnt["qgb"]
            ten_d0, sca_d0 = p.cnt["ten"], p.cnt["sca"]
            GRP_SZ = [8, 8, 8, 8, 8, 4]
            QN = [0, 1, 2, 2, 3, 3]
            Q4 = I_FFN // 4

            TK6 = {}
            c = vec_rbc0
            c += 4; TK6["A", 0] = c
            for j in range(IT):
                c += 1; TK6["thq", j] = c
            for ht in range(DT):
                if ht + 1 < DT:
                    c += 4; TK6["A", ht + 1] = c
                for gg in range(6):
                    c += 1; TK6["evac", ht, gg] = c
            vec_P6e = c
            c = sca_d0
            for ht in range(DT):
                for q in range(4):
                    c += 1; TK6["B", ht, q] = c
                if ht >= 1:
                    c += 1; TK6["dfm", ht - 1] = c
            c += 1; TK6["dfm", DT - 1] = c
            sca_P6e = c
            c = ten_d0
            for ht in range(DT):
                for gg in range(6):
                    c += 1; TK6["tr", ht, gg] = c
                c += 1; TK6["mm", ht] = c
            ten_P6e = c
            vec_thq = TK6["thq", IT - 1]

            @blk.gpsimd
            def _(g):
                for ht in range(DT):
                    if ht >= 2:
                        p.w(g, "sca", TK6["B", ht - 2, 3])
                        p.w(g, "vec", TK6["A", ht - 2])
                    else:
                        p.w(g, "ten", TK["mmu", IT - 1])
                        p.w(g, "vec", TK["Au", IT - 1])
                    p.inc(g.dma_start(out=w2b[ht % 2][:], in_=w2[ht]),
                          "qga" if ht % 2 == 0 else "qgb", 16)

            def emit_A(v, ht):
                p.w(v, "qga" if ht % 2 == 0 else "qgb",
                    (qga_d0 if ht % 2 == 0 else qgb_d0) + 16 * (ht // 2 + 1))
                if ht >= 8:
                    p.w(v, "sca", TK6["dfm", ht - 8])
                wb2 = w2b[ht % 2]
                v.tensor_reduce(out=TMPA[:, 0:1], in_=wb2[:, 0:I_FFN // 2],
                                axis=AX, op=OP.max, apply_absolute_value=True)
                p.chain(v.tensor_reduce(out=TMPA[:, 1:2],
                                        in_=wb2[:, I_FFN // 2:],
                                        axis=AX, op=OP.max,
                                        apply_absolute_value=True), v, "vec")
                p.chain(v.tensor_max(TMPA[:, 2:3], TMPA[:, 0:1],
                                     TMPA[:, 1:2]), v, "vec")
                p.chain(v.tensor_scalar(out=WS2(ht), in0=TMPA[:, 2:3],
                                        scalar1=RQMAX, scalar2=QEPS,
                                        op0=OP.mult, op1=OP.max), v, "vec")
                p.inc(v.reciprocal(WR2(ht), WS2(ht)), "vec")
                assert p.cnt["vec"] == TK6["A", ht]

            @blk.vector
            def _(v):
                p.w(v, "ten", ten_rbc)
                v.tensor_copy(out=rbc[:, 0:512], in_=ps_mm[0][:])
                p.chain(v.tensor_copy(out=rbc[:, 512:1024], in_=ps_mm[1][:]),
                        v, "vec")
                assert p.cnt["vec"] == vec_rbc0
                emit_A(v, 0)
                tsc = [ost[:, 0:1024], ost[:, 1024:2048]]
                for j in range(IT + 1):
                    if j < IT:
                        v.tensor_tensor(out=tsc[j % 2], in0=h_buf[:, j, :],
                                        in1=rbc[:], op=OP.mult)
                    if j >= 1:
                        p.inc(v.tensor_scalar(out=h_buf[:, j - 1, :],
                                              in0=tsc[(j - 1) % 2],
                                              scalar1=MAGIC, scalar2=MAGIC,
                                              op0=OP.add, op1=OP.subtract),
                              "vec")
                        assert p.cnt["vec"] == TK6["thq", j - 1]
                for ht in range(DT):
                    if ht + 1 < DT:
                        emit_A(v, ht + 1)
                    for gg, sz in enumerate(GRP_SZ):
                        p.w(v, "ten", TK6["tr", ht, gg])
                        off = sum(GRP_SZ[:gg])
                        p.inc(v.tensor_copy(
                            out=t2q[:, off * 128:(off + sz) * 128],
                            in_=ps_tr[gg % 2][:, 0:sz * 128]), "vec")
                        assert p.cnt["vec"] == TK6["evac", ht, gg]
                assert p.cnt["vec"] == vec_P6e

            @blk.scalar
            def _(s):
                for ht in range(DT):
                    p.w(s, "vec", TK6["A", ht])
                    p.w(s, "qga" if ht % 2 == 0 else "qgb",
                        (qga_d0 if ht % 2 == 0 else qgb_d0) + 16 * (ht // 2 + 1))
                    if ht >= 1:
                        p.w(s, "ten", TK6["tr", ht - 1, 5])
                    wb2 = w2b[ht % 2]
                    def a1(qq):
                        s.activation(out=wuq[qq % 2][:],
                                     in_=wb2[:, qq * Q4:(qq + 1) * Q4],
                                     func=AF.Identity, scale=WR2(ht),
                                     bias=mgb[:, 0:1])

                    def a2(qq):
                        p.inc(s.activation(out=s2n[:, qq * Q4:(qq + 1) * Q4],
                                           in_=wuq[qq % 2][:],
                                           func=AF.Identity,
                                           bias=mgb[:, 1:2]), "sca")

                    a1(0); a1(1); a2(0); a1(2); a2(1); a1(3); a2(2); a2(3)
                    assert p.cnt["sca"] == TK6["B", ht, 3]
                    if ht >= 1:
                        p.w(s, "ten", TK6["mm", ht - 1])
                        last = None
                        for half in range(2):
                            last = s.activation(
                                out=down_fm[:, ht - 1,
                                            half * 512:(half + 1) * 512],
                                in_=ps_mm[half][:], func=AF.Identity,
                                scale=WS2(ht - 1))
                        p.inc(last, "sca")
                        assert p.cnt["sca"] == TK6["dfm", ht - 1]
                p.w(s, "ten", TK6["mm", DT - 1])
                last = None
                for half in range(2):
                    last = s.activation(
                        out=down_fm[:, DT - 1, half * 512:(half + 1) * 512],
                        in_=ps_mm[half][:], func=AF.Identity,
                        scale=WS2(DT - 1))
                p.inc(last, "sca")
                assert p.cnt["sca"] == sca_P6e

            @blk.tensor
            def _(t):
                for ht in range(DT):
                    off = 0
                    for gg, sz in enumerate(GRP_SZ):
                        p.w(t, "sca", TK6["B", ht, QN[gg]])
                        if gg >= 2:
                            p.w(t, "vec", TK6["evac", ht, gg - 2])
                        elif ht >= 1:
                            p.w(t, "vec", TK6["evac", ht - 1, gg + 4])
                        last = None
                        for j in range(sz):
                            ic = off + j
                            last = t.transpose(
                                ps_tr[gg % 2][:, j * 128:(j + 1) * 128],
                                s2n[:, ic * 128:(ic + 1) * 128], ident[:])
                        p.inc(last, "ten")
                        assert p.cnt["ten"] == TK6["tr", ht, gg]
                        off += sz
                    p.w(t, "vec", TK6["evac", ht, 5])
                    if ht >= 1:
                        p.w(t, "sca", TK6["dfm", ht - 1])
                    last = None
                    for half in range(2):
                        ps = ps_mm[half]
                        for ic in range(IT):
                            if ht == 0 and half == 0:
                                p.w(t, "vec", TK6["thq", ic])
                            last = t.matmul(
                                ps[:], t2q[:, ic * 128:(ic + 1) * 128],
                                h_buf[:, ic, half * 512:(half + 1) * 512],
                                start=(ic == 0), stop=(ic == IT - 1))
                    p.inc(last, "ten")
                    assert p.cnt["ten"] == TK6["mm", ht]
            ten_P6 = p.cnt["ten"]

            if "thq" in taps:
                ap = tap("thq", [128, IT * NO], BF16)

                @blk.sync
                def _(sync):
                    p.w(sync, "vec", vec_thq)
                    p.inc(sync.dma_start(
                        out=ap, in_=h_buf[:].rearrange("p a b -> p (a b)")),
                        "qo", 16)

            sca_P6 = p.cnt["sca"]

            ten_fin0, vec_fin0 = p.cnt["ten"], p.cnt["vec"]
            qsa_f0, qsb_f0 = p.cnt["qsa"], p.cnt["qsb"]
            qo_f0 = p.cnt["qo"]
            obuf = [ost, osu]

            @blk.sync
            def _(sync):
                p.w(sync, "sca", sca_P6)
                p.w(sync, "qp", qp_x1)
                for nt in range(NOT_):
                    if nt >= 2:
                        p.w(sync, "vec", vec_fin0 + 2 * (nt - 2) + 2)
                    p.inc(sync.dma_start(out=xbuf[nt % 2][:], in_=x1_scr[nt]),
                          _ldsem(nt), 16)

            @blk.scalar
            def _(s):
                for nt in range(NOT_):
                    p.w(s, "vec", vec_fin0 + 2 * nt + 2)
                    p.inc(s.dma_start(out=out[nt * 128:(nt + 1) * 128, :],
                                      in_=obuf[nt % 2][:]), "qo", 16)

            @blk.tensor
            def _(t):
                p.w(t, "sca", sca_P6)
                for nt in range(NOT_):
                    if nt >= 1:
                        p.w(t, "vec", vec_fin0 + 2 * (nt - 1) + 1)
                    last = None
                    for ht in range(DT):
                        last = t.transpose(
                            ps_tr[ht // 8][:, (ht % 8) * 128:(ht % 8 + 1) * 128],
                            down_fm[:, ht, nt * 128:(nt + 1) * 128], ident[:])
                    p.inc(last, "ten")

            @blk.vector
            def _(v):
                for nt in range(NOT_):
                    p.w(v, "ten", ten_fin0 + nt + 1)
                    p.w(v, _ldsem(nt), _ldval(nt, qsa_f0, qsb_f0))
                    if nt >= 2:
                        p.w(v, "qo", qo_f0 + 16 * (nt - 1))
                    last = None
                    for half in range(2):
                        last = v.tensor_scalar(
                            out=obuf[nt % 2][:, half * 1024:(half + 1) * 1024],
                            in0=ps_tr[half][:],
                            scalar1=SHS[:, nt:nt + 1], scalar2=None,
                            op0=OP.mult)
                    p.chain(last, v, "vec")
                    p.inc(v.tensor_add(obuf[nt % 2][:], obuf[nt % 2][:],
                                       xbuf[nt % 2][:]), "vec")



        @blk.sync
        def _(sync):
            for nm in ("qs", "qsa", "qsb", "qga", "qgb", "qg2", "qp", "qo"):
                p.w(sync, nm, p.cnt[nm])

    return nc, tap_aps


def _tile_w(W):
    O, Din = W.shape
    A = W.reshape(O // 256, 2, 128, Din // 128, 128)
    A = A.transpose(0, 4, 1, 3, 2)
    return np.ascontiguousarray(A.reshape(O // 256, 128, 2 * Din))


def _perm_for_core(c):
    if c % 2 == 0:
        return (0, 3), (1, 2)
    return (1, 2), (0, 3)


def _moff_for_core(c):
    own, partner = _perm_for_core(c)
    blocks = [own[0], own[1], partner[0], partner[1]]
    gstart = [512 * b for b in blocks]
    offs = []
    for jt in JSET_A:
        offs.append(gstart[0] - (gstart[jt // 4] + 128 * (jt % 4)))
    for jt in JSET_B:
        offs.append(gstart[1] - (gstart[jt // 4] + 128 * (jt % 4)))
    m = np.zeros((128, 24), np.float32)
    m[:] = np.array(offs, np.float32)[None, :]
    return m


def make_inputs_for_core(c, X, W):
    own, partner = _perm_for_core(c)
    xb = X[c // 2]
    rows = np.concatenate([xb[512 * blk:512 * (blk + 1)]
                           for blk in (own + partner)], axis=0)
    d = {"x_in": np.ascontiguousarray(rows), "moff": _moff_for_core(c)}
    d.update(W)
    return d


def prep_weights(L0_attn_Wq, L0_attn_Wk, L0_attn_Wv, L0_attn_Wg, L0_attn_Wp,
                 L0_ffn_w_gate_up, L0_ffn_w_down):
    return {
        "wq": _tile_w(np.asarray(L0_attn_Wq, np.float32)),
        "wk": _tile_w(np.asarray(L0_attn_Wk, np.float32)),
        "wv": _tile_w(np.asarray(L0_attn_Wv, np.float32)),
        "wg": _tile_w(np.asarray(L0_attn_Wg, np.float32)),
        "wp": np.ascontiguousarray(
            np.asarray(L0_attn_Wp, np.float32).T.reshape(DT, 128, D)),
        "w1": np.ascontiguousarray(
            np.asarray(L0_ffn_w_gate_up, np.float32).reshape(2, IT, 128, D)
            .transpose(1, 2, 0, 3).reshape(IT, 128, 2 * D)),
        "w2": np.ascontiguousarray(
            np.asarray(L0_ffn_w_down, np.float32).reshape(DT, 128, I_FFN)),
    }


def kernel(L0_attn_x, L0_attn_Wq, L0_attn_Wk, L0_attn_Wv, L0_attn_Wg,
           L0_attn_Wp, L0_ffn_w_gate_up, L0_ffn_w_down):
    X = np.ascontiguousarray(np.asarray(L0_attn_x, np.float32))
    W = prep_weights(L0_attn_Wq, L0_attn_Wk, L0_attn_Wv, L0_attn_Wg,
                     L0_attn_Wp, L0_ffn_w_gate_up, L0_ffn_w_down)
    nc, _ = build()
    in_maps = [make_inputs_for_core(c, X, W) for c in range(8)]
    res = run_bass_kernel_spmd(nc, in_maps, core_ids=list(range(8)))
    outp = np.empty((4, 2048, 2048), np.float32)
    for c in range(8):
        own, _ = _perm_for_core(c)
        o = np.asarray(res.results[c]["out"], np.float32)
        b = c // 2
        outp[b, 512 * own[0]:512 * (own[0] + 1)] = o[0:512]
        outp[b, 512 * own[1]:512 * (own[1] + 1)] = o[512:1024]
    return outp
